# version 03
# baseline (speedup 1.0000x reference)
import numpy as np
import ml_dtypes
import concourse.bass as bass
import concourse.mybir as mybir
from concourse.bass_utils import run_bass_kernel_spmd

F32 = mybir.dt.float32
BF16 = mybir.dt.bfloat16
AF = mybir.ActivationFunctionType
ALU = mybir.AluOpType

D = 1024
DFF = 4096
S = 8192
B = 8
T = 512
NTILE = S // T
FC = D // 128
MCF = DFF // 128
TC = T // 128
NH = 16
DH = 64
LAY = 3
DHS = float(DH) ** -0.25
LNC = float(np.log(2 * 32) * -0.5)
LN_EPS = 1e-5
ATTN_EPS = 1e-6
NSLOT = 3
CH = 4096
CONSTVALS = (1.0 / D, LN_EPS, -0.5, 0.5, -LNC, LNC, ATTN_EPS, DHS, -1.0, 1.0,
             0.0, 2.0, 0.25)

ENGINES = ("pe", "act", "dve", "pool", "sp")
PARANOID = frozenset(("act", "dve", "pool"))


class Ctx:
    def __init__(self):
        self.i = None


class Prog:
    def __init__(self):
        self.ops = {e: [] for e in ENGINES}
        self.waits = {e: [] for e in ENGINES}
        self.opclock = {e: [] for e in ENGINES}
        self.dmaclock = {}
        self.clock = {e: {} for e in ENGINES}
        self.group_count = {}
        self.group_engine = {}
        self.need_inc = {e: set() for e in ENGINES}
        self.res = {}
        self.n_waits = 0
        self.seg = 0
        self.seg_start = {e: [0] for e in ENGINES}
        self.loop = None
        self.group_final = []
        self._seen_groups = set()

    def _need(self, eng, clk, needed, p):
        if p is None:
            return
        if p[0] == "e":
            if p[1] == eng and eng not in PARANOID:
                return
            sk, v = ("e", p[1]), p[2]
        else:
            sk, v = ("d", p[1]), p[2]
        if clk.get(sk, -1) >= v:
            return
        if needed.get(sk, -1) < v:
            needed[sk] = v

    def emit(self, eng, fn, *args, reads=(), writes=(), dma=None, commit=True,
             force_inc=False, **kwargs):
        pos = len(self.ops[eng])
        clk = self.clock[eng]
        needed = {}
        for k in reads:
            st = self.res.get(k)
            if st is not None:
                self._need(eng, clk, needed, st[0])
        for k in writes:
            st = self.res.get(k)
            if st is not None:
                self._need(eng, clk, needed, st[0])
                for r in st[1]:
                    self._need(eng, clk, needed, r)
        wl = []
        for sk, v in sorted(needed.items(), key=repr):
            if sk[0] == "e":
                self.need_inc[sk[1]].add((sk[1], v))
                snap = self.opclock[sk[1]][v]
            else:
                snap = self.dmaclock[(sk[1], v)]
            for k2, v2 in snap.items():
                if clk.get(k2, -1) < v2:
                    clk[k2] = v2
            if clk.get(sk, -1) < v:
                clk[sk] = v
            wl.append((sk, v))
        self.n_waits += len(wl)
        self.waits[eng].append(wl)
        self.ops[eng].append((fn, args, kwargs, dma, self.seg))
        if force_inc and dma is None:
            self.need_inc[eng].add((eng, pos))
        if dma is not None:
            assert self.group_engine.setdefault(dma, eng) == eng
            cnt = self.group_count.get(dma, 0) + 1
            self.group_count[dma] = cnt
            me = ("d", dma, cnt)
            self.dmaclock[(dma, cnt)] = dict(clk)
            self.opclock[eng].append(dict(clk))
        else:
            me = ("e", eng, pos)
            snap = dict(clk)
            snap[("e", eng)] = pos
            self.opclock[eng].append(snap)
        if commit:
            self._register(me, reads, writes)
        return me

    def commit_dma(self, group, reads=(), writes=()):
        cnt = self.group_count[group]
        self._register(("d", group, cnt), reads, writes)

    def _register(self, me, reads, writes):
        for k in writes:
            self.res[k] = [me, []]
        for k in reads:
            st = self.res.get(k)
            if st is None:
                st = [None, []]
                self.res[k] = st
            st[1].append(me)

    def _cut_segment(self, marker):
        for g in self.group_count:
            assert g not in self._seen_groups, f"group {g} in two segments"
        self._seen_groups |= set(self.group_count)
        self.group_final.append(dict(self.group_count))
        for e in ENGINES:
            self.ops[e].append((marker, None, None, None, self.seg))
            self.waits[e].append([])
            self.opclock[e].append({})
        self.seg += 1
        for e in ENGINES:
            self.seg_start[e].append(len(self.ops[e]))
            self.clock[e] = {}
        self.group_count = {}
        self.dmaclock = {}
        self.res = {k: [None, []] for k in self.res}

    def begin_loop(self, n_iters):
        assert self.loop is None
        self._cut_segment("@LOOP")
        self.loop = n_iters

    def end_loop(self):
        self._cut_segment("@ENDLOOP")

    def finalize(self, nc, fences=None):
        if fences is None:
            fences = {}
        self._cut_segment("@END")
        sems = {}
        for e in ENGINES:
            sems[("e", e)] = nc.alloc_semaphore(f"sem_{e}")
        for g in self.group_engine:
            nm = "_".join(map(str, g)) if isinstance(g, tuple) else str(g)
            sems[("d", g)] = nc.alloc_semaphore(f"dg_{nm}")
        bar = nc.alloc_semaphore("bar")
        bar2 = nc.alloc_semaphore("bar2")
        clear_ids = [sems[("e", e)] for e in ENGINES]
        body_per_iter = (self.group_final[1]
                         if self.loop is not None and len(self.group_final) > 1
                         else {})

        def seg_of(e, pos):
            lo = 0
            for si, st in enumerate(self.seg_start[e]):
                if st <= pos:
                    lo = si
            return lo

        inc_rank = {e: {} for e in ENGINES}
        inc_totals = {e: {} for e in ENGINES}
        for e in ENGINES:
            for p in sorted(p for (_, p) in self.need_inc[e]):
                s = seg_of(e, p)
                inc_totals[e][s] = inc_totals[e].get(s, 0) + 1
                inc_rank[e][p] = inc_totals[e][s]
        markedset = {e: set(p for (_, p) in self.need_inc[e]) for e in ENGINES}
        n_eng = len(ENGINES)
        loop_n = self.loop
        engmap = dict(pe="tensor", act="scalar", dve="vector", pool="gpsimd",
                      sp="sync")
        enumap = dict(pe=mybir.EngineType.PE, act=mybir.EngineType.Activation,
                      dve=mybir.EngineType.DVE, pool=mybir.EngineType.Pool,
                      sp=mybir.EngineType.SP)

        dyn_pairs = {e: [] for e in ENGINES}
        if loop_n is not None:
            for e in ENGINES:
                seen = dict()
                lo = self.seg_start[e][1]
                hi = self.seg_start[e][2] - 1
                for i in range(lo, hi):
                    for sk, v in self.waits[e][i]:
                        if sk[0] == "d" and sk[1] not in seen:
                            seen[sk[1]] = True
                for g, cnt in self.group_final[1].items():
                    if self.group_engine[g] == e and cnt and g not in seen:
                        seen[g] = True
                dyn_pairs[e] = list(seen)
        waitregs = {}
        for e in ENGINES:
            for g in dyn_pairs[e]:
                rh = nc.alloc_registers(f"wr_{e}_{g}", engines=[enumap[e]])
                waitregs[(e, g)] = list(rh)[0]
        cur_target = {}

        def dyn_wait(engine, e, g, target):
            reg = waitregs[(e, g)]
            cur = cur_target.get((e, g), 0)
            delta = target - cur
            assert delta >= 0, (e, g, target, cur)
            if delta:
                engine.reg_alu(reg, reg, delta, mybir.AluOpType.add)
            engine.wait_ge(sems[("d", g)], reg)
            cur_target[(e, g)] = target

        def emit_barrier(engine, e, ctx, seg_idx, in_loop):
            gf = self.group_final[seg_idx]
            for g, cnt in sorted(gf.items(), key=repr):
                if self.group_engine[g] == e and cnt:
                    if in_loop:
                        dyn_wait(engine, e, g, 16 * cnt)
                    else:
                        engine.wait_ge(sems[("d", g)], 16 * cnt)
            if in_loop:
                per_map = self.group_final[1]
                for g in dyn_pairs[e]:
                    per = per_map.get(g, 0)
                    tgt = 16 * per
                    cur = cur_target.get((e, g), 0)
                    if tgt != cur:
                        engine.reg_alu(waitregs[(e, g)], waitregs[(e, g)],
                                       tgt - cur, mybir.AluOpType.add)
                    cur_target[(e, g)] = 0
            for e2 in ENGINES:
                tot = inc_totals[e2].get(seg_idx, 0)
                if tot:
                    engine.wait_ge(sems[("e", e2)], tot)
            if e in ("pe", "sp") or fences.get(e) is None:
                engine.nop().then_inc(bar, 1)
            elif e == "act":
                engine.copy(fences[e], fences[e]).then_inc(bar, 1)
            else:
                engine.memset(fences[e], 0.0).then_inc(bar, 1)
            nbar = seg_idx + 1
            if in_loop:
                barv = n_eng * nbar + n_eng * ctx.i
                bar2v = nbar + ctx.i
            else:
                barv = n_eng * nbar
                bar2v = nbar
            if e == "pool":
                engine.wait_ge(bar, barv)
                for s in clear_ids:
                    engine.sem_clear(s)
                engine.nop().then_inc(bar2, 1)
            else:
                engine.wait_ge(bar2, bar2v)

        with nc.Block() as blk:
            for e in ENGINES:
                def body(engine, e=e):
                    ctx = Ctx()
                    esem = sems[("e", e)]

                    def run_ops(lo, hi):
                        for i in range(lo, hi):
                            fn, args, kwargs, dma, _seg = self.ops[e][i]
                            for sk, v in self.waits[e][i]:
                                if sk[0] == "e":
                                    engine.wait_ge(sems[sk], inc_rank[sk[1]][v])
                                elif ctx.i is not None:
                                    dyn_wait(engine, e, sk[1], 16 * v)
                                else:
                                    engine.wait_ge(sems[sk], 16 * v)
                            rargs = [a(ctx) if callable(a) else a for a in args]
                            inst = getattr(engine, fn)(*rargs, **kwargs)
                            if dma is not None:
                                inst.then_inc(sems[("d", dma)], 16)
                            elif i in markedset[e]:
                                inst.then_inc(esem, 1)

                    marks = [i for i, op in enumerate(self.ops[e])
                             if op[0] in ("@LOOP", "@ENDLOOP", "@END")]
                    assert loop_n is not None
                    mloop, mend, mfin = marks
                    run_ops(0, mloop)
                    emit_barrier(engine, e, ctx, self.ops[e][mloop][4], False)
                    for g in dyn_pairs[e]:
                        engine.reg_mov(waitregs[(e, g)], 0)
                    with engine.Fori(0, loop_n) as iv:
                        ctx.i = iv
                        run_ops(mloop + 1, mend)
                        emit_barrier(engine, e, ctx, self.ops[e][mend][4], True)
                    ctx.i = None
                    run_ops(mend + 1, mfin)

                getattr(blk, engmap[e])(body)
        return sems


FFN_SZ = 2 * 32 * 1024 + 2 * 32 * 4096 + 8 * 4096
ATT_SZ = 4 * 8 * 1024 + 32 * 1024 + 8 * 4096
CACHE_PP = FFN_SZ + LAY * ATT_SZ


def stack_off(w, l, mi):
    if w == "w1":
        return mi * 2048
    if w == "w2":
        return mi * 2048 + 1024
    if w == "h1":
        return 65536 + mi * 4096
    if w == "h2":
        return 196608 + mi * 4096
    if w == "w3":
        return 327680 + mi * 4096
    base = FFN_SZ + l * ATT_SZ
    if w == "wq":
        return base + mi * 1024
    if w == "wk":
        return base + 8192 + mi * 1024
    if w == "wv":
        return base + 16384 + mi * 1024
    if w == "wo":
        return base + 24576 + mi * 1024
    if w == "wff1":
        return base + 32768 + mi * 1024
    if w == "wff2":
        return base + 65536 + mi * 4096
    raise KeyError(w)


def sv_cols():
    cols = {}
    c = 0
    for l in range(LAY):
        for nm, n in (("bq", 8), ("bk", 8), ("bv", 8), ("bo", 8),
                      ("g1", 8), ("b1", 8), ("g2", 8), ("b2", 8),
                      ("bf1", 32), ("bf2", 8)):
            cols[(nm, l)] = c
            c += n
    for nm, n in (("s1", 32), ("s2", 32), ("sh1", 32), ("sh2", 32), ("s3", 8)):
        cols[(nm, None)] = c
        c += n
    return cols, c


SVC, SV_N = sv_cols()
DVC = {}
_c = 0
for l in range(LAY):
    DVC[("bqs", l)] = _c; _c += 8
    DVC[("bks", l)] = _c; _c += 8
DVC[("s3h", None)] = _c; _c += 8
DV_N = _c


import os as _os
DEV_STAGE = _os.environ.get("K_STAGE", "")


def build_nc(NT=NTILE):
    nc = bass.Bass("TRN2", target_bir_lowering=False)

    xt = nc.dram_tensor("xt", [D * S], F32, kind="ExternalInput")
    yt = nc.dram_tensor("yt", [D * S], F32, kind="ExternalOutput")
    w1t = nc.dram_tensor("w1t", [D, DFF], F32, kind="ExternalInput").ap()
    w2t = nc.dram_tensor("w2t", [D, DFF], F32, kind="ExternalInput").ap()
    h1t = nc.dram_tensor("h1t", [DFF, DFF], F32, kind="ExternalInput").ap()
    h2t = nc.dram_tensor("h2t", [DFF, DFF], F32, kind="ExternalInput").ap()
    w3t = nc.dram_tensor("w3t", [DFF, D], F32, kind="ExternalInput").ap()
    wq = nc.dram_tensor("wq", [LAY, D, D], F32, kind="ExternalInput").ap()
    wk = nc.dram_tensor("wk", [LAY, D, D], F32, kind="ExternalInput").ap()
    wv = nc.dram_tensor("wv", [LAY, D, D], F32, kind="ExternalInput").ap()
    wo = nc.dram_tensor("wo", [LAY, D, D], F32, kind="ExternalInput").ap()
    wff1 = nc.dram_tensor("wff1", [LAY, D, DFF], F32, kind="ExternalInput").ap()
    wff2 = nc.dram_tensor("wff2", [LAY, DFF, D], F32, kind="ExternalInput").ap()
    svin = nc.dram_tensor("svin", [128, SV_N], F32, kind="ExternalInput").ap()
    obdin = nc.dram_tensor("obdin", [128, 64 * LAY], F32, kind="ExternalInput").ap()
    cfin = nc.dram_tensor("cfin", [128, 272], F32, kind="ExternalInput").ap()
    cbin = nc.dram_tensor("cbin", [128, 131], BF16, kind="ExternalInput").ap()
    cache = nc.dram_tensor("wcache", [128 * CACHE_PP], BF16, kind="Internal")

    XT = nc.alloc_sbuf_tensor("XT", [128, FC * 512], F32)
    RS = nc.alloc_sbuf_tensor("RS", [128, FC * 512], F32)
    HS = nc.alloc_sbuf_tensor("HS", [128, FC * 512], F32)
    RB = nc.alloc_sbuf_tensor("RB", [128, FC * 512], BF16)
    HB = nc.alloc_sbuf_tensor("HB", [128, FC * 512], BF16)
    BIGA = nc.alloc_sbuf_tensor("BIGA", [128, MCF * 512], BF16)
    BIGB = nc.alloc_sbuf_tensor("BIGB", [128, MCF * 512], BF16)
    W = [nc.alloc_sbuf_tensor(f"W{i}", [128, CH], BF16) for i in range(NSLOT)]
    KT = nc.alloc_sbuf_tensor("KT", [128, FC * 128], BF16)
    VT = nc.alloc_sbuf_tensor("VT", [128, FC * 130], BF16)
    S_sb = nc.alloc_sbuf_tensor("S_sb", [128, LAY * FC * 65], F32)
    S_bf = nc.alloc_sbuf_tensor("S_bf", [128, FC * 65], BF16)
    ATS = nc.alloc_sbuf_tensor("ATS", [128, 16 * 128], BF16)
    SQA = nc.alloc_sbuf_tensor("SQA", [128, 4 * 512], BF16)
    T32 = nc.alloc_sbuf_tensor("T32", [128, 2 * 512], F32)
    RSB = nc.alloc_sbuf_tensor("RSB", [128, 512], F32)
    ROWS = nc.alloc_sbuf_tensor("ROWS", [1, 1536], F32)
    WROW = nc.alloc_sbuf_tensor("WROW", [2, FC * 512], BF16)
    WK = nc.alloc_sbuf_tensor("WK", [128, FC * 2], F32)
    ATM = nc.alloc_sbuf_tensor("ATM", [128, FC * 128], BF16)
    RD2 = nc.alloc_sbuf_tensor("RD2", [128, FC * 4], F32)
    SV = nc.alloc_sbuf_tensor("SV", [128, SV_N], F32)
    DV = nc.alloc_sbuf_tensor("DV", [128, DV_N], F32)
    OBD = nc.alloc_sbuf_tensor("OBD", [128, 64 * LAY], F32)
    OM2 = nc.alloc_sbuf_tensor("OM2", [128, 128 * LAY], BF16)
    CF = nc.alloc_sbuf_tensor("CF", [128, 272], F32)
    CB = nc.alloc_sbuf_tensor("CB", [128, 131], BF16)
    CAST = [nc.alloc_sbuf_tensor(f"CAST{i}", [128, 1024], BF16) for i in range(2)]
    FP = nc.alloc_sbuf_tensor("FP", [128, 1], F32)
    FA = nc.alloc_sbuf_tensor("FA", [1, 2], F32)
    FD = nc.alloc_sbuf_tensor("FD", [1, 2], F32)

    pbt = [nc.alloc_psum_tensor(f"pb{i}", [128, 512], F32) for i in range(6)]
    pb = pbt
    pt = [nc.alloc_psum_tensor(f"pt{i}", [128, 1024], BF16) for i in range(2)]

    P = Prog()

    def sv_ap(nm, l, mi, n=1):
        return SV[:, SVC[(nm, l)] + mi: SVC[(nm, l)] + mi + n]

    def dv_ap(nm, l, mi):
        return DV[:, DVC[(nm, l)] + mi: DVC[(nm, l)] + mi + 1]

    for j, v in enumerate(CONSTVALS):
        nc.const_aps.register(F32, v, CF[:, 256 + j:257 + j])

    ONES_F = CF[0:1, 0:128]
    TRIU = CF[:, 0:128]
    E2 = CF[0:2, 128:256]
    IDENT = CB[:, 0:128]
    ONESCOL = CB[:, 128:129]
    BDONES = CB[:, 129:131]

    P.emit("pool", "dma_start", SV[:, :], svin, dma="cl", commit=False)
    P.emit("pool", "dma_start", OBD[:, :], obdin, dma="cl", commit=False)
    P.emit("pool", "dma_start", CF[:, :], cfin, dma="cl", commit=False)
    P.emit("pool", "dma_start", CB[:, :], cbin, dma="cl", commit=False)
    P.commit_dma("cl", writes=[("SV",), ("OBD",), ("CF",), ("CB",)])

    for l in range(LAY):
        P.emit("dve", "tensor_scalar_mul", DV[:, DVC[("bqs", l)]:DVC[("bqs", l)] + 8],
               sv_ap("bq", l, 0, 8), DHS, reads=[("SV",)], writes=[("DV",)],
               force_inc=True)
        P.emit("dve", "tensor_scalar_mul", DV[:, DVC[("bks", l)]:DVC[("bks", l)] + 8],
               sv_ap("bk", l, 0, 8), DHS, reads=[("SV",)], writes=[("DV",)],
               force_inc=True)
    P.emit("dve", "tensor_scalar_mul", DV[:, DVC[("s3h", None)]:DVC[("s3h", None)] + 8],
           sv_ap("s3", None, 0, 8), 0.5, reads=[("SV",)], writes=[("DV",)],
           force_inc=True)

    for l in range(LAY):
        ob = OBD[:, l * 64: l * 64 + 64]
        o2 = 128 * l
        P.emit("act", "activation", OM2[:, o2:o2 + 32], ob[:, 0:32], AF.Identity,
               reads=[("OBD",)], writes=[("OM2",)], force_inc=True)
        P.emit("act", "activation", OM2[:, o2 + 32:o2 + 64], ob[:, 0:32], AF.Identity,
               scale=-1.0, reads=[("OBD",)], writes=[("OM2",)], force_inc=True)
        P.emit("act", "activation", OM2[:, o2 + 64:o2 + 96], ob[:, 32:64], AF.Identity,
               reads=[("OBD",)], writes=[("OM2",)], force_inc=True)
        P.emit("act", "activation", OM2[:, o2 + 96:o2 + 128], ob[:, 32:64], AF.Identity,
               scale=-1.0, reads=[("OBD",)], writes=[("OM2",)], force_inc=True)

    P.emit("dve", "memset", S_sb[:, :], 0.0, writes=[("S", 0), ("S", 1), ("S", 2)], force_inc=True)
    for fc in range(FC):
        P.emit("dve", "memset", VT[:, fc * 130 + 64: fc * 130 + 65], 1.0,
               writes=[("VTones", fc)], force_inc=True)
        P.emit("dve", "memset", VT[:, fc * 130 + 129: fc * 130 + 130], 1.0,
               writes=[("VTones2", fc)], force_inc=True)
    P.emit("act", "memzero", FA[:, :], force_inc=True)
    P.emit("dve", "memset", FD[:, :], 0.0, force_inc=True)
    P.emit("pool", "memset", FP[:, :], 0.0, force_inc=True)

    def prepass():
        cnt = [0]

        def one(w, l, K, M, src3):
            pitch = {"w1": 2048, "w2": 2048}.get(w, K)
            for ki in range(K // 128):
                for mb in range(0, M, 1024):
                    slot = cnt[0] % 2
                    cnt[0] += 1
                    if src3 is None:
                        blk_src = {"w1": w1t, "w2": w2t, "h1": h1t,
                                   "h2": h2t, "w3": w3t}[w][
                                       ki * 128:(ki + 1) * 128, mb:mb + 1024]
                    else:
                        blk_src = src3[l, ki * 128:(ki + 1) * 128, mb:mb + 1024]
                    P.emit("pool", "dma_start", CAST[slot][:, 0:1024], blk_src,
                           dma=("pl", slot), writes=[("CAST", slot)])
                    nmi = min(8, M // 128 - mb // 128)
                    out = bass.AP(cache,
                                  stack_off(w, l, mb // 128) + ki * 128,
                                  [[CACHE_PP, 128], [pitch, nmi], [1, 128]])
                    src = CAST[slot].ap()[:, 0:1024].rearrange(
                        "p (m e) -> p m e", m=8)[:, 0:nmi, :]
                    P.emit("sp", "dma_start", out, src, dma=("ps", slot),
                           reads=[("CAST", slot)])

        one("w1", None, D, DFF, None)
        one("w2", None, D, DFF, None)
        one("h1", None, DFF, DFF, None)
        one("h2", None, DFF, DFF, None)
        one("w3", None, DFF, D, None)
        for l in range(LAY):
            one("wq", l, D, D, wq)
            one("wk", l, D, D, wk)
            one("wv", l, D, D, wv)
            one("wo", l, D, D, wo)
            one("wff1", l, D, DFF, wff1)
            one("wff2", l, DFF, D, wff2)

    prepass()

    P.begin_loop(NT)

    seq = (list(range(0, 88)) + list(range(88, 160)) + list(range(0, 88)))
    st = {"spos": 0, "issued": -1, "slot_chunk": [None] * NSLOT}

    def ensure(tpos):
        tgt = min(tpos + NSLOT - 2, len(seq) - 1)
        while st["issued"] < tgt:
            c = st["issued"] + 1
            slot = c % NSLOT
            cid = seq[c]
            src = bass.AP(cache, cid * CH, [[CACHE_PP, 128], [1, CH]])
            P.emit("sp", "dma_start", W[slot][:, :], src, dma=("rg", slot),
                   writes=[("W", slot)])
            st["issued"] = c
            st["slot_chunk"][slot] = c

    def next_stack(w, l, mi, K):
        assert st["spos"] % 1024 == 0
        exp = stack_off(w, l, mi)
        pieces = []
        for ki in range(K // 128):
            pos = st["spos"] + ki * 128
            tpos = pos // CH
            ensure(tpos)
            slot = tpos % NSLOT
            off = pos % CH
            assert seq[tpos] * CH + off == exp + ki * 128, (w, l, mi, ki)
            assert st["slot_chunk"][slot] == tpos,                 ("slot retargeted before consumption", w, l, mi, ki)
            pieces.append((W[slot], ("W", slot), off))
        st["spos"] += K
        return pieces

    rot = [0]

    def bank():
        b = rot[0] % 4
        rot[0] += 1
        return b

    def x_ap(fc, out=False):
        tns = yt if out else xt
        def mk(ctx, fc=fc, tns=tns):
            return bass.AP(tns, ctx.i * 512 + fc * 128 * S, [[S, 128], [1, 512]])
        return mk

    XSL = [XT[:, fc * 512:(fc + 1) * 512] for fc in range(FC)]
    RSL = [RS[:, fc * 512:(fc + 1) * 512] for fc in range(FC)]
    HSL = [HS[:, fc * 512:(fc + 1) * 512] for fc in range(FC)]
    RBL = [RB[:, fc * 512:(fc + 1) * 512] for fc in range(FC)]
    HBL = [HB[:, fc * 512:(fc + 1) * 512] for fc in range(FC)]
    AL = [BIGA[:, m * 512:(m + 1) * 512] for m in range(MCF)]
    BL = [BIGB[:, m * 512:(m + 1) * 512] for m in range(MCF)]

    def matmul_stack(w, l, mi, K, rhs_slices, rhs_keys, bi, extra_reads=(),
                     pieces=None):
        if pieces is None:
            pieces = next_stack(w, l, mi, K)
        n = len(pieces)
        for ki, (wt, wkey, off) in enumerate(pieces):
            P.emit("pe", "matmul", pb[bi][:, :], wt[:, off:off + 128],
                   rhs_slices[ki], start=(ki == 0), stop=(ki == n - 1),
                   reads=[wkey, rhs_keys[ki]] + list(extra_reads),
                   writes=[("pb", bi)])

    def bundle(stacks, banks, rhs_keys=()):
        rd = set(rhs_keys)
        for pieces in stacks:
            for _, wkey, _ in pieces:
                rd.add(wkey)
        P.emit("pe", "nop", reads=sorted(rd, key=repr),
               writes=[("pb", b) for b in banks])

    def ln_stats_fc(fc, src_slice, src_key):
        r = fc % 2
        P.emit("act", "copy", SQA[:, r * 512:(r + 1) * 512], src_slice,
               reads=[src_key], writes=[("SQA", r)])
        P.emit("dve", "tensor_mul", SQA[:, (2 + r) * 512:(3 + r) * 512],
               SQA[:, r * 512:(r + 1) * 512], SQA[:, r * 512:(r + 1) * 512],
               reads=[("SQA", r)], writes=[("SQB", r)])
        P.emit("pe", "matmul", pb[4][0:1, 0:512], ONESCOL,
               SQA[:, r * 512:(r + 1) * 512], start=(fc == 0), stop=(fc == 7),
               reads=[("SQA", r), ("CB",)], writes=[("pb", 4)])
        P.emit("pe", "matmul", pb[5][0:1, 0:512], ONESCOL,
               SQA[:, (2 + r) * 512:(3 + r) * 512], start=(fc == 0),
               stop=(fc == 7), reads=[("SQB", r), ("CB",)], writes=[("pb", 5)])

    def emit_ln(src_slices, src_keys, gamma=None, beta=None,
                out_f32=None, out_f32_keys=None, out_bf=None, out_bf_keys=None,
                stats_done=False):
        if not stats_done:
            for fc in range(FC):
                ln_stats_fc(fc, src_slices[fc], src_keys[fc])
        mu = ROWS[0:1, 0:512]
        ms = ROWS[0:1, 512:1024]
        tmp = ROWS[0:1, 1024:1536]
        P.emit("act", "activation", mu, pb[4][0:1, 0:512], AF.Identity, scale=1.0 / D,
               reads=[("pb", 4)], writes=[("mu",)])
        P.emit("act", "activation", ms, pb[5][0:1, 0:512], AF.Identity, scale=1.0 / D,
               reads=[("pb", 5)], writes=[("ms",)])
        P.emit("dve", "tensor_mul", tmp, mu, mu, reads=[("mu",)], writes=[("tmp",)])
        P.emit("dve", "tensor_sub", ms, ms, tmp, reads=[("ms",), ("tmp",)],
               writes=[("ms",)])
        P.emit("act", "activation", tmp, ms, AF.Ln, bias=LN_EPS,
               reads=[("ms",)], writes=[("tmp",)])
        P.emit("act", "activation", tmp, tmp, AF.Exp, scale=-0.5,
               reads=[("tmp",)], writes=[("tmp",)])
        P.emit("pe", "matmul", pb[4][:, :], ONES_F, mu, start=True, stop=True,
               reads=[("mu",), ("CF",)], writes=[("pb", 4)])
        P.emit("pe", "matmul", pb[5][:, :], ONES_F, tmp, start=True, stop=True,
               reads=[("tmp",), ("CF",)], writes=[("pb", 5)])
        P.emit("act", "copy", RSB[:, :], pb[5][:, :], reads=[("pb", 5)],
               writes=[("RSB",)])
        for fc in range(FC):
            r = fc % 2
            t32 = T32[:, r * 512:(r + 1) * 512]
            P.emit("dve", "tensor_sub", t32, src_slices[fc], pb[4][:, :],
                   reads=[src_keys[fc], ("pb", 4)], writes=[("T32", r)])
            if gamma is None:
                if out_f32 is not None:
                    P.emit("dve", "tensor_mul", out_f32[fc], t32, RSB[:, :],
                           reads=[("T32", r), ("RSB",)], writes=[out_f32_keys[fc]])
                    P.emit("act", "copy", out_bf[fc], out_f32[fc],
                           reads=[out_f32_keys[fc]], writes=[out_bf_keys[fc]])
                else:
                    P.emit("dve", "tensor_mul", out_bf[fc], t32, RSB[:, :],
                           reads=[("T32", r), ("RSB",)], writes=[out_bf_keys[fc]])
            else:
                P.emit("dve", "tensor_mul", t32, t32, RSB[:, :],
                       reads=[("T32", r), ("RSB",)], writes=[("T32", r)])
                P.emit("act", "activation", out_f32[fc], t32, AF.Identity,
                       scale=sv_ap(gamma[0], gamma[1], fc),
                       bias=sv_ap(beta[0], beta[1], fc),
                       reads=[("T32", r), ("SV",)], writes=[out_f32_keys[fc]])
                P.emit("act", "copy", out_bf[fc], out_f32[fc],
                       reads=[out_f32_keys[fc]], writes=[out_bf_keys[fc]])

    def emit_ffn(ln_hook=None):
        for mi in range(MCF):
            p1 = next_stack("w1", None, mi, D)
            p2 = next_stack("w2", None, mi, D)
            b1 = bank()
            b2 = bank()
            bundle([p1, p2], [b1, b2], [("RB", k) for k in range(FC)])
            matmul_stack("w1", None, mi, D, RBL, [("RB", k) for k in range(FC)], b1,
                         pieces=p1)
            matmul_stack("w2", None, mi, D, RBL, [("RB", k) for k in range(FC)], b2,
                         pieces=p2)
            P.emit("act", "activation", AL[mi], pb[b1][:, :], AF.Silu,
                   scale=sv_ap("s1", None, mi), reads=[("pb", b1), ("SV",)],
                   writes=[("A", mi)])
            P.emit("act", "activation", BL[mi], pb[b2][:, :], AF.Identity,
                   scale=sv_ap("s2", None, mi), reads=[("pb", b2), ("SV",)],
                   writes=[("B", mi)])
            P.emit("dve", "tensor_mul", AL[mi], AL[mi], BL[mi],
                   reads=[("A", mi), ("B", mi)], writes=[("A", mi)])
        for mi in range(MCF):
            bi = bank()
            matmul_stack("h1", None, mi, DFF, AL, [("A", k) for k in range(MCF)], bi)
            P.emit("act", "activation", BL[mi], pb[bi][:, :], AF.Gelu,
                   scale=sv_ap("sh1", None, mi), reads=[("pb", bi), ("SV",)],
                   writes=[("B", mi)])
        for mi in range(MCF):
            bi = bank()
            matmul_stack("h2", None, mi, DFF, BL, [("B", k) for k in range(MCF)], bi)
            P.emit("act", "activation", AL[mi], pb[bi][:, :], AF.Gelu,
                   scale=sv_ap("sh2", None, mi), reads=[("pb", bi), ("SV",)],
                   writes=[("A", mi)])
        for mi in range(FC):
            bi = bank()
            matmul_stack("w3", None, mi, DFF, AL, [("A", k) for k in range(MCF)], bi)
            r = mi % 2
            t32 = T32[:, r * 512:(r + 1) * 512]
            P.emit("act", "activation", t32, pb[bi][:, :], AF.Identity,
                   scale=dv_ap("s3h", None, mi), reads=[("pb", bi), ("DV",)],
                   writes=[("T32", r)])
            P.emit("dve", "tensor_add", XSL[mi], XSL[mi], t32,
                   reads=[("XT", mi), ("T32", r)], writes=[("XT", mi)])
            if ln_hook is not None:
                ln_hook(mi)

    def emit_favor(l, base, side):
        for fc in range(FC):
            p = fc % 2
            bs, bu = 3 * p, 3 * p + 1
            buf = BL[base + fc]
            bkey = ("B", base + fc)
            if side == "k":
                r = fc % 2
                sq = SQA[:, r * 512:(r + 1) * 512]
                P.emit("act", "activation", sq, buf, AF.Square,
                       reads=[bkey], writes=[("SQA", r)])
                P.emit("pe", "matmul", pb[bs][0:2, 0:512], BDONES, sq,
                       start=True, stop=True, reads=[("SQA", r), ("CB",)],
                       writes=[("pb", bs)])
                P.emit("act", "activation", WROW[0:2, fc * 512:(fc + 1) * 512],
                       pb[bs][0:2, 0:512], AF.Exp, scale=-0.5, bias=LNC,
                       reads=[("pb", bs)], writes=[("WR", fc)])
            P.emit("pe", "matmul", pb[bu][:, :], OM2[:, l * 128:(l + 1) * 128],
                   buf, start=True, stop=True, reads=[bkey, ("OM2",)],
                   writes=[("pb", bu)])
            P.emit("act", "activation", buf, pb[bu][:, :], AF.Exp,
                   reads=[("pb", bu)], writes=[bkey])

    def emit_la_chunk(l, tc):
        ts = tc * 128
        lb = l * FC * 65
        P.emit("act", "copy", S_bf[:, :], S_sb[:, lb:lb + FC * 65],
               reads=[("S", l)], writes=[("Sbf",)])
        for fc in range(FC):
            p = fc % 2
            kbuf = BL[8 + fc]
            vbuf = BL[16 + fc]
            P.emit("pe", "transpose", pt[p][:, 384:386],
                   WROW[0:2, fc * 512 + ts: fc * 512 + ts + 128], IDENT[0:2, 0:2],
                   reads=[("WR", fc), ("CB",)], writes=[("pt", p)])
            P.emit("pe", "transpose", pt[p][:, 0:128], kbuf[:, ts:ts + 128],
                   IDENT, reads=[("B", 8 + fc), ("CB",)], writes=[("pt", p)])
            P.emit("pe", "transpose", pt[p][:, 128:256], vbuf[:, ts:ts + 128],
                   IDENT, reads=[("B", 16 + fc), ("CB",)], writes=[("pt", p)])
            P.emit("dve", "tensor_copy", WK[:, fc * 2:(fc + 1) * 2],
                   pt[p][:, 384:386], reads=[("pt", p)], writes=[("WK", fc)])
            P.emit("dve", "tensor_scalar_mul", KT[:, fc * 128:fc * 128 + 64],
                   pt[p][:, 0:64], WK[:, fc * 2:fc * 2 + 1],
                   reads=[("pt", p), ("WK", fc)], writes=[("KT", fc)])
            P.emit("dve", "tensor_scalar_mul", KT[:, fc * 128 + 64:fc * 128 + 128],
                   pt[p][:, 64:128], WK[:, fc * 2 + 1:fc * 2 + 2],
                   reads=[("pt", p), ("WK", fc)], writes=[("KT", fc)])
            vt_dst = VT.ap()[:, fc * 130: fc * 130 + 130].rearrange(
                "p (s c) -> p s c", s=2, c=65)[:, :, 0:64]
            P.emit("dve", "tensor_copy", vt_dst,
                   pt[p][:, 128:256].rearrange("p (s c) -> p s c", s=2, c=64),
                   reads=[("pt", p)], writes=[("VT", fc)])
        for fc in range(FC):
            p = fc % 2
            bAe, bAo = 2 * p, 2 * p + 1
            qb = BL[0 + fc]
            qk = ("B", 0 + fc)
            kb = BL[8 + fc]
            kk = ("B", 8 + fc)
            P.emit("pe", "matmul", pb[bAe][:, 0:128], kb[0:64, ts:ts + 128],
                   qb[0:64, ts:ts + 128], start=True, stop=True,
                   reads=[kk, qk], writes=[("pb", bAe)])
            P.emit("pe", "matmul", pb[bAo][:, 0:128], kb[64:128, ts:ts + 128],
                   qb[64:128, ts:ts + 128], start=True, stop=True,
                   reads=[kk, qk], writes=[("pb", bAo)])
            P.emit("dve", "scalar_tensor_tensor", ATS[:, (2 * fc) * 128:(2 * fc + 1) * 128],
                   pb[bAe][:, 0:128], WK[:, fc * 2:fc * 2 + 1], TRIU,
                   ALU.mult, ALU.mult,
                   reads=[("pb", bAe), ("WK", fc), ("CF",)], writes=[("ATS", 2 * fc)])
            P.emit("dve", "scalar_tensor_tensor", ATS[:, (2 * fc + 1) * 128:(2 * fc + 2) * 128],
                   pb[bAo][:, 0:128], WK[:, fc * 2 + 1:fc * 2 + 2], TRIU,
                   ALU.mult, ALU.mult,
                   reads=[("pb", bAo), ("WK", fc), ("CF",)], writes=[("ATS", 2 * fc + 1)])
        for fc in range(FC):
            p = fc % 2
            bO = 4 + p
            qb = BL[0 + fc]
            qk = ("B", 0 + fc)
            P.emit("pe", "matmul", pb[bO][:, 0:65],
                   qb[0:64, ts:ts + 128], S_bf[0:64, fc * 65:(fc + 1) * 65],
                   start=True, stop=False, reads=[qk, ("Sbf",)],
                   writes=[("pb", bO)])
            P.emit("pe", "matmul", pb[bO][:, 0:65],
                   ATS[:, (2 * fc) * 128:(2 * fc + 1) * 128],
                   VT[:, fc * 130: fc * 130 + 65],
                   start=False, stop=True,
                   reads=[("ATS", 2 * fc), ("VT", fc), ("VTones", fc)],
                   writes=[("pb", bO)])
            P.emit("pe", "matmul", pb[bO][:, 65:130],
                   qb[64:128, ts:ts + 128], S_bf[64:128, fc * 65:(fc + 1) * 65],
                   start=True, stop=False, reads=[qk, ("Sbf",)],
                   writes=[("pb", bO)])
            P.emit("pe", "matmul", pb[bO][:, 65:130],
                   ATS[:, (2 * fc + 1) * 128:(2 * fc + 2) * 128],
                   VT[:, fc * 130 + 65: fc * 130 + 130],
                   start=False, stop=True,
                   reads=[("ATS", 2 * fc + 1), ("VT", fc), ("VTones2", fc)],
                   writes=[("pb", bO)])
            dens = bass.AP(pbt[bO], 64, [[512, 128], [65, 2]])
            P.emit("act", "activation", RD2[:, fc * 4: fc * 4 + 2], dens,
                   AF.Identity, bias=ATTN_EPS, reads=[("pb", bO)],
                   writes=[("RDr", fc)])
            P.emit("dve", "reciprocal", RD2[:, fc * 4 + 2: fc * 4 + 4],
                   RD2[:, fc * 4: fc * 4 + 2], reads=[("RDr", fc)],
                   writes=[("RD", fc)])
            P.emit("act", "activation", ATM[:, fc * 128: fc * 128 + 64],
                   pb[bO][:, 0:64], AF.Identity,
                   scale=RD2[:, fc * 4 + 2: fc * 4 + 3],
                   reads=[("pb", bO), ("RD", fc)], writes=[("ATM", fc)])
            P.emit("act", "activation", ATM[:, fc * 128 + 64: fc * 128 + 128],
                   pb[bO][:, 65:129], AF.Identity,
                   scale=RD2[:, fc * 4 + 3: fc * 4 + 4],
                   reads=[("pb", bO), ("RD", fc)], writes=[("ATM", fc)])
        for fc in range(FC):
            p = fc % 2
            bAo = 2 * p + 1
            P.emit("pe", "transpose", pt[p][:, 256:384],
                   ATM[:, fc * 128:(fc + 1) * 128], IDENT,
                   reads=[("ATM", fc), ("CB",)], writes=[("pt", p)])
            P.emit("dve", "tensor_copy", BL[24 + fc][:, ts:ts + 128],
                   pt[p][:, 256:384], reads=[("pt", p)], writes=[("B", 24 + fc)])
            P.emit("pe", "matmul", pb[bAo][0:64, 128:193],
                   KT[:, fc * 128: fc * 128 + 64],
                   VT[:, fc * 130: fc * 130 + 65], start=True, stop=True,
                   reads=[("KT", fc), ("VT", fc), ("VTones", fc)],
                   writes=[("pb", bAo)])
            P.emit("pe", "matmul", pb[bAo][64:128, 128:193],
                   KT[:, fc * 128 + 64: fc * 128 + 128],
                   VT[:, fc * 130 + 65: fc * 130 + 130], start=True, stop=True,
                   tile_position=(0, 64),
                   reads=[("KT", fc), ("VT", fc), ("VTones2", fc)],
                   writes=[("pb", bAo)])
            P.emit("dve", "tensor_add", S_sb[:, lb + fc * 65:lb + (fc + 1) * 65],
                   S_sb[:, lb + fc * 65:lb + (fc + 1) * 65], pb[bAo][:, 128:193],
                   reads=[("S", l), ("pb", bAo)], writes=[("S", l)])

    def emit_attn(l):
        for (wname, base, scale, bias_nm, bias_dv) in (
                ("wq", 0, DHS, "bqs", True), ("wk", 8, DHS, "bks", True),
                ("wv", 16, 1.0, "bv", False)):
            for mi0 in range(0, FC, 2):
                ps = [next_stack(wname, l, mi0, D), next_stack(wname, l, mi0 + 1, D)]
                bs2 = [bank(), bank()]
                bundle(ps, bs2, [("RB", k) for k in range(FC)])
                for j, mi in enumerate((mi0, mi0 + 1)):
                    matmul_stack(wname, l, mi, D, RBL,
                                 [("RB", k) for k in range(FC)], bs2[j],
                                 pieces=ps[j])
                    b_ap = (dv_ap(bias_nm, l, mi) if bias_dv
                            else sv_ap(bias_nm, l, mi))
                    P.emit("act", "activation", BL[base + mi], pb[bs2[j]][:, :],
                           AF.Identity, scale=scale, bias=b_ap,
                           reads=[("pb", bs2[j]), ("DV",) if bias_dv else ("SV",)],
                           writes=[("B", base + mi)])
        emit_favor(l, 0, "q")
        emit_favor(l, 8, "k")
        for tc in range(TC):
            emit_la_chunk(l, tc)
        for mi in range(FC):
            bi = bank()
            matmul_stack("wo", l, mi, D, [BL[24 + k] for k in range(FC)],
                         [("B", 24 + k) for k in range(FC)], bi)
            r = mi % 2
            t32 = T32[:, r * 512:(r + 1) * 512]
            P.emit("act", "activation", t32, pb[bi][:, :], AF.Identity,
                   bias=sv_ap("bo", l, mi), reads=[("pb", bi), ("SV",)],
                   writes=[("T32", r)])
            P.emit("dve", "tensor_add", HSL[mi], RSL[mi], t32,
                   reads=[("RS", mi), ("T32", r)], writes=[("HS", mi)])
            ln_stats_fc(mi, HSL[mi], ("HS", mi))
        if DEV_STAGE == f"wo{l}":
            return
        emit_ln(HSL, [("HS", k) for k in range(FC)], gamma=("g1", l),
                beta=("b1", l), out_f32=HSL,
                out_f32_keys=[("HS", k) for k in range(FC)],
                out_bf=HBL, out_bf_keys=[("HB", k) for k in range(FC)],
                stats_done=True)
        if DEV_STAGE == f"ln1_{l}":
            return
        for mi0 in range(0, MCF, 2):
            ps = [next_stack("wff1", l, mi0, D), next_stack("wff1", l, mi0 + 1, D)]
            bs2 = [bank(), bank()]
            bundle(ps, bs2, [("HB", k) for k in range(FC)])
            for j, mi in enumerate((mi0, mi0 + 1)):
                matmul_stack("wff1", l, mi, D, HBL,
                             [("HB", k) for k in range(FC)], bs2[j], pieces=ps[j])
                P.emit("act", "activation", AL[mi], pb[bs2[j]][:, :], AF.Relu,
                       bias=sv_ap("bf1", l, mi), reads=[("pb", bs2[j]), ("SV",)],
                       writes=[("A", mi)])
        for mi in range(FC):
            bi = bank()
            matmul_stack("wff2", l, mi, DFF, AL, [("A", k) for k in range(MCF)], bi)
            r = mi % 2
            t32 = T32[:, r * 512:(r + 1) * 512]
            P.emit("act", "activation", t32, pb[bi][:, :], AF.Identity,
                   bias=sv_ap("bf2", l, mi), reads=[("pb", bi), ("SV",)],
                   writes=[("T32", r)])
            P.emit("dve", "tensor_add", HSL[mi], HSL[mi], t32,
                   reads=[("HS", mi), ("T32", r)], writes=[("HS", mi)])
            ln_stats_fc(mi, HSL[mi], ("HS", mi))
        if DEV_STAGE == f"y{l}":
            return
        emit_ln(HSL, [("HS", k) for k in range(FC)], gamma=("g2", l),
                beta=("b2", l), out_f32=RSL,
                out_f32_keys=[("RS", k) for k in range(FC)],
                out_bf=RBL, out_bf_keys=[("RB", k) for k in range(FC)],
                stats_done=True)

    for fc in range(FC):
        P.emit("pool", "dma_start", XSL[fc], x_ap(fc), dma="xl", commit=False)
    P.commit_dma("xl", writes=[("XT", k) for k in range(FC)])

    def store(slices, keys):
        for fc in range(FC):
            P.emit("pool", "dma_start", x_ap(fc, out=True), slices[fc],
                   dma="yo", reads=[keys[fc]], writes=[("yo", fc)])

    done = False
    while True:
        emit_ln(XSL, [("XT", k) for k in range(FC)], out_bf=RBL,
                out_bf_keys=[("RB", k) for k in range(FC)])
        emit_ffn(ln_hook=(lambda mi: ln_stats_fc(mi, XSL[mi], ("XT", mi)))
                 if DEV_STAGE not in ("ffn1",) else None)
        if DEV_STAGE == "ffn1":
            store(XSL, [("XT", k) for k in range(FC)]); break
        emit_ln(XSL, [("XT", k) for k in range(FC)], out_f32=RSL,
                out_f32_keys=[("RS", k) for k in range(FC)], out_bf=RBL,
                out_bf_keys=[("RB", k) for k in range(FC)], stats_done=True)
        if DEV_STAGE == "preln":
            store(RSL, [("RS", k) for k in range(FC)]); break
        brk = False
        for l in range(LAY):
            emit_attn(l)
            if DEV_STAGE in (f"l{l}", f"wo{l}", f"ln1_{l}", f"y{l}"):
                brk = True
                break
        if brk:
            if DEV_STAGE.startswith("l") and not DEV_STAGE.startswith("ln1") and not DEV_STAGE.startswith("y"):
                store(RSL, [("RS", k) for k in range(FC)])
            else:
                store(HSL, [("HS", k) for k in range(FC)])
            break
        for fc in range(FC):
            P.emit("dve", "tensor_add", XSL[fc], XSL[fc], RSL[fc],
                   reads=[("XT", fc), ("RS", fc)], writes=[("XT", fc)])
            ln_stats_fc(fc, XSL[fc], ("XT", fc))
        if DEV_STAGE == "xadd":
            store(XSL, [("XT", k) for k in range(FC)]); break
        emit_ln(XSL, [("XT", k) for k in range(FC)], out_bf=RBL,
                out_bf_keys=[("RB", k) for k in range(FC)], stats_done=True)
        emit_ffn()
        store(XSL, [("XT", k) for k in range(FC)])
        done = True
        break

    if done:
        assert st["spos"] == len(seq) * CH, st["spos"]
    P.end_loop()
    P.finalize(nc, fences={"pool": FP[:, :], "act": FA[0:1, 0:1],
                           "dve": FD[0:1, 0:1]})
    n_ops = sum(len(P.ops[e]) for e in ENGINES)
    return nc, n_ops, P.n_waits


def host_prepare(inputs):
    f = np.ascontiguousarray

    def svblock(v):
        v = np.asarray(v)
        return f(v.reshape(-1, 128).T)

    sv = np.zeros((128, SV_N), np.float32)
    for l in range(LAY):
        for nm, src in (("bq", inputs["L0_attn_bq"][l]),
                        ("bk", inputs["L0_attn_bk"][l]),
                        ("bv", inputs["L0_attn_bv"][l]),
                        ("bo", inputs["L0_attn_bo"][l]),
                        ("g1", inputs["L0_attn_ln1_g"][l]),
                        ("b1", inputs["L0_attn_ln1_b"][l]),
                        ("g2", inputs["L0_attn_ln2_g"][l]),
                        ("b2", inputs["L0_attn_ln2_b"][l]),
                        ("bf1", inputs["L0_attn_bf1"][l]),
                        ("bf2", inputs["L0_attn_bf2"][l])):
            blk = svblock(src)
            sv[:, SVC[(nm, l)]:SVC[(nm, l)] + blk.shape[1]] = blk
    for nm, src in (("s1", inputs["L0_ffn_w1_s"]), ("s2", inputs["L0_ffn_w2_s"]),
                    ("sh1", inputs["L0_ffn_h1_s"]), ("sh2", inputs["L0_ffn_h2_s"]),
                    ("s3", inputs["L0_ffn_w3_s"])):
        blk = svblock(src)
        sv[:, SVC[(nm, None)]:SVC[(nm, None)] + blk.shape[1]] = blk

    obd = np.zeros((128, 64 * LAY), np.float32)
    om = np.asarray(inputs["L0_attn_omega"])
    for l in range(LAY):
        obd[0:64, l * 64:l * 64 + 32] = om[l]
        obd[64:128, l * 64 + 32:l * 64 + 64] = om[l]

    cf = np.zeros((128, 272), np.float32)
    cf[:, 0:128] = np.triu(np.ones((128, 128), np.float32))
    cf[0, 128:192] = 1.0
    cf[1, 192:256] = 1.0
    cf[:, 256:256 + len(CONSTVALS)] = np.asarray(CONSTVALS, np.float32)[None, :]
    cb = np.zeros((128, 131), np.float32)
    cb[:, 0:128] = np.eye(128, dtype=np.float32)
    cb[:, 128] = 1.0
    cb[0:64, 129] = 1.0
    cb[64:128, 130] = 1.0
    cb = cb.astype(ml_dtypes.bfloat16)

    shared = dict(
        w1t=f(np.asarray(inputs["L0_ffn_w1_w"]).T),
        w2t=f(np.asarray(inputs["L0_ffn_w2_w"]).T),
        h1t=f(np.asarray(inputs["L0_ffn_h1_w"]).T),
        h2t=f(np.asarray(inputs["L0_ffn_h2_w"]).T),
        w3t=f(np.asarray(inputs["L0_ffn_w3_w"]).T),
        wq=f(np.asarray(inputs["L0_attn_Wq"])),
        wk=f(np.asarray(inputs["L0_attn_Wk"])),
        wv=f(np.asarray(inputs["L0_attn_Wv"])),
        wo=f(np.asarray(inputs["L0_attn_Wo"])),
        wff1=f(np.asarray(inputs["L0_attn_W1"])),
        wff2=f(np.asarray(inputs["L0_attn_W2"])),
        svin=sv, obdin=obd, cfin=cf, cbin=cb,
    )
    x = np.asarray(inputs["L0_attn_x"])
    xts = [f(x[c].T).reshape(-1) for c in range(B)]
    return shared, xts


_CACHE = {}


def kernel(**inputs):
    shared, xts = host_prepare(inputs)
    if "nc" not in _CACHE:
        nc, n_ops, n_waits = build_nc(NTILE)
        _CACHE["nc"] = nc
    nc = _CACHE["nc"]
    in_maps = [dict(shared, xt=xts[c]) for c in range(B)]
    res = run_bass_kernel_spmd(nc, in_maps, core_ids=list(range(B)))
    out = np.empty((B, S, D), np.float32)
    for c in range(B):
        out[c] = res.results[c]["yt"].reshape(D, S).T
    return out


if __name__ == "__main__":
    nc, n_ops, n_waits = build_nc(1)
    print(f"ops={n_ops} waits={n_waits}")


# version 04
# speedup vs baseline: 1.0448x; 1.0448x over previous
import numpy as np
import ml_dtypes
import concourse.bass as bass
import concourse.mybir as mybir
from concourse.bass_utils import run_bass_kernel_spmd

F32 = mybir.dt.float32
BF16 = mybir.dt.bfloat16
AF = mybir.ActivationFunctionType
ALU = mybir.AluOpType

D = 1024
DFF = 4096
S = 8192
B = 8
T = 512
NTILE = S // T
FC = D // 128
MCF = DFF // 128
TC = T // 128
NH = 16
DH = 64
LAY = 3
DHS = float(DH) ** -0.25
LNC = float(np.log(2 * 32) * -0.5)
LN_EPS = 1e-5
ATTN_EPS = 1e-6
NSLOT = 6
CH = 2048
CONSTVALS = (1.0 / D, LN_EPS, -0.5, 0.5, -LNC, LNC, ATTN_EPS, DHS, -1.0, 1.0,
             0.0, 2.0, 0.25)

ENGINES = ("pe", "act", "dve", "pool", "sp")
PARANOID = frozenset(("act", "dve", "pool"))


class Ctx:
    def __init__(self):
        self.i = None


class Prog:
    def __init__(self):
        self.ops = {e: [] for e in ENGINES}
        self.waits = {e: [] for e in ENGINES}
        self.opclock = {e: [] for e in ENGINES}
        self.dmaclock = {}
        self.clock = {e: {} for e in ENGINES}
        self.group_count = {}
        self.group_engine = {}
        self.need_inc = {e: set() for e in ENGINES}
        self.res = {}
        self.n_waits = 0
        self.seg = 0
        self.seg_start = {e: [0] for e in ENGINES}
        self.loop = None
        self.group_final = []
        self._seen_groups = set()

    def _need(self, eng, clk, needed, p):
        if p is None:
            return
        if p[0] == "e":
            if p[1] == eng and eng not in PARANOID:
                return
            sk, v = ("e", p[1]), p[2]
        else:
            sk, v = ("d", p[1]), p[2]
        if clk.get(sk, -1) >= v:
            return
        if needed.get(sk, -1) < v:
            needed[sk] = v

    def emit(self, eng, fn, *args, reads=(), writes=(), dma=None, commit=True,
             force_inc=False, **kwargs):
        pos = len(self.ops[eng])
        clk = self.clock[eng]
        needed = {}
        for k in reads:
            st = self.res.get(k)
            if st is not None:
                self._need(eng, clk, needed, st[0])
        for k in writes:
            st = self.res.get(k)
            if st is not None:
                self._need(eng, clk, needed, st[0])
                for r in st[1]:
                    self._need(eng, clk, needed, r)
        wl = []
        for sk, v in sorted(needed.items(), key=repr):
            if sk[0] == "e":
                self.need_inc[sk[1]].add((sk[1], v))
                snap = self.opclock[sk[1]][v]
            else:
                snap = self.dmaclock[(sk[1], v)]
            for k2, v2 in snap.items():
                if clk.get(k2, -1) < v2:
                    clk[k2] = v2
            if clk.get(sk, -1) < v:
                clk[sk] = v
            wl.append((sk, v))
        self.n_waits += len(wl)
        self.waits[eng].append(wl)
        self.ops[eng].append((fn, args, kwargs, dma, self.seg))
        if force_inc and dma is None:
            self.need_inc[eng].add((eng, pos))
        if dma is not None:
            assert self.group_engine.setdefault(dma, eng) == eng
            cnt = self.group_count.get(dma, 0) + 1
            self.group_count[dma] = cnt
            me = ("d", dma, cnt)
            self.dmaclock[(dma, cnt)] = dict(clk)
            self.opclock[eng].append(dict(clk))
        else:
            me = ("e", eng, pos)
            snap = dict(clk)
            snap[("e", eng)] = pos
            self.opclock[eng].append(snap)
        if commit:
            self._register(me, reads, writes)
        return me

    def commit_dma(self, group, reads=(), writes=()):
        cnt = self.group_count[group]
        self._register(("d", group, cnt), reads, writes)

    def _register(self, me, reads, writes):
        for k in writes:
            self.res[k] = [me, []]
        for k in reads:
            st = self.res.get(k)
            if st is None:
                st = [None, []]
                self.res[k] = st
            st[1].append(me)

    def _cut_segment(self, marker):
        for g in self.group_count:
            assert g not in self._seen_groups, f"group {g} in two segments"
        self._seen_groups |= set(self.group_count)
        self.group_final.append(dict(self.group_count))
        for e in ENGINES:
            self.ops[e].append((marker, None, None, None, self.seg))
            self.waits[e].append([])
            self.opclock[e].append({})
        self.seg += 1
        for e in ENGINES:
            self.seg_start[e].append(len(self.ops[e]))
            self.clock[e] = {}
        self.group_count = {}
        self.dmaclock = {}
        self.res = {k: [None, []] for k in self.res}

    def begin_loop(self, n_iters):
        assert self.loop is None
        self._cut_segment("@LOOP")
        self.loop = n_iters

    def end_loop(self):
        self._cut_segment("@ENDLOOP")

    def finalize(self, nc, fences=None):
        if fences is None:
            fences = {}
        self._cut_segment("@END")
        sems = {}
        for e in ENGINES:
            sems[("e", e)] = nc.alloc_semaphore(f"sem_{e}")
        for g in self.group_engine:
            nm = "_".join(map(str, g)) if isinstance(g, tuple) else str(g)
            sems[("d", g)] = nc.alloc_semaphore(f"dg_{nm}")
        bar = nc.alloc_semaphore("bar")
        bar2 = nc.alloc_semaphore("bar2")
        clear_ids = [sems[("e", e)] for e in ENGINES]
        body_per_iter = (self.group_final[1]
                         if self.loop is not None and len(self.group_final) > 1
                         else {})

        def seg_of(e, pos):
            lo = 0
            for si, st in enumerate(self.seg_start[e]):
                if st <= pos:
                    lo = si
            return lo

        inc_rank = {e: {} for e in ENGINES}
        inc_totals = {e: {} for e in ENGINES}
        for e in ENGINES:
            for p in sorted(p for (_, p) in self.need_inc[e]):
                s = seg_of(e, p)
                inc_totals[e][s] = inc_totals[e].get(s, 0) + 1
                inc_rank[e][p] = inc_totals[e][s]
        markedset = {e: set(p for (_, p) in self.need_inc[e]) for e in ENGINES}
        n_eng = len(ENGINES)
        loop_n = self.loop
        engmap = dict(pe="tensor", act="scalar", dve="vector", pool="gpsimd",
                      sp="sync")
        enumap = dict(pe=mybir.EngineType.PE, act=mybir.EngineType.Activation,
                      dve=mybir.EngineType.DVE, pool=mybir.EngineType.Pool,
                      sp=mybir.EngineType.SP)

        dyn_pairs = {e: [] for e in ENGINES}
        if loop_n is not None:
            for e in ENGINES:
                seen = dict()
                lo = self.seg_start[e][1]
                hi = self.seg_start[e][2] - 1
                for i in range(lo, hi):
                    for sk, v in self.waits[e][i]:
                        if sk[0] == "d" and sk[1] not in seen:
                            seen[sk[1]] = True
                for g, cnt in self.group_final[1].items():
                    if self.group_engine[g] == e and cnt and g not in seen:
                        seen[g] = True
                dyn_pairs[e] = list(seen)
        waitregs = {}
        for e in ENGINES:
            for g in dyn_pairs[e]:
                rh = nc.alloc_registers(f"wr_{e}_{g}", engines=[enumap[e]])
                waitregs[(e, g)] = list(rh)[0]
        cur_target = {}

        def dyn_wait(engine, e, g, target):
            reg = waitregs[(e, g)]
            cur = cur_target.get((e, g), 0)
            delta = target - cur
            assert delta >= 0, (e, g, target, cur)
            if delta:
                engine.reg_alu(reg, reg, delta, mybir.AluOpType.add)
            engine.wait_ge(sems[("d", g)], reg)
            cur_target[(e, g)] = target

        def emit_barrier(engine, e, ctx, seg_idx, in_loop):
            gf = self.group_final[seg_idx]
            for g, cnt in sorted(gf.items(), key=repr):
                if self.group_engine[g] == e and cnt:
                    if in_loop:
                        dyn_wait(engine, e, g, 16 * cnt)
                    else:
                        engine.wait_ge(sems[("d", g)], 16 * cnt)
            if in_loop:
                per_map = self.group_final[1]
                for g in dyn_pairs[e]:
                    per = per_map.get(g, 0)
                    tgt = 16 * per
                    cur = cur_target.get((e, g), 0)
                    if tgt != cur:
                        engine.reg_alu(waitregs[(e, g)], waitregs[(e, g)],
                                       tgt - cur, mybir.AluOpType.add)
                    cur_target[(e, g)] = 0
            for e2 in ENGINES:
                tot = inc_totals[e2].get(seg_idx, 0)
                if tot:
                    engine.wait_ge(sems[("e", e2)], tot)
            if e in ("pe", "sp") or fences.get(e) is None:
                engine.nop().then_inc(bar, 1)
            elif e == "act":
                engine.copy(fences[e], fences[e]).then_inc(bar, 1)
            else:
                engine.memset(fences[e], 0.0).then_inc(bar, 1)
            nbar = seg_idx + 1
            if in_loop:
                barv = n_eng * nbar + n_eng * ctx.i
                bar2v = nbar + ctx.i
            else:
                barv = n_eng * nbar
                bar2v = nbar
            if e == "pool":
                engine.wait_ge(bar, barv)
                for s in clear_ids:
                    engine.sem_clear(s)
                engine.nop().then_inc(bar2, 1)
            else:
                engine.wait_ge(bar2, bar2v)

        with nc.Block() as blk:
            for e in ENGINES:
                def body(engine, e=e):
                    ctx = Ctx()
                    esem = sems[("e", e)]

                    def run_ops(lo, hi):
                        for i in range(lo, hi):
                            fn, args, kwargs, dma, _seg = self.ops[e][i]
                            for sk, v in self.waits[e][i]:
                                if sk[0] == "e":
                                    engine.wait_ge(sems[sk], inc_rank[sk[1]][v])
                                elif ctx.i is not None:
                                    dyn_wait(engine, e, sk[1], 16 * v)
                                else:
                                    engine.wait_ge(sems[sk], 16 * v)
                            rargs = [a(ctx) if callable(a) else a for a in args]
                            inst = getattr(engine, fn)(*rargs, **kwargs)
                            if dma is not None:
                                inst.then_inc(sems[("d", dma)], 16)
                            elif i in markedset[e]:
                                inst.then_inc(esem, 1)

                    marks = [i for i, op in enumerate(self.ops[e])
                             if op[0] in ("@LOOP", "@ENDLOOP", "@END")]
                    assert loop_n is not None
                    mloop, mend, mfin = marks
                    run_ops(0, mloop)
                    emit_barrier(engine, e, ctx, self.ops[e][mloop][4], False)
                    for g in dyn_pairs[e]:
                        engine.reg_mov(waitregs[(e, g)], 0)
                    with engine.Fori(0, loop_n) as iv:
                        ctx.i = iv
                        run_ops(mloop + 1, mend)
                        emit_barrier(engine, e, ctx, self.ops[e][mend][4], True)
                    ctx.i = None
                    run_ops(mend + 1, mfin)

                getattr(blk, engmap[e])(body)
        return sems


FFN_SZ = 2 * 32 * 1024 + 2 * 32 * 4096 + 8 * 4096
ATT_SZ = 4 * 8 * 1024 + 32 * 1024 + 8 * 4096
CACHE_PP = FFN_SZ + LAY * ATT_SZ


def stack_off(w, l, mi):
    if w == "w1":
        return mi * 2048
    if w == "w2":
        return mi * 2048 + 1024
    if w == "h1":
        return 65536 + mi * 4096
    if w == "h2":
        return 196608 + mi * 4096
    if w == "w3":
        return 327680 + mi * 4096
    base = FFN_SZ + l * ATT_SZ
    if w == "wq":
        return base + mi * 1024
    if w == "wk":
        return base + 8192 + mi * 1024
    if w == "wv":
        return base + 16384 + mi * 1024
    if w == "wo":
        return base + 24576 + mi * 1024
    if w == "wff1":
        return base + 32768 + mi * 1024
    if w == "wff2":
        return base + 65536 + mi * 4096
    raise KeyError(w)


def sv_cols():
    cols = {}
    c = 0
    for l in range(LAY):
        for nm, n in (("bq", 8), ("bk", 8), ("bv", 8), ("bo", 8),
                      ("g1", 8), ("b1", 8), ("g2", 8), ("b2", 8),
                      ("bf1", 32), ("bf2", 8)):
            cols[(nm, l)] = c
            c += n
    for nm, n in (("s1", 32), ("s2", 32), ("sh1", 32), ("sh2", 32), ("s3", 8)):
        cols[(nm, None)] = c
        c += n
    return cols, c


SVC, SV_N = sv_cols()
DVC = {}
_c = 0
for l in range(LAY):
    DVC[("bqs", l)] = _c; _c += 8
    DVC[("bks", l)] = _c; _c += 8
DVC[("s3h", None)] = _c; _c += 8
DV_N = _c


import os as _os
DEV_STAGE = _os.environ.get("K_STAGE", "")


def build_nc(NT=NTILE):
    nc = bass.Bass("TRN2", target_bir_lowering=False)

    xt = nc.dram_tensor("xt", [D * S], F32, kind="ExternalInput")
    yt = nc.dram_tensor("yt", [D * S], F32, kind="ExternalOutput")
    w1t = nc.dram_tensor("w1t", [D, DFF], F32, kind="ExternalInput").ap()
    w2t = nc.dram_tensor("w2t", [D, DFF], F32, kind="ExternalInput").ap()
    h1t = nc.dram_tensor("h1t", [DFF, DFF], F32, kind="ExternalInput").ap()
    h2t = nc.dram_tensor("h2t", [DFF, DFF], F32, kind="ExternalInput").ap()
    w3t = nc.dram_tensor("w3t", [DFF, D], F32, kind="ExternalInput").ap()
    wq = nc.dram_tensor("wq", [LAY, D, D], F32, kind="ExternalInput").ap()
    wk = nc.dram_tensor("wk", [LAY, D, D], F32, kind="ExternalInput").ap()
    wv = nc.dram_tensor("wv", [LAY, D, D], F32, kind="ExternalInput").ap()
    wo = nc.dram_tensor("wo", [LAY, D, D], F32, kind="ExternalInput").ap()
    wff1 = nc.dram_tensor("wff1", [LAY, D, DFF], F32, kind="ExternalInput").ap()
    wff2 = nc.dram_tensor("wff2", [LAY, DFF, D], F32, kind="ExternalInput").ap()
    svin = nc.dram_tensor("svin", [128, SV_N], F32, kind="ExternalInput").ap()
    obdin = nc.dram_tensor("obdin", [128, 64 * LAY], F32, kind="ExternalInput").ap()
    cfin = nc.dram_tensor("cfin", [128, 272], F32, kind="ExternalInput").ap()
    cbin = nc.dram_tensor("cbin", [128, 131], BF16, kind="ExternalInput").ap()
    cache = nc.dram_tensor("wcache", [128 * CACHE_PP], BF16, kind="Internal")

    XT = nc.alloc_sbuf_tensor("XT", [128, FC * 512], F32)
    RS = nc.alloc_sbuf_tensor("RS", [128, FC * 512], F32)
    HS = nc.alloc_sbuf_tensor("HS", [128, FC * 512], F32)
    RB = nc.alloc_sbuf_tensor("RB", [128, FC * 512], BF16)
    HB = nc.alloc_sbuf_tensor("HB", [128, FC * 512], BF16)
    BIGA = nc.alloc_sbuf_tensor("BIGA", [128, MCF * 512], BF16)
    BIGB = nc.alloc_sbuf_tensor("BIGB", [128, MCF * 512], BF16)
    W = [nc.alloc_sbuf_tensor(f"W{i}", [128, CH], BF16) for i in range(NSLOT)]
    KT = nc.alloc_sbuf_tensor("KT", [128, FC * 128], BF16)
    VT = nc.alloc_sbuf_tensor("VT", [128, FC * 130], BF16)
    S_sb = nc.alloc_sbuf_tensor("S_sb", [128, LAY * FC * 65], F32)
    S_bf = nc.alloc_sbuf_tensor("S_bf", [128, FC * 65], BF16)
    ATS = nc.alloc_sbuf_tensor("ATS", [128, 16 * 128], BF16)
    SQA = nc.alloc_sbuf_tensor("SQA", [128, 4 * 512], BF16)
    T32 = nc.alloc_sbuf_tensor("T32", [128, 2 * 512], F32)
    RSB = nc.alloc_sbuf_tensor("RSB", [128, 512], F32)
    ROWS = nc.alloc_sbuf_tensor("ROWS", [1, 1536], F32)
    WROW = nc.alloc_sbuf_tensor("WROW", [2, FC * 512], BF16)
    WK = nc.alloc_sbuf_tensor("WK", [128, FC * 2], F32)
    ATM = nc.alloc_sbuf_tensor("ATM", [128, FC * 128], BF16)
    RD2 = nc.alloc_sbuf_tensor("RD2", [128, FC * 4], F32)
    SV = nc.alloc_sbuf_tensor("SV", [128, SV_N], F32)
    DV = nc.alloc_sbuf_tensor("DV", [128, DV_N], F32)
    OBD = nc.alloc_sbuf_tensor("OBD", [128, 64 * LAY], F32)
    OM2 = nc.alloc_sbuf_tensor("OM2", [128, 128 * LAY], BF16)
    CF = nc.alloc_sbuf_tensor("CF", [128, 272], F32)
    CB = nc.alloc_sbuf_tensor("CB", [128, 131], BF16)
    CAST = [nc.alloc_sbuf_tensor(f"CAST{i}", [128, 1024], BF16) for i in range(2)]
    FP = nc.alloc_sbuf_tensor("FP", [128, 1], F32)
    FA = nc.alloc_sbuf_tensor("FA", [1, 2], F32)
    FD = nc.alloc_sbuf_tensor("FD", [1, 2], F32)

    pbt = [nc.alloc_psum_tensor(f"pb{i}", [128, 512], F32) for i in range(6)]
    pb = pbt
    pt = [nc.alloc_psum_tensor(f"pt{i}", [128, 1024], BF16) for i in range(2)]

    P = Prog()

    def sv_ap(nm, l, mi, n=1):
        return SV[:, SVC[(nm, l)] + mi: SVC[(nm, l)] + mi + n]

    def dv_ap(nm, l, mi):
        return DV[:, DVC[(nm, l)] + mi: DVC[(nm, l)] + mi + 1]

    for j, v in enumerate(CONSTVALS):
        nc.const_aps.register(F32, v, CF[:, 256 + j:257 + j])

    ONES_F = CF[0:1, 0:128]
    TRIU = CF[:, 0:128]
    E2 = CF[0:2, 128:256]
    IDENT = CB[:, 0:128]
    ONESCOL = CB[:, 128:129]
    BDONES = CB[:, 129:131]

    P.emit("pool", "dma_start", SV[:, :], svin, dma="cl", commit=False)
    P.emit("pool", "dma_start", OBD[:, :], obdin, dma="cl", commit=False)
    P.emit("pool", "dma_start", CF[:, :], cfin, dma="cl", commit=False)
    P.emit("pool", "dma_start", CB[:, :], cbin, dma="cl", commit=False)
    P.commit_dma("cl", writes=[("SV",), ("OBD",), ("CF",), ("CB",)])

    for l in range(LAY):
        P.emit("dve", "tensor_scalar_mul", DV[:, DVC[("bqs", l)]:DVC[("bqs", l)] + 8],
               sv_ap("bq", l, 0, 8), DHS, reads=[("SV",)], writes=[("DV",)],
               force_inc=True)
        P.emit("dve", "tensor_scalar_mul", DV[:, DVC[("bks", l)]:DVC[("bks", l)] + 8],
               sv_ap("bk", l, 0, 8), DHS, reads=[("SV",)], writes=[("DV",)],
               force_inc=True)
    P.emit("dve", "tensor_scalar_mul", DV[:, DVC[("s3h", None)]:DVC[("s3h", None)] + 8],
           sv_ap("s3", None, 0, 8), 0.5, reads=[("SV",)], writes=[("DV",)],
           force_inc=True)

    for l in range(LAY):
        ob = OBD[:, l * 64: l * 64 + 64]
        o2 = 128 * l
        P.emit("act", "activation", OM2[:, o2:o2 + 32], ob[:, 0:32], AF.Identity,
               reads=[("OBD",)], writes=[("OM2",)], force_inc=True)
        P.emit("act", "activation", OM2[:, o2 + 32:o2 + 64], ob[:, 0:32], AF.Identity,
               scale=-1.0, reads=[("OBD",)], writes=[("OM2",)], force_inc=True)
        P.emit("act", "activation", OM2[:, o2 + 64:o2 + 96], ob[:, 32:64], AF.Identity,
               reads=[("OBD",)], writes=[("OM2",)], force_inc=True)
        P.emit("act", "activation", OM2[:, o2 + 96:o2 + 128], ob[:, 32:64], AF.Identity,
               scale=-1.0, reads=[("OBD",)], writes=[("OM2",)], force_inc=True)

    P.emit("dve", "memset", S_sb[:, :], 0.0, writes=[("S", 0), ("S", 1), ("S", 2)], force_inc=True)
    for fc in range(FC):
        P.emit("dve", "memset", VT[:, fc * 130 + 64: fc * 130 + 65], 1.0,
               writes=[("VTones", fc)], force_inc=True)
        P.emit("dve", "memset", VT[:, fc * 130 + 129: fc * 130 + 130], 1.0,
               writes=[("VTones2", fc)], force_inc=True)
    P.emit("act", "memzero", FA[:, :], force_inc=True)
    P.emit("dve", "memset", FD[:, :], 0.0, force_inc=True)
    P.emit("pool", "memset", FP[:, :], 0.0, force_inc=True)

    def prepass():
        cnt = [0]

        def one(w, l, K, M, src3):
            pitch = {"w1": 2048, "w2": 2048}.get(w, K)
            for ki in range(K // 128):
                for mb in range(0, M, 1024):
                    slot = cnt[0] % 2
                    cnt[0] += 1
                    if src3 is None:
                        blk_src = {"w1": w1t, "w2": w2t, "h1": h1t,
                                   "h2": h2t, "w3": w3t}[w][
                                       ki * 128:(ki + 1) * 128, mb:mb + 1024]
                    else:
                        blk_src = src3[l, ki * 128:(ki + 1) * 128, mb:mb + 1024]
                    P.emit("pool", "dma_start", CAST[slot][:, 0:1024], blk_src,
                           dma=("pl", slot), writes=[("CAST", slot)])
                    nmi = min(8, M // 128 - mb // 128)
                    out = bass.AP(cache,
                                  stack_off(w, l, mb // 128) + ki * 128,
                                  [[CACHE_PP, 128], [pitch, nmi], [1, 128]])
                    src = CAST[slot].ap()[:, 0:1024].rearrange(
                        "p (m e) -> p m e", m=8)[:, 0:nmi, :]
                    P.emit("sp", "dma_start", out, src, dma=("ps", slot),
                           reads=[("CAST", slot)])

        one("w1", None, D, DFF, None)
        one("w2", None, D, DFF, None)
        one("h1", None, DFF, DFF, None)
        one("h2", None, DFF, DFF, None)
        one("w3", None, DFF, D, None)
        for l in range(LAY):
            one("wq", l, D, D, wq)
            one("wk", l, D, D, wk)
            one("wv", l, D, D, wv)
            one("wo", l, D, D, wo)
            one("wff1", l, D, DFF, wff1)
            one("wff2", l, DFF, D, wff2)

    prepass()

    P.begin_loop(NT)

    seq = (list(range(0, 176)) + list(range(176, 320)) + list(range(0, 176)))
    st = {"spos": 0, "issued": -1, "slot_chunk": [None] * NSLOT}

    def ensure(tpos):
        tgt = min(tpos + NSLOT - 2, len(seq) - 1)
        while st["issued"] < tgt:
            c = st["issued"] + 1
            slot = c % NSLOT
            cid = seq[c]
            src = bass.AP(cache, cid * CH, [[CACHE_PP, 128], [1, CH]])
            P.emit("sp", "dma_start", W[slot][:, :], src, dma=("rg", slot),
                   writes=[("W", slot)])
            st["issued"] = c
            st["slot_chunk"][slot] = c

    def next_stack(w, l, mi, K):
        assert st["spos"] % 1024 == 0
        exp = stack_off(w, l, mi)
        pieces = []
        for ki in range(K // 128):
            pos = st["spos"] + ki * 128
            tpos = pos // CH
            ensure(tpos)
            slot = tpos % NSLOT
            off = pos % CH
            assert seq[tpos] * CH + off == exp + ki * 128, (w, l, mi, ki)
            assert st["slot_chunk"][slot] == tpos,                 ("slot retargeted before consumption", w, l, mi, ki)
            pieces.append((W[slot], ("W", slot), off))
        st["spos"] += K
        return pieces

    rot = [0]

    def bank():
        b = rot[0] % 4
        rot[0] += 1
        return b

    def x_ap(fc, out=False):
        tns = yt if out else xt
        def mk(ctx, fc=fc, tns=tns):
            return bass.AP(tns, ctx.i * 512 + fc * 128 * S, [[S, 128], [1, 512]])
        return mk

    XSL = [XT[:, fc * 512:(fc + 1) * 512] for fc in range(FC)]
    RSL = [RS[:, fc * 512:(fc + 1) * 512] for fc in range(FC)]
    HSL = [HS[:, fc * 512:(fc + 1) * 512] for fc in range(FC)]
    RBL = [RB[:, fc * 512:(fc + 1) * 512] for fc in range(FC)]
    HBL = [HB[:, fc * 512:(fc + 1) * 512] for fc in range(FC)]
    AL = [BIGA[:, m * 512:(m + 1) * 512] for m in range(MCF)]
    BL = [BIGB[:, m * 512:(m + 1) * 512] for m in range(MCF)]

    def matmul_stack(w, l, mi, K, rhs_slices, rhs_keys, bi, extra_reads=(),
                     pieces=None):
        if pieces is None:
            pieces = next_stack(w, l, mi, K)
        n = len(pieces)
        for ki, (wt, wkey, off) in enumerate(pieces):
            P.emit("pe", "matmul", pb[bi][:, :], wt[:, off:off + 128],
                   rhs_slices[ki], start=(ki == 0), stop=(ki == n - 1),
                   reads=[wkey, rhs_keys[ki]] + list(extra_reads),
                   writes=[("pb", bi)])

    def bundle(stacks, banks, rhs_keys=()):
        rd = set(rhs_keys)
        for pieces in stacks:
            for _, wkey, _ in pieces:
                rd.add(wkey)
        P.emit("pe", "nop", reads=sorted(rd, key=repr),
               writes=[("pb", b) for b in banks])

    def ln_stats_fc(fc, src_slice, src_key):
        r = fc % 2
        P.emit("act", "copy", SQA[:, r * 512:(r + 1) * 512], src_slice,
               reads=[src_key], writes=[("SQA", r)])
        P.emit("dve", "tensor_mul", SQA[:, (2 + r) * 512:(3 + r) * 512],
               SQA[:, r * 512:(r + 1) * 512], SQA[:, r * 512:(r + 1) * 512],
               reads=[("SQA", r)], writes=[("SQB", r)])
        P.emit("pe", "matmul", pb[4][0:1, 0:512], ONESCOL,
               SQA[:, r * 512:(r + 1) * 512], start=(fc == 0), stop=(fc == 7),
               reads=[("SQA", r), ("CB",)], writes=[("pb", 4)])
        P.emit("pe", "matmul", pb[5][0:1, 0:512], ONESCOL,
               SQA[:, (2 + r) * 512:(3 + r) * 512], start=(fc == 0),
               stop=(fc == 7), reads=[("SQB", r), ("CB",)], writes=[("pb", 5)])

    def emit_ln(src_slices, src_keys, gamma=None, beta=None,
                out_f32=None, out_f32_keys=None, out_bf=None, out_bf_keys=None,
                stats_done=False):
        if not stats_done:
            for fc in range(FC):
                ln_stats_fc(fc, src_slices[fc], src_keys[fc])
        mu = ROWS[0:1, 0:512]
        ms = ROWS[0:1, 512:1024]
        tmp = ROWS[0:1, 1024:1536]
        P.emit("act", "activation", mu, pb[4][0:1, 0:512], AF.Identity, scale=1.0 / D,
               reads=[("pb", 4)], writes=[("mu",)])
        P.emit("act", "activation", ms, pb[5][0:1, 0:512], AF.Identity, scale=1.0 / D,
               reads=[("pb", 5)], writes=[("ms",)])
        P.emit("dve", "tensor_mul", tmp, mu, mu, reads=[("mu",)], writes=[("tmp",)])
        P.emit("dve", "tensor_sub", ms, ms, tmp, reads=[("ms",), ("tmp",)],
               writes=[("ms",)])
        P.emit("act", "activation", tmp, ms, AF.Ln, bias=LN_EPS,
               reads=[("ms",)], writes=[("tmp",)])
        P.emit("act", "activation", tmp, tmp, AF.Exp, scale=-0.5,
               reads=[("tmp",)], writes=[("tmp",)])
        P.emit("pe", "matmul", pb[4][:, :], ONES_F, mu, start=True, stop=True,
               reads=[("mu",), ("CF",)], writes=[("pb", 4)])
        P.emit("pe", "matmul", pb[5][:, :], ONES_F, tmp, start=True, stop=True,
               reads=[("tmp",), ("CF",)], writes=[("pb", 5)])
        P.emit("act", "copy", RSB[:, :], pb[5][:, :], reads=[("pb", 5)],
               writes=[("RSB",)])
        for fc in range(FC):
            r = fc % 2
            t32 = T32[:, r * 512:(r + 1) * 512]
            P.emit("dve", "tensor_sub", t32, src_slices[fc], pb[4][:, :],
                   reads=[src_keys[fc], ("pb", 4)], writes=[("T32", r)])
            if gamma is None:
                if out_f32 is not None:
                    P.emit("dve", "tensor_mul", out_f32[fc], t32, RSB[:, :],
                           reads=[("T32", r), ("RSB",)], writes=[out_f32_keys[fc]])
                    P.emit("act", "copy", out_bf[fc], out_f32[fc],
                           reads=[out_f32_keys[fc]], writes=[out_bf_keys[fc]])
                else:
                    P.emit("dve", "tensor_mul", out_bf[fc], t32, RSB[:, :],
                           reads=[("T32", r), ("RSB",)], writes=[out_bf_keys[fc]])
            else:
                P.emit("dve", "tensor_mul", t32, t32, RSB[:, :],
                       reads=[("T32", r), ("RSB",)], writes=[("T32", r)])
                P.emit("act", "activation", out_f32[fc], t32, AF.Identity,
                       scale=sv_ap(gamma[0], gamma[1], fc),
                       bias=sv_ap(beta[0], beta[1], fc),
                       reads=[("T32", r), ("SV",)], writes=[out_f32_keys[fc]])
                P.emit("act", "copy", out_bf[fc], out_f32[fc],
                       reads=[out_f32_keys[fc]], writes=[out_bf_keys[fc]])

    def emit_ffn(ln_hook=None):
        for mi in range(MCF):
            p1 = next_stack("w1", None, mi, D)
            p2 = next_stack("w2", None, mi, D)
            b1 = bank()
            b2 = bank()
            bundle([p1, p2], [b1, b2], [("RB", k) for k in range(FC)])
            matmul_stack("w1", None, mi, D, RBL, [("RB", k) for k in range(FC)], b1,
                         pieces=p1)
            matmul_stack("w2", None, mi, D, RBL, [("RB", k) for k in range(FC)], b2,
                         pieces=p2)
            P.emit("act", "activation", AL[mi], pb[b1][:, :], AF.Silu,
                   scale=sv_ap("s1", None, mi), reads=[("pb", b1), ("SV",)],
                   writes=[("A", mi)])
            P.emit("act", "activation", BL[mi], pb[b2][:, :], AF.Identity,
                   scale=sv_ap("s2", None, mi), reads=[("pb", b2), ("SV",)],
                   writes=[("B", mi)])
            P.emit("dve", "tensor_mul", AL[mi], AL[mi], BL[mi],
                   reads=[("A", mi), ("B", mi)], writes=[("A", mi)])
        for mi in range(MCF):
            bi = bank()
            matmul_stack("h1", None, mi, DFF, AL, [("A", k) for k in range(MCF)], bi)
            P.emit("act", "activation", BL[mi], pb[bi][:, :], AF.Gelu,
                   scale=sv_ap("sh1", None, mi), reads=[("pb", bi), ("SV",)],
                   writes=[("B", mi)])
        for mi in range(MCF):
            bi = bank()
            matmul_stack("h2", None, mi, DFF, BL, [("B", k) for k in range(MCF)], bi)
            P.emit("act", "activation", AL[mi], pb[bi][:, :], AF.Gelu,
                   scale=sv_ap("sh2", None, mi), reads=[("pb", bi), ("SV",)],
                   writes=[("A", mi)])
        for mi in range(FC):
            bi = bank()
            matmul_stack("w3", None, mi, DFF, AL, [("A", k) for k in range(MCF)], bi)
            r = mi % 2
            t32 = T32[:, r * 512:(r + 1) * 512]
            P.emit("act", "activation", t32, pb[bi][:, :], AF.Identity,
                   scale=dv_ap("s3h", None, mi), reads=[("pb", bi), ("DV",)],
                   writes=[("T32", r)])
            P.emit("dve", "tensor_add", XSL[mi], XSL[mi], t32,
                   reads=[("XT", mi), ("T32", r)], writes=[("XT", mi)])
            if ln_hook is not None:
                ln_hook(mi)

    def emit_favor(l, base, side):
        for fc in range(FC):
            p = fc % 2
            bs, bu = 3 * p, 3 * p + 1
            buf = BL[base + fc]
            bkey = ("B", base + fc)
            if side == "k":
                r = fc % 2
                sq = SQA[:, r * 512:(r + 1) * 512]
                P.emit("act", "activation", sq, buf, AF.Square,
                       reads=[bkey], writes=[("SQA", r)])
                P.emit("pe", "matmul", pb[bs][0:2, 0:512], BDONES, sq,
                       start=True, stop=True, reads=[("SQA", r), ("CB",)],
                       writes=[("pb", bs)])
                P.emit("act", "activation", WROW[0:2, fc * 512:(fc + 1) * 512],
                       pb[bs][0:2, 0:512], AF.Exp, scale=-0.5, bias=LNC,
                       reads=[("pb", bs)], writes=[("WR", fc)])
            P.emit("pe", "matmul", pb[bu][:, :], OM2[:, l * 128:(l + 1) * 128],
                   buf, start=True, stop=True, reads=[bkey, ("OM2",)],
                   writes=[("pb", bu)])
            P.emit("act", "activation", buf, pb[bu][:, :], AF.Exp,
                   reads=[("pb", bu)], writes=[bkey])

    def emit_la_chunk(l, tc):
        ts = tc * 128
        lb = l * FC * 65
        P.emit("act", "copy", S_bf[:, :], S_sb[:, lb:lb + FC * 65],
               reads=[("S", l)], writes=[("Sbf",)])
        for fc in range(FC):
            p = fc % 2
            kbuf = BL[8 + fc]
            vbuf = BL[16 + fc]
            P.emit("pe", "transpose", pt[p][:, 384:386],
                   WROW[0:2, fc * 512 + ts: fc * 512 + ts + 128], IDENT[0:2, 0:2],
                   reads=[("WR", fc), ("CB",)], writes=[("pt", p)])
            P.emit("pe", "transpose", pt[p][:, 0:128], kbuf[:, ts:ts + 128],
                   IDENT, reads=[("B", 8 + fc), ("CB",)], writes=[("pt", p)])
            P.emit("pe", "transpose", pt[p][:, 128:256], vbuf[:, ts:ts + 128],
                   IDENT, reads=[("B", 16 + fc), ("CB",)], writes=[("pt", p)])
            P.emit("dve", "tensor_copy", WK[:, fc * 2:(fc + 1) * 2],
                   pt[p][:, 384:386], reads=[("pt", p)], writes=[("WK", fc)])
            P.emit("dve", "tensor_scalar_mul", KT[:, fc * 128:fc * 128 + 64],
                   pt[p][:, 0:64], WK[:, fc * 2:fc * 2 + 1],
                   reads=[("pt", p), ("WK", fc)], writes=[("KT", fc)])
            P.emit("dve", "tensor_scalar_mul", KT[:, fc * 128 + 64:fc * 128 + 128],
                   pt[p][:, 64:128], WK[:, fc * 2 + 1:fc * 2 + 2],
                   reads=[("pt", p), ("WK", fc)], writes=[("KT", fc)])
            vt_dst = VT.ap()[:, fc * 130: fc * 130 + 130].rearrange(
                "p (s c) -> p s c", s=2, c=65)[:, :, 0:64]
            P.emit("dve", "tensor_copy", vt_dst,
                   pt[p][:, 128:256].rearrange("p (s c) -> p s c", s=2, c=64),
                   reads=[("pt", p)], writes=[("VT", fc)])
        for fc in range(FC):
            p = fc % 2
            bAe, bAo = 2 * p, 2 * p + 1
            qb = BL[0 + fc]
            qk = ("B", 0 + fc)
            kb = BL[8 + fc]
            kk = ("B", 8 + fc)
            P.emit("pe", "matmul", pb[bAe][:, 0:128], kb[0:64, ts:ts + 128],
                   qb[0:64, ts:ts + 128], start=True, stop=True,
                   reads=[kk, qk], writes=[("pb", bAe)])
            P.emit("pe", "matmul", pb[bAo][:, 0:128], kb[64:128, ts:ts + 128],
                   qb[64:128, ts:ts + 128], start=True, stop=True,
                   reads=[kk, qk], writes=[("pb", bAo)])
            P.emit("dve", "scalar_tensor_tensor", ATS[:, (2 * fc) * 128:(2 * fc + 1) * 128],
                   pb[bAe][:, 0:128], WK[:, fc * 2:fc * 2 + 1], TRIU,
                   ALU.mult, ALU.mult,
                   reads=[("pb", bAe), ("WK", fc), ("CF",)], writes=[("ATS", 2 * fc)])
            P.emit("dve", "scalar_tensor_tensor", ATS[:, (2 * fc + 1) * 128:(2 * fc + 2) * 128],
                   pb[bAo][:, 0:128], WK[:, fc * 2 + 1:fc * 2 + 2], TRIU,
                   ALU.mult, ALU.mult,
                   reads=[("pb", bAo), ("WK", fc), ("CF",)], writes=[("ATS", 2 * fc + 1)])
        for fc in range(FC):
            p = fc % 2
            bO = 4 + p
            qb = BL[0 + fc]
            qk = ("B", 0 + fc)
            P.emit("pe", "matmul", pb[bO][:, 0:65],
                   qb[0:64, ts:ts + 128], S_bf[0:64, fc * 65:(fc + 1) * 65],
                   start=True, stop=False, reads=[qk, ("Sbf",)],
                   writes=[("pb", bO)])
            P.emit("pe", "matmul", pb[bO][:, 0:65],
                   ATS[:, (2 * fc) * 128:(2 * fc + 1) * 128],
                   VT[:, fc * 130: fc * 130 + 65],
                   start=False, stop=True,
                   reads=[("ATS", 2 * fc), ("VT", fc), ("VTones", fc)],
                   writes=[("pb", bO)])
            P.emit("pe", "matmul", pb[bO][:, 65:130],
                   qb[64:128, ts:ts + 128], S_bf[64:128, fc * 65:(fc + 1) * 65],
                   start=True, stop=False, reads=[qk, ("Sbf",)],
                   writes=[("pb", bO)])
            P.emit("pe", "matmul", pb[bO][:, 65:130],
                   ATS[:, (2 * fc + 1) * 128:(2 * fc + 2) * 128],
                   VT[:, fc * 130 + 65: fc * 130 + 130],
                   start=False, stop=True,
                   reads=[("ATS", 2 * fc + 1), ("VT", fc), ("VTones2", fc)],
                   writes=[("pb", bO)])
            dens = bass.AP(pbt[bO], 64, [[512, 128], [65, 2]])
            P.emit("act", "activation", RD2[:, fc * 4: fc * 4 + 2], dens,
                   AF.Identity, bias=ATTN_EPS, reads=[("pb", bO)],
                   writes=[("RDr", fc)])
            P.emit("dve", "reciprocal", RD2[:, fc * 4 + 2: fc * 4 + 4],
                   RD2[:, fc * 4: fc * 4 + 2], reads=[("RDr", fc)],
                   writes=[("RD", fc)])
            P.emit("act", "activation", ATM[:, fc * 128: fc * 128 + 64],
                   pb[bO][:, 0:64], AF.Identity,
                   scale=RD2[:, fc * 4 + 2: fc * 4 + 3],
                   reads=[("pb", bO), ("RD", fc)], writes=[("ATM", fc)])
            P.emit("act", "activation", ATM[:, fc * 128 + 64: fc * 128 + 128],
                   pb[bO][:, 65:129], AF.Identity,
                   scale=RD2[:, fc * 4 + 3: fc * 4 + 4],
                   reads=[("pb", bO), ("RD", fc)], writes=[("ATM", fc)])
        for fc in range(FC):
            p = fc % 2
            bAo = 2 * p + 1
            P.emit("pe", "transpose", pt[p][:, 256:384],
                   ATM[:, fc * 128:(fc + 1) * 128], IDENT,
                   reads=[("ATM", fc), ("CB",)], writes=[("pt", p)])
            P.emit("dve", "tensor_copy", BL[24 + fc][:, ts:ts + 128],
                   pt[p][:, 256:384], reads=[("pt", p)], writes=[("B", 24 + fc)])
            P.emit("pe", "matmul", pb[bAo][0:64, 128:193],
                   KT[:, fc * 128: fc * 128 + 64],
                   VT[:, fc * 130: fc * 130 + 65], start=True, stop=True,
                   reads=[("KT", fc), ("VT", fc), ("VTones", fc)],
                   writes=[("pb", bAo)])
            P.emit("pe", "matmul", pb[bAo][64:128, 128:193],
                   KT[:, fc * 128 + 64: fc * 128 + 128],
                   VT[:, fc * 130 + 65: fc * 130 + 130], start=True, stop=True,
                   tile_position=(0, 64),
                   reads=[("KT", fc), ("VT", fc), ("VTones2", fc)],
                   writes=[("pb", bAo)])
            P.emit("dve", "tensor_add", S_sb[:, lb + fc * 65:lb + (fc + 1) * 65],
                   S_sb[:, lb + fc * 65:lb + (fc + 1) * 65], pb[bAo][:, 128:193],
                   reads=[("S", l), ("pb", bAo)], writes=[("S", l)])

    def emit_attn(l):
        for (wname, base, scale, bias_nm, bias_dv) in (
                ("wq", 0, DHS, "bqs", True), ("wk", 8, DHS, "bks", True),
                ("wv", 16, 1.0, "bv", False)):
            for mi0 in range(0, FC, 2):
                ps = [next_stack(wname, l, mi0, D), next_stack(wname, l, mi0 + 1, D)]
                bs2 = [bank(), bank()]
                bundle(ps, bs2, [("RB", k) for k in range(FC)])
                for j, mi in enumerate((mi0, mi0 + 1)):
                    matmul_stack(wname, l, mi, D, RBL,
                                 [("RB", k) for k in range(FC)], bs2[j],
                                 pieces=ps[j])
                    b_ap = (dv_ap(bias_nm, l, mi) if bias_dv
                            else sv_ap(bias_nm, l, mi))
                    P.emit("act", "activation", BL[base + mi], pb[bs2[j]][:, :],
                           AF.Identity, scale=scale, bias=b_ap,
                           reads=[("pb", bs2[j]), ("DV",) if bias_dv else ("SV",)],
                           writes=[("B", base + mi)])
        emit_favor(l, 0, "q")
        emit_favor(l, 8, "k")
        for tc in range(TC):
            emit_la_chunk(l, tc)
        for mi in range(FC):
            bi = bank()
            matmul_stack("wo", l, mi, D, [BL[24 + k] for k in range(FC)],
                         [("B", 24 + k) for k in range(FC)], bi)
            r = mi % 2
            t32 = T32[:, r * 512:(r + 1) * 512]
            P.emit("act", "activation", t32, pb[bi][:, :], AF.Identity,
                   bias=sv_ap("bo", l, mi), reads=[("pb", bi), ("SV",)],
                   writes=[("T32", r)])
            P.emit("dve", "tensor_add", HSL[mi], RSL[mi], t32,
                   reads=[("RS", mi), ("T32", r)], writes=[("HS", mi)])
            ln_stats_fc(mi, HSL[mi], ("HS", mi))
        if DEV_STAGE == f"wo{l}":
            return
        emit_ln(HSL, [("HS", k) for k in range(FC)], gamma=("g1", l),
                beta=("b1", l), out_f32=HSL,
                out_f32_keys=[("HS", k) for k in range(FC)],
                out_bf=HBL, out_bf_keys=[("HB", k) for k in range(FC)],
                stats_done=True)
        if DEV_STAGE == f"ln1_{l}":
            return
        for mi0 in range(0, MCF, 2):
            ps = [next_stack("wff1", l, mi0, D), next_stack("wff1", l, mi0 + 1, D)]
            bs2 = [bank(), bank()]
            bundle(ps, bs2, [("HB", k) for k in range(FC)])
            for j, mi in enumerate((mi0, mi0 + 1)):
                matmul_stack("wff1", l, mi, D, HBL,
                             [("HB", k) for k in range(FC)], bs2[j], pieces=ps[j])
                P.emit("act", "activation", AL[mi], pb[bs2[j]][:, :], AF.Relu,
                       bias=sv_ap("bf1", l, mi), reads=[("pb", bs2[j]), ("SV",)],
                       writes=[("A", mi)])
        for mi in range(FC):
            bi = bank()
            matmul_stack("wff2", l, mi, DFF, AL, [("A", k) for k in range(MCF)], bi)
            r = mi % 2
            t32 = T32[:, r * 512:(r + 1) * 512]
            P.emit("act", "activation", t32, pb[bi][:, :], AF.Identity,
                   bias=sv_ap("bf2", l, mi), reads=[("pb", bi), ("SV",)],
                   writes=[("T32", r)])
            P.emit("dve", "tensor_add", HSL[mi], HSL[mi], t32,
                   reads=[("HS", mi), ("T32", r)], writes=[("HS", mi)])
            ln_stats_fc(mi, HSL[mi], ("HS", mi))
        if DEV_STAGE == f"y{l}":
            return
        emit_ln(HSL, [("HS", k) for k in range(FC)], gamma=("g2", l),
                beta=("b2", l), out_f32=RSL,
                out_f32_keys=[("RS", k) for k in range(FC)],
                out_bf=RBL, out_bf_keys=[("RB", k) for k in range(FC)],
                stats_done=True)

    for fc in range(FC):
        P.emit("pool", "dma_start", XSL[fc], x_ap(fc), dma="xl", commit=False)
    P.commit_dma("xl", writes=[("XT", k) for k in range(FC)])

    def store(slices, keys):
        for fc in range(FC):
            P.emit("pool", "dma_start", x_ap(fc, out=True), slices[fc],
                   dma="yo", reads=[keys[fc]], writes=[("yo", fc)])

    done = False
    while True:
        emit_ln(XSL, [("XT", k) for k in range(FC)], out_bf=RBL,
                out_bf_keys=[("RB", k) for k in range(FC)])
        emit_ffn(ln_hook=(lambda mi: ln_stats_fc(mi, XSL[mi], ("XT", mi)))
                 if DEV_STAGE not in ("ffn1",) else None)
        if DEV_STAGE == "ffn1":
            store(XSL, [("XT", k) for k in range(FC)]); break
        emit_ln(XSL, [("XT", k) for k in range(FC)], out_f32=RSL,
                out_f32_keys=[("RS", k) for k in range(FC)], out_bf=RBL,
                out_bf_keys=[("RB", k) for k in range(FC)], stats_done=True)
        if DEV_STAGE == "preln":
            store(RSL, [("RS", k) for k in range(FC)]); break
        brk = False
        for l in range(LAY):
            emit_attn(l)
            if DEV_STAGE in (f"l{l}", f"wo{l}", f"ln1_{l}", f"y{l}"):
                brk = True
                break
        if brk:
            if DEV_STAGE.startswith("l") and not DEV_STAGE.startswith("ln1") and not DEV_STAGE.startswith("y"):
                store(RSL, [("RS", k) for k in range(FC)])
            else:
                store(HSL, [("HS", k) for k in range(FC)])
            break
        for fc in range(FC):
            P.emit("dve", "tensor_add", XSL[fc], XSL[fc], RSL[fc],
                   reads=[("XT", fc), ("RS", fc)], writes=[("XT", fc)])
            ln_stats_fc(fc, XSL[fc], ("XT", fc))
        if DEV_STAGE == "xadd":
            store(XSL, [("XT", k) for k in range(FC)]); break
        emit_ln(XSL, [("XT", k) for k in range(FC)], out_bf=RBL,
                out_bf_keys=[("RB", k) for k in range(FC)], stats_done=True)
        emit_ffn()
        store(XSL, [("XT", k) for k in range(FC)])
        done = True
        break

    if done:
        assert st["spos"] == len(seq) * CH, st["spos"]
    P.end_loop()
    P.finalize(nc, fences={"pool": FP[:, :], "act": FA[0:1, 0:1],
                           "dve": FD[0:1, 0:1]})
    n_ops = sum(len(P.ops[e]) for e in ENGINES)
    return nc, n_ops, P.n_waits


def host_prepare(inputs):
    f = np.ascontiguousarray

    def svblock(v):
        v = np.asarray(v)
        return f(v.reshape(-1, 128).T)

    sv = np.zeros((128, SV_N), np.float32)
    for l in range(LAY):
        for nm, src in (("bq", inputs["L0_attn_bq"][l]),
                        ("bk", inputs["L0_attn_bk"][l]),
                        ("bv", inputs["L0_attn_bv"][l]),
                        ("bo", inputs["L0_attn_bo"][l]),
                        ("g1", inputs["L0_attn_ln1_g"][l]),
                        ("b1", inputs["L0_attn_ln1_b"][l]),
                        ("g2", inputs["L0_attn_ln2_g"][l]),
                        ("b2", inputs["L0_attn_ln2_b"][l]),
                        ("bf1", inputs["L0_attn_bf1"][l]),
                        ("bf2", inputs["L0_attn_bf2"][l])):
            blk = svblock(src)
            sv[:, SVC[(nm, l)]:SVC[(nm, l)] + blk.shape[1]] = blk
    for nm, src in (("s1", inputs["L0_ffn_w1_s"]), ("s2", inputs["L0_ffn_w2_s"]),
                    ("sh1", inputs["L0_ffn_h1_s"]), ("sh2", inputs["L0_ffn_h2_s"]),
                    ("s3", inputs["L0_ffn_w3_s"])):
        blk = svblock(src)
        sv[:, SVC[(nm, None)]:SVC[(nm, None)] + blk.shape[1]] = blk

    obd = np.zeros((128, 64 * LAY), np.float32)
    om = np.asarray(inputs["L0_attn_omega"])
    for l in range(LAY):
        obd[0:64, l * 64:l * 64 + 32] = om[l]
        obd[64:128, l * 64 + 32:l * 64 + 64] = om[l]

    cf = np.zeros((128, 272), np.float32)
    cf[:, 0:128] = np.triu(np.ones((128, 128), np.float32))
    cf[0, 128:192] = 1.0
    cf[1, 192:256] = 1.0
    cf[:, 256:256 + len(CONSTVALS)] = np.asarray(CONSTVALS, np.float32)[None, :]
    cb = np.zeros((128, 131), np.float32)
    cb[:, 0:128] = np.eye(128, dtype=np.float32)
    cb[:, 128] = 1.0
    cb[0:64, 129] = 1.0
    cb[64:128, 130] = 1.0
    cb = cb.astype(ml_dtypes.bfloat16)

    shared = dict(
        w1t=f(np.asarray(inputs["L0_ffn_w1_w"]).T),
        w2t=f(np.asarray(inputs["L0_ffn_w2_w"]).T),
        h1t=f(np.asarray(inputs["L0_ffn_h1_w"]).T),
        h2t=f(np.asarray(inputs["L0_ffn_h2_w"]).T),
        w3t=f(np.asarray(inputs["L0_ffn_w3_w"]).T),
        wq=f(np.asarray(inputs["L0_attn_Wq"])),
        wk=f(np.asarray(inputs["L0_attn_Wk"])),
        wv=f(np.asarray(inputs["L0_attn_Wv"])),
        wo=f(np.asarray(inputs["L0_attn_Wo"])),
        wff1=f(np.asarray(inputs["L0_attn_W1"])),
        wff2=f(np.asarray(inputs["L0_attn_W2"])),
        svin=sv, obdin=obd, cfin=cf, cbin=cb,
    )
    x = np.asarray(inputs["L0_attn_x"])
    xts = [f(x[c].T).reshape(-1) for c in range(B)]
    return shared, xts


_CACHE = {}


def kernel(**inputs):
    shared, xts = host_prepare(inputs)
    if "nc" not in _CACHE:
        nc, n_ops, n_waits = build_nc(NTILE)
        _CACHE["nc"] = nc
    nc = _CACHE["nc"]
    in_maps = [dict(shared, xt=xts[c]) for c in range(B)]
    res = run_bass_kernel_spmd(nc, in_maps, core_ids=list(range(B)))
    out = np.empty((B, S, D), np.float32)
    for c in range(B):
        out[c] = res.results[c]["yt"].reshape(D, S).T
    return out


if __name__ == "__main__":
    nc, n_ops, n_waits = build_nc(1)
    print(f"ops={n_ops} waits={n_waits}")


# version 05
# speedup vs baseline: 1.0486x; 1.0036x over previous
import numpy as np
import ml_dtypes
import concourse.bass as bass
import concourse.mybir as mybir
from concourse.bass_utils import run_bass_kernel_spmd

F32 = mybir.dt.float32
BF16 = mybir.dt.bfloat16
AF = mybir.ActivationFunctionType
ALU = mybir.AluOpType

D = 1024
DFF = 4096
S = 8192
B = 8
T = 512
NTILE = S // T
FC = D // 128
MCF = DFF // 128
TC = T // 128
NH = 16
DH = 64
LAY = 3
DHS = float(DH) ** -0.25
LNC = float(np.log(2 * 32) * -0.5)
LN_EPS = 1e-5
ATTN_EPS = 1e-6
NSLOT = 6
CH = 2048
CONSTVALS = (1.0 / D, LN_EPS, -0.5, 0.5, -LNC, LNC, ATTN_EPS, DHS, -1.0, 1.0,
             0.0, 2.0, 0.25)

ENGINES = ("pe", "act", "dve", "pool", "sp")
PARANOID = frozenset(("act", "dve", "pool"))


class Ctx:
    def __init__(self):
        self.i = None


class Prog:
    def __init__(self):
        self.ops = {e: [] for e in ENGINES}
        self.waits = {e: [] for e in ENGINES}
        self.opclock = {e: [] for e in ENGINES}
        self.dmaclock = {}
        self.clock = {e: {} for e in ENGINES}
        self.group_count = {}
        self.group_engine = {}
        self.need_inc = {e: set() for e in ENGINES}
        self.res = {}
        self.n_waits = 0
        self.seg = 0
        self.seg_start = {e: [0] for e in ENGINES}
        self.loop = None
        self.group_final = []
        self._seen_groups = set()

    def _need(self, eng, clk, needed, p):
        if p is None:
            return
        if p[0] == "e":
            if p[1] == eng and eng not in PARANOID:
                return
            sk, v = ("e", p[1]), p[2]
        else:
            sk, v = ("d", p[1]), p[2]
        if clk.get(sk, -1) >= v:
            return
        if needed.get(sk, -1) < v:
            needed[sk] = v

    def emit(self, eng, fn, *args, reads=(), writes=(), dma=None, commit=True,
             force_inc=False, **kwargs):
        pos = len(self.ops[eng])
        clk = self.clock[eng]
        needed = {}
        for k in reads:
            st = self.res.get(k)
            if st is not None:
                self._need(eng, clk, needed, st[0])
        for k in writes:
            st = self.res.get(k)
            if st is not None:
                self._need(eng, clk, needed, st[0])
                for r in st[1]:
                    self._need(eng, clk, needed, r)
        wl = []
        for sk, v in sorted(needed.items(), key=repr):
            if sk[0] == "e":
                self.need_inc[sk[1]].add((sk[1], v))
                snap = self.opclock[sk[1]][v]
            else:
                snap = self.dmaclock[(sk[1], v)]
            for k2, v2 in snap.items():
                if clk.get(k2, -1) < v2:
                    clk[k2] = v2
            if clk.get(sk, -1) < v:
                clk[sk] = v
            wl.append((sk, v))
        self.n_waits += len(wl)
        self.waits[eng].append(wl)
        self.ops[eng].append((fn, args, kwargs, dma, self.seg))
        if force_inc and dma is None:
            self.need_inc[eng].add((eng, pos))
        if dma is not None:
            assert self.group_engine.setdefault(dma, eng) == eng
            cnt = self.group_count.get(dma, 0) + 1
            self.group_count[dma] = cnt
            me = ("d", dma, cnt)
            self.dmaclock[(dma, cnt)] = dict(clk)
            self.opclock[eng].append(dict(clk))
        else:
            me = ("e", eng, pos)
            snap = dict(clk)
            snap[("e", eng)] = pos
            self.opclock[eng].append(snap)
        if commit:
            self._register(me, reads, writes)
        return me

    def commit_dma(self, group, reads=(), writes=()):
        cnt = self.group_count[group]
        self._register(("d", group, cnt), reads, writes)

    def _register(self, me, reads, writes):
        for k in writes:
            self.res[k] = [me, []]
        for k in reads:
            st = self.res.get(k)
            if st is None:
                st = [None, []]
                self.res[k] = st
            st[1].append(me)

    def _cut_segment(self, marker):
        for g in self.group_count:
            assert g not in self._seen_groups, f"group {g} in two segments"
        self._seen_groups |= set(self.group_count)
        self.group_final.append(dict(self.group_count))
        for e in ENGINES:
            self.ops[e].append((marker, None, None, None, self.seg))
            self.waits[e].append([])
            self.opclock[e].append({})
        self.seg += 1
        for e in ENGINES:
            self.seg_start[e].append(len(self.ops[e]))
            self.clock[e] = {}
        self.group_count = {}
        self.dmaclock = {}
        self.res = {k: [None, []] for k in self.res}

    def begin_loop(self, n_iters):
        assert self.loop is None
        self._cut_segment("@LOOP")
        self.loop = n_iters

    def end_loop(self):
        self._cut_segment("@ENDLOOP")

    def finalize(self, nc, fences=None):
        if fences is None:
            fences = {}
        self._cut_segment("@END")
        sems = {}
        for e in ENGINES:
            sems[("e", e)] = nc.alloc_semaphore(f"sem_{e}")
        for g in self.group_engine:
            nm = "_".join(map(str, g)) if isinstance(g, tuple) else str(g)
            sems[("d", g)] = nc.alloc_semaphore(f"dg_{nm}")
        bar = nc.alloc_semaphore("bar")
        bar2 = nc.alloc_semaphore("bar2")
        clear_ids = [sems[("e", e)] for e in ENGINES]
        body_per_iter = (self.group_final[1]
                         if self.loop is not None and len(self.group_final) > 1
                         else {})

        def seg_of(e, pos):
            lo = 0
            for si, st in enumerate(self.seg_start[e]):
                if st <= pos:
                    lo = si
            return lo

        inc_rank = {e: {} for e in ENGINES}
        inc_totals = {e: {} for e in ENGINES}
        for e in ENGINES:
            for p in sorted(p for (_, p) in self.need_inc[e]):
                s = seg_of(e, p)
                inc_totals[e][s] = inc_totals[e].get(s, 0) + 1
                inc_rank[e][p] = inc_totals[e][s]
        markedset = {e: set(p for (_, p) in self.need_inc[e]) for e in ENGINES}
        n_eng = len(ENGINES)
        loop_n = self.loop
        engmap = dict(pe="tensor", act="scalar", dve="vector", pool="gpsimd",
                      sp="sync")
        enumap = dict(pe=mybir.EngineType.PE, act=mybir.EngineType.Activation,
                      dve=mybir.EngineType.DVE, pool=mybir.EngineType.Pool,
                      sp=mybir.EngineType.SP)

        dyn_pairs = {e: [] for e in ENGINES}
        if loop_n is not None:
            for e in ENGINES:
                seen = dict()
                lo = self.seg_start[e][1]
                hi = self.seg_start[e][2] - 1
                for i in range(lo, hi):
                    for sk, v in self.waits[e][i]:
                        if sk[0] == "d" and sk[1] not in seen:
                            seen[sk[1]] = True
                for g, cnt in self.group_final[1].items():
                    if self.group_engine[g] == e and cnt and g not in seen:
                        seen[g] = True
                dyn_pairs[e] = list(seen)
        waitregs = {}
        for e in ENGINES:
            for g in dyn_pairs[e]:
                rh = nc.alloc_registers(f"wr_{e}_{g}", engines=[enumap[e]])
                waitregs[(e, g)] = list(rh)[0]
        cur_target = {}

        def dyn_wait(engine, e, g, target):
            reg = waitregs[(e, g)]
            cur = cur_target.get((e, g), 0)
            delta = target - cur
            assert delta >= 0, (e, g, target, cur)
            if delta:
                engine.reg_alu(reg, reg, delta, mybir.AluOpType.add)
            engine.wait_ge(sems[("d", g)], reg)
            cur_target[(e, g)] = target

        def emit_barrier(engine, e, ctx, seg_idx, in_loop):
            gf = self.group_final[seg_idx]
            for g, cnt in sorted(gf.items(), key=repr):
                if self.group_engine[g] == e and cnt:
                    if in_loop:
                        dyn_wait(engine, e, g, 16 * cnt)
                    else:
                        engine.wait_ge(sems[("d", g)], 16 * cnt)
            if in_loop:
                per_map = self.group_final[1]
                for g in dyn_pairs[e]:
                    per = per_map.get(g, 0)
                    tgt = 16 * per
                    cur = cur_target.get((e, g), 0)
                    if tgt != cur:
                        engine.reg_alu(waitregs[(e, g)], waitregs[(e, g)],
                                       tgt - cur, mybir.AluOpType.add)
                    cur_target[(e, g)] = 0
            for e2 in ENGINES:
                tot = inc_totals[e2].get(seg_idx, 0)
                if tot:
                    engine.wait_ge(sems[("e", e2)], tot)
            if e in ("pe", "sp") or fences.get(e) is None:
                engine.nop().then_inc(bar, 1)
            elif e == "act":
                engine.copy(fences[e], fences[e]).then_inc(bar, 1)
            else:
                engine.memset(fences[e], 0.0).then_inc(bar, 1)
            nbar = seg_idx + 1
            if in_loop:
                barv = n_eng * nbar + n_eng * ctx.i
                bar2v = nbar + ctx.i
            else:
                barv = n_eng * nbar
                bar2v = nbar
            if e == "pool":
                engine.wait_ge(bar, barv)
                for s in clear_ids:
                    engine.sem_clear(s)
                engine.nop().then_inc(bar2, 1)
            else:
                engine.wait_ge(bar2, bar2v)

        with nc.Block() as blk:
            for e in ENGINES:
                def body(engine, e=e):
                    ctx = Ctx()
                    esem = sems[("e", e)]

                    def run_ops(lo, hi):
                        for i in range(lo, hi):
                            fn, args, kwargs, dma, _seg = self.ops[e][i]
                            for sk, v in self.waits[e][i]:
                                if sk[0] == "e":
                                    engine.wait_ge(sems[sk], inc_rank[sk[1]][v])
                                elif ctx.i is not None:
                                    dyn_wait(engine, e, sk[1], 16 * v)
                                else:
                                    engine.wait_ge(sems[sk], 16 * v)
                            rargs = [a(ctx) if callable(a) else a for a in args]
                            inst = getattr(engine, fn)(*rargs, **kwargs)
                            if dma is not None:
                                inst.then_inc(sems[("d", dma)], 16)
                            elif i in markedset[e]:
                                inst.then_inc(esem, 1)

                    marks = [i for i, op in enumerate(self.ops[e])
                             if op[0] in ("@LOOP", "@ENDLOOP", "@END")]
                    assert loop_n is not None
                    mloop, mend, mfin = marks
                    run_ops(0, mloop)
                    emit_barrier(engine, e, ctx, self.ops[e][mloop][4], False)
                    for g in dyn_pairs[e]:
                        engine.reg_mov(waitregs[(e, g)], 0)
                    with engine.Fori(0, loop_n) as iv:
                        ctx.i = iv
                        run_ops(mloop + 1, mend)
                        emit_barrier(engine, e, ctx, self.ops[e][mend][4], True)
                    ctx.i = None
                    run_ops(mend + 1, mfin)

                getattr(blk, engmap[e])(body)
        return sems


FFN_SZ = 2 * 32 * 1024 + 2 * 32 * 4096 + 8 * 4096
ATT_SZ = 4 * 8 * 1024 + 32 * 1024 + 8 * 4096
CACHE_PP = FFN_SZ + LAY * ATT_SZ


def stack_off(w, l, mi):
    if w == "w1":
        return mi * 2048
    if w == "w2":
        return mi * 2048 + 1024
    if w == "h1":
        return 65536 + mi * 4096
    if w == "h2":
        return 196608 + mi * 4096
    if w == "w3":
        return 327680 + mi * 4096
    base = FFN_SZ + l * ATT_SZ
    if w == "wq":
        return base + mi * 1024
    if w == "wk":
        return base + 8192 + mi * 1024
    if w == "wv":
        return base + 16384 + mi * 1024
    if w == "wo":
        return base + 24576 + mi * 1024
    if w == "wff1":
        return base + 32768 + mi * 1024
    if w == "wff2":
        return base + 65536 + mi * 4096
    raise KeyError(w)


def sv_cols():
    cols = {}
    c = 0
    for l in range(LAY):
        for nm, n in (("bq", 8), ("bk", 8), ("bv", 8), ("bo", 8),
                      ("g1", 8), ("b1", 8), ("g2", 8), ("b2", 8),
                      ("bf1", 32), ("bf2", 8)):
            cols[(nm, l)] = c
            c += n
    for nm, n in (("s1", 32), ("s2", 32), ("sh1", 32), ("sh2", 32), ("s3", 8)):
        cols[(nm, None)] = c
        c += n
    return cols, c


SVC, SV_N = sv_cols()
DVC = {}
_c = 0
for l in range(LAY):
    DVC[("bqs", l)] = _c; _c += 8
    DVC[("bks", l)] = _c; _c += 8
DVC[("s3h", None)] = _c; _c += 8
DV_N = _c


import os as _os
DEV_STAGE = _os.environ.get("K_STAGE", "")


def build_nc(NT=NTILE):
    nc = bass.Bass("TRN2", target_bir_lowering=False)

    xt = nc.dram_tensor("xt", [D * S], F32, kind="ExternalInput")
    yt = nc.dram_tensor("yt", [D * S], F32, kind="ExternalOutput")
    w1t = nc.dram_tensor("w1t", [D, DFF], F32, kind="ExternalInput").ap()
    w2t = nc.dram_tensor("w2t", [D, DFF], F32, kind="ExternalInput").ap()
    h1t = nc.dram_tensor("h1t", [DFF, DFF], F32, kind="ExternalInput").ap()
    h2t = nc.dram_tensor("h2t", [DFF, DFF], F32, kind="ExternalInput").ap()
    w3t = nc.dram_tensor("w3t", [DFF, D], F32, kind="ExternalInput").ap()
    wq = nc.dram_tensor("wq", [LAY, D, D], F32, kind="ExternalInput").ap()
    wk = nc.dram_tensor("wk", [LAY, D, D], F32, kind="ExternalInput").ap()
    wv = nc.dram_tensor("wv", [LAY, D, D], F32, kind="ExternalInput").ap()
    wo = nc.dram_tensor("wo", [LAY, D, D], F32, kind="ExternalInput").ap()
    wff1 = nc.dram_tensor("wff1", [LAY, D, DFF], F32, kind="ExternalInput").ap()
    wff2 = nc.dram_tensor("wff2", [LAY, DFF, D], F32, kind="ExternalInput").ap()
    svin = nc.dram_tensor("svin", [128, SV_N], F32, kind="ExternalInput").ap()
    obdin = nc.dram_tensor("obdin", [128, 64 * LAY], F32, kind="ExternalInput").ap()
    cfin = nc.dram_tensor("cfin", [128, 272], F32, kind="ExternalInput").ap()
    cbin = nc.dram_tensor("cbin", [128, 131], BF16, kind="ExternalInput").ap()
    cache = nc.dram_tensor("wcache", [128 * CACHE_PP], BF16, kind="Internal")

    XT = nc.alloc_sbuf_tensor("XT", [128, FC * 512], F32)
    RS = nc.alloc_sbuf_tensor("RS", [128, FC * 512], F32)
    HS = nc.alloc_sbuf_tensor("HS", [128, FC * 512], F32)
    RB = nc.alloc_sbuf_tensor("RB", [128, FC * 512], BF16)
    HB = nc.alloc_sbuf_tensor("HB", [128, FC * 512], BF16)
    BIGA = nc.alloc_sbuf_tensor("BIGA", [128, MCF * 512], BF16)
    BIGB = nc.alloc_sbuf_tensor("BIGB", [128, MCF * 512], BF16)
    W = [nc.alloc_sbuf_tensor(f"W{i}", [128, CH], BF16) for i in range(NSLOT)]
    KT = nc.alloc_sbuf_tensor("KT", [128, FC * 128], BF16)
    VT = nc.alloc_sbuf_tensor("VT", [128, FC * 130], BF16)
    S_sb = nc.alloc_sbuf_tensor("S_sb", [128, LAY * FC * 65], F32)
    S_bf = nc.alloc_sbuf_tensor("S_bf", [128, FC * 65], BF16)
    ATS = nc.alloc_sbuf_tensor("ATS", [128, 16 * 128], BF16)
    SQA = nc.alloc_sbuf_tensor("SQA", [128, 4 * 512], BF16)
    T32 = nc.alloc_sbuf_tensor("T32", [128, 2 * 512], F32)
    RSB = nc.alloc_sbuf_tensor("RSB", [128, 512], F32)
    ROWS = nc.alloc_sbuf_tensor("ROWS", [1, 1536], F32)
    WROW = nc.alloc_sbuf_tensor("WROW", [2, FC * 512], BF16)
    WK = nc.alloc_sbuf_tensor("WK", [128, FC * 2], F32)
    ATM = nc.alloc_sbuf_tensor("ATM", [128, FC * 128], BF16)
    RD2 = nc.alloc_sbuf_tensor("RD2", [128, FC * 4], F32)
    SV = nc.alloc_sbuf_tensor("SV", [128, SV_N], F32)
    DV = nc.alloc_sbuf_tensor("DV", [128, DV_N], F32)
    OBD = nc.alloc_sbuf_tensor("OBD", [128, 64 * LAY], F32)
    OM2 = nc.alloc_sbuf_tensor("OM2", [128, 128 * LAY], BF16)
    CF = nc.alloc_sbuf_tensor("CF", [128, 272], F32)
    CB = nc.alloc_sbuf_tensor("CB", [128, 131], BF16)
    CAST = [nc.alloc_sbuf_tensor(f"CAST{i}", [128, 1024], BF16) for i in range(2)]
    FP = nc.alloc_sbuf_tensor("FP", [128, 1], F32)
    FA = nc.alloc_sbuf_tensor("FA", [1, 2], F32)
    FD = nc.alloc_sbuf_tensor("FD", [1, 2], F32)

    pbt = [nc.alloc_psum_tensor(f"pb{i}", [128, 512], F32) for i in range(6)]
    pb = pbt
    pt = [nc.alloc_psum_tensor(f"pt{i}", [128, 1024], BF16) for i in range(2)]

    P = Prog()

    def sv_ap(nm, l, mi, n=1):
        return SV[:, SVC[(nm, l)] + mi: SVC[(nm, l)] + mi + n]

    def dv_ap(nm, l, mi):
        return DV[:, DVC[(nm, l)] + mi: DVC[(nm, l)] + mi + 1]

    for j, v in enumerate(CONSTVALS):
        nc.const_aps.register(F32, v, CF[:, 256 + j:257 + j])

    ONES_F = CF[0:1, 0:128]
    TRIU = CF[:, 0:128]
    E2 = CF[0:2, 128:256]
    IDENT = CB[:, 0:128]
    ONESCOL = CB[:, 128:129]
    BDONES = CB[:, 129:131]

    P.emit("pool", "dma_start", SV[:, :], svin, dma="cl", commit=False)
    P.emit("pool", "dma_start", OBD[:, :], obdin, dma="cl", commit=False)
    P.emit("pool", "dma_start", CF[:, :], cfin, dma="cl", commit=False)
    P.emit("pool", "dma_start", CB[:, :], cbin, dma="cl", commit=False)
    P.commit_dma("cl", writes=[("SV",), ("OBD",), ("CF",), ("CB",)])

    for l in range(LAY):
        P.emit("dve", "tensor_scalar_mul", DV[:, DVC[("bqs", l)]:DVC[("bqs", l)] + 8],
               sv_ap("bq", l, 0, 8), DHS, reads=[("SV",)], writes=[("DV",)],
               force_inc=True)
        P.emit("dve", "tensor_scalar_mul", DV[:, DVC[("bks", l)]:DVC[("bks", l)] + 8],
               sv_ap("bk", l, 0, 8), DHS, reads=[("SV",)], writes=[("DV",)],
               force_inc=True)
    P.emit("dve", "tensor_scalar_mul", DV[:, DVC[("s3h", None)]:DVC[("s3h", None)] + 8],
           sv_ap("s3", None, 0, 8), 0.5, reads=[("SV",)], writes=[("DV",)],
           force_inc=True)

    for l in range(LAY):
        ob = OBD[:, l * 64: l * 64 + 64]
        o2 = 128 * l
        P.emit("act", "activation", OM2[:, o2:o2 + 32], ob[:, 0:32], AF.Identity,
               reads=[("OBD",)], writes=[("OM2",)], force_inc=True)
        P.emit("act", "activation", OM2[:, o2 + 32:o2 + 64], ob[:, 0:32], AF.Identity,
               scale=-1.0, reads=[("OBD",)], writes=[("OM2",)], force_inc=True)
        P.emit("act", "activation", OM2[:, o2 + 64:o2 + 96], ob[:, 32:64], AF.Identity,
               reads=[("OBD",)], writes=[("OM2",)], force_inc=True)
        P.emit("act", "activation", OM2[:, o2 + 96:o2 + 128], ob[:, 32:64], AF.Identity,
               scale=-1.0, reads=[("OBD",)], writes=[("OM2",)], force_inc=True)

    P.emit("dve", "memset", S_sb[:, :], 0.0, writes=[("S", 0), ("S", 1), ("S", 2)], force_inc=True)
    for fc in range(FC):
        P.emit("dve", "memset", VT[:, fc * 130 + 64: fc * 130 + 65], 1.0,
               writes=[("VTones", fc)], force_inc=True)
        P.emit("dve", "memset", VT[:, fc * 130 + 129: fc * 130 + 130], 1.0,
               writes=[("VTones2", fc)], force_inc=True)
    P.emit("act", "memzero", FA[:, :], force_inc=True)
    P.emit("dve", "memset", FD[:, :], 0.0, force_inc=True)
    P.emit("pool", "memset", FP[:, :], 0.0, force_inc=True)

    def prepass():
        cnt = [0]

        def one(w, l, K, M, src3):
            pitch = {"w1": 2048, "w2": 2048}.get(w, K)
            for ki in range(K // 128):
                for mb in range(0, M, 1024):
                    slot = cnt[0] % 2
                    cnt[0] += 1
                    if src3 is None:
                        blk_src = {"w1": w1t, "w2": w2t, "h1": h1t,
                                   "h2": h2t, "w3": w3t}[w][
                                       ki * 128:(ki + 1) * 128, mb:mb + 1024]
                    else:
                        blk_src = src3[l, ki * 128:(ki + 1) * 128, mb:mb + 1024]
                    P.emit("pool", "dma_start", CAST[slot][:, 0:1024], blk_src,
                           dma=("pl", slot), writes=[("CAST", slot)])
                    nmi = min(8, M // 128 - mb // 128)
                    out = bass.AP(cache,
                                  stack_off(w, l, mb // 128) + ki * 128,
                                  [[CACHE_PP, 128], [pitch, nmi], [1, 128]])
                    src = CAST[slot].ap()[:, 0:1024].rearrange(
                        "p (m e) -> p m e", m=8)[:, 0:nmi, :]
                    P.emit("sp", "dma_start", out, src, dma=("ps", slot),
                           reads=[("CAST", slot)])

        one("w1", None, D, DFF, None)
        one("w2", None, D, DFF, None)
        one("h1", None, DFF, DFF, None)
        one("h2", None, DFF, DFF, None)
        one("w3", None, DFF, D, None)
        for l in range(LAY):
            one("wq", l, D, D, wq)
            one("wk", l, D, D, wk)
            one("wv", l, D, D, wv)
            one("wo", l, D, D, wo)
            one("wff1", l, D, DFF, wff1)
            one("wff2", l, DFF, D, wff2)

    prepass()

    P.begin_loop(NT)

    seq = (list(range(0, 176)) + list(range(176, 320)) + list(range(0, 176)))
    st = {"spos": 0, "issued": -1, "slot_chunk": [None] * NSLOT}

    def ensure(tpos):
        tgt = min(tpos + NSLOT - 2, len(seq) - 1)
        while st["issued"] < tgt:
            c = st["issued"] + 1
            slot = c % NSLOT
            cid = seq[c]
            src = bass.AP(cache, cid * CH, [[CACHE_PP, 128], [1, CH]])
            P.emit("sp", "dma_start", W[slot][:, :], src, dma=("rg", slot),
                   writes=[("W", slot)])
            st["issued"] = c
            st["slot_chunk"][slot] = c

    def next_stack(w, l, mi, K):
        assert st["spos"] % 1024 == 0
        exp = stack_off(w, l, mi)
        pieces = []
        for ki in range(K // 128):
            pos = st["spos"] + ki * 128
            tpos = pos // CH
            ensure(tpos)
            slot = tpos % NSLOT
            off = pos % CH
            assert seq[tpos] * CH + off == exp + ki * 128, (w, l, mi, ki)
            assert st["slot_chunk"][slot] == tpos,                 ("slot retargeted before consumption", w, l, mi, ki)
            pieces.append((W[slot], ("W", slot), off))
        st["spos"] += K
        return pieces

    rot = [0]

    def bank():
        b = rot[0] % 4
        rot[0] += 1
        return b

    def x_ap(fc, out=False):
        tns = yt if out else xt
        def mk(ctx, fc=fc, tns=tns):
            return bass.AP(tns, ctx.i * 512 + fc * 128 * S, [[S, 128], [1, 512]])
        return mk

    XSL = [XT[:, fc * 512:(fc + 1) * 512] for fc in range(FC)]
    RSL = [RS[:, fc * 512:(fc + 1) * 512] for fc in range(FC)]
    HSL = [HS[:, fc * 512:(fc + 1) * 512] for fc in range(FC)]
    RBL = [RB[:, fc * 512:(fc + 1) * 512] for fc in range(FC)]
    HBL = [HB[:, fc * 512:(fc + 1) * 512] for fc in range(FC)]
    AL = [BIGA[:, m * 512:(m + 1) * 512] for m in range(MCF)]
    BL = [BIGB[:, m * 512:(m + 1) * 512] for m in range(MCF)]

    def matmul_stack(w, l, mi, K, rhs_slices, rhs_keys, bi, extra_reads=(),
                     pieces=None):
        if pieces is None:
            pieces = next_stack(w, l, mi, K)
        n = len(pieces)
        for ki, (wt, wkey, off) in enumerate(pieces):
            P.emit("pe", "matmul", pb[bi][:, :], wt[:, off:off + 128],
                   rhs_slices[ki], start=(ki == 0), stop=(ki == n - 1),
                   reads=[wkey, rhs_keys[ki]] + list(extra_reads),
                   writes=[("pb", bi)])

    def bundle(stacks, banks, rhs_keys=()):
        rd = set()
        for pieces in stacks:
            for _, wkey, _ in pieces:
                rd.add(wkey)
        P.emit("pe", "nop", reads=sorted(rd, key=repr),
               writes=[("pb", b) for b in banks])

    def ln_stats_fc(fc, src_slice, src_key):
        r = fc % 2
        P.emit("act", "copy", SQA[:, r * 512:(r + 1) * 512], src_slice,
               reads=[src_key], writes=[("SQA", r)])
        P.emit("dve", "tensor_mul", SQA[:, (2 + r) * 512:(3 + r) * 512],
               SQA[:, r * 512:(r + 1) * 512], SQA[:, r * 512:(r + 1) * 512],
               reads=[("SQA", r)], writes=[("SQB", r)])
        P.emit("pe", "matmul", pb[4][0:1, 0:512], ONESCOL,
               SQA[:, r * 512:(r + 1) * 512], start=(fc == 0), stop=(fc == 7),
               reads=[("SQA", r), ("CB",)], writes=[("pb", 4)])
        P.emit("pe", "matmul", pb[5][0:1, 0:512], ONESCOL,
               SQA[:, (2 + r) * 512:(3 + r) * 512], start=(fc == 0),
               stop=(fc == 7), reads=[("SQB", r), ("CB",)], writes=[("pb", 5)])

    def emit_ln(src_slices, src_keys, gamma=None, beta=None,
                out_f32=None, out_f32_keys=None, out_bf=None, out_bf_keys=None,
                stats_done=False):
        if not stats_done:
            for fc in range(FC):
                ln_stats_fc(fc, src_slices[fc], src_keys[fc])
        mu = ROWS[0:1, 0:512]
        ms = ROWS[0:1, 512:1024]
        tmp = ROWS[0:1, 1024:1536]
        P.emit("act", "activation", mu, pb[4][0:1, 0:512], AF.Identity, scale=1.0 / D,
               reads=[("pb", 4)], writes=[("mu",)])
        P.emit("act", "activation", ms, pb[5][0:1, 0:512], AF.Identity, scale=1.0 / D,
               reads=[("pb", 5)], writes=[("ms",)])
        P.emit("dve", "tensor_mul", tmp, mu, mu, reads=[("mu",)], writes=[("tmp",)])
        P.emit("dve", "tensor_sub", ms, ms, tmp, reads=[("ms",), ("tmp",)],
               writes=[("ms",)])
        P.emit("act", "activation", tmp, ms, AF.Ln, bias=LN_EPS,
               reads=[("ms",)], writes=[("tmp",)])
        P.emit("act", "activation", tmp, tmp, AF.Exp, scale=-0.5,
               reads=[("tmp",)], writes=[("tmp",)])
        P.emit("pe", "matmul", pb[4][:, :], ONES_F, mu, start=True, stop=True,
               reads=[("mu",), ("CF",)], writes=[("pb", 4)])
        P.emit("pe", "matmul", pb[5][:, :], ONES_F, tmp, start=True, stop=True,
               reads=[("tmp",), ("CF",)], writes=[("pb", 5)])
        P.emit("act", "copy", RSB[:, :], pb[5][:, :], reads=[("pb", 5)],
               writes=[("RSB",)])
        for fc in range(FC):
            r = fc % 2
            t32 = T32[:, r * 512:(r + 1) * 512]
            P.emit("dve", "tensor_sub", t32, src_slices[fc], pb[4][:, :],
                   reads=[src_keys[fc], ("pb", 4)], writes=[("T32", r)])
            if gamma is None:
                if out_f32 is not None:
                    P.emit("dve", "tensor_mul", out_f32[fc], t32, RSB[:, :],
                           reads=[("T32", r), ("RSB",)], writes=[out_f32_keys[fc]])
                    P.emit("act", "copy", out_bf[fc], out_f32[fc],
                           reads=[out_f32_keys[fc]], writes=[out_bf_keys[fc]])
                else:
                    P.emit("dve", "tensor_mul", out_bf[fc], t32, RSB[:, :],
                           reads=[("T32", r), ("RSB",)], writes=[out_bf_keys[fc]])
            else:
                P.emit("dve", "tensor_mul", t32, t32, RSB[:, :],
                       reads=[("T32", r), ("RSB",)], writes=[("T32", r)])
                P.emit("act", "activation", out_f32[fc], t32, AF.Identity,
                       scale=sv_ap(gamma[0], gamma[1], fc),
                       bias=sv_ap(beta[0], beta[1], fc),
                       reads=[("T32", r), ("SV",)], writes=[out_f32_keys[fc]])
                P.emit("act", "copy", out_bf[fc], out_f32[fc],
                       reads=[out_f32_keys[fc]], writes=[out_bf_keys[fc]])

    def emit_ffn(ln_hook=None):
        for mi in range(MCF):
            p1 = next_stack("w1", None, mi, D)
            p2 = next_stack("w2", None, mi, D)
            b1 = bank()
            b2 = bank()
            bundle([p1, p2], [b1, b2])
            matmul_stack("w1", None, mi, D, RBL, [("RB", k) for k in range(FC)], b1,
                         pieces=p1)
            matmul_stack("w2", None, mi, D, RBL, [("RB", k) for k in range(FC)], b2,
                         pieces=p2)
            P.emit("act", "activation", AL[mi], pb[b1][:, :], AF.Silu,
                   scale=sv_ap("s1", None, mi), reads=[("pb", b1), ("SV",)],
                   writes=[("A", mi)])
            P.emit("act", "activation", BL[mi], pb[b2][:, :], AF.Identity,
                   scale=sv_ap("s2", None, mi), reads=[("pb", b2), ("SV",)],
                   writes=[("B", mi)])
            P.emit("dve", "tensor_mul", AL[mi], AL[mi], BL[mi],
                   reads=[("A", mi), ("B", mi)], writes=[("A", mi)])
        for mi in range(MCF):
            bi = bank()
            matmul_stack("h1", None, mi, DFF, AL, [("A", k) for k in range(MCF)], bi)
            P.emit("act", "activation", BL[mi], pb[bi][:, :], AF.Gelu,
                   scale=sv_ap("sh1", None, mi), reads=[("pb", bi), ("SV",)],
                   writes=[("B", mi)])
        for mi in range(MCF):
            bi = bank()
            matmul_stack("h2", None, mi, DFF, BL, [("B", k) for k in range(MCF)], bi)
            P.emit("act", "activation", AL[mi], pb[bi][:, :], AF.Gelu,
                   scale=sv_ap("sh2", None, mi), reads=[("pb", bi), ("SV",)],
                   writes=[("A", mi)])
        for mi in range(FC):
            bi = bank()
            matmul_stack("w3", None, mi, DFF, AL, [("A", k) for k in range(MCF)], bi)
            r = mi % 2
            t32 = T32[:, r * 512:(r + 1) * 512]
            P.emit("act", "activation", t32, pb[bi][:, :], AF.Identity,
                   scale=dv_ap("s3h", None, mi), reads=[("pb", bi), ("DV",)],
                   writes=[("T32", r)])
            P.emit("dve", "tensor_add", XSL[mi], XSL[mi], t32,
                   reads=[("XT", mi), ("T32", r)], writes=[("XT", mi)])
            if ln_hook is not None and mi >= 2:
                ln_hook(mi - 2)
        if ln_hook is not None:
            ln_hook(FC - 2)
            ln_hook(FC - 1)

    def emit_favor(l, base, side):
        for fc in range(FC):
            p = fc % 2
            bs, bu = 3 * p, 3 * p + 1
            buf = BL[base + fc]
            bkey = ("B", base + fc)
            if side == "k":
                r = fc % 2
                sq = SQA[:, r * 512:(r + 1) * 512]
                P.emit("act", "activation", sq, buf, AF.Square,
                       reads=[bkey], writes=[("SQA", r)])
                P.emit("pe", "matmul", pb[bs][0:2, 0:512], BDONES, sq,
                       start=True, stop=True, reads=[("SQA", r), ("CB",)],
                       writes=[("pb", bs)])
                P.emit("act", "activation", WROW[0:2, fc * 512:(fc + 1) * 512],
                       pb[bs][0:2, 0:512], AF.Exp, scale=-0.5, bias=LNC,
                       reads=[("pb", bs)], writes=[("WR", fc)])
            P.emit("pe", "matmul", pb[bu][:, :], OM2[:, l * 128:(l + 1) * 128],
                   buf, start=True, stop=True, reads=[bkey, ("OM2",)],
                   writes=[("pb", bu)])
            P.emit("act", "activation", buf, pb[bu][:, :], AF.Exp,
                   reads=[("pb", bu)], writes=[bkey])

    def emit_la_chunk(l, tc):
        ts = tc * 128
        lb = l * FC * 65
        P.emit("act", "copy", S_bf[:, :], S_sb[:, lb:lb + FC * 65],
               reads=[("S", l)], writes=[("Sbf",)])
        for fc in range(FC):
            p = fc % 2
            kbuf = BL[8 + fc]
            vbuf = BL[16 + fc]
            P.emit("pe", "transpose", pt[p][:, 384:386],
                   WROW[0:2, fc * 512 + ts: fc * 512 + ts + 128], IDENT[0:2, 0:2],
                   reads=[("WR", fc), ("CB",)], writes=[("pt", p)])
            P.emit("pe", "transpose", pt[p][:, 0:128], kbuf[:, ts:ts + 128],
                   IDENT, reads=[("B", 8 + fc), ("CB",)], writes=[("pt", p)])
            P.emit("pe", "transpose", pt[p][:, 128:256], vbuf[:, ts:ts + 128],
                   IDENT, reads=[("B", 16 + fc), ("CB",)], writes=[("pt", p)])
            P.emit("dve", "tensor_copy", WK[:, fc * 2:(fc + 1) * 2],
                   pt[p][:, 384:386], reads=[("pt", p)], writes=[("WK", fc)])
            P.emit("dve", "tensor_scalar_mul", KT[:, fc * 128:fc * 128 + 64],
                   pt[p][:, 0:64], WK[:, fc * 2:fc * 2 + 1],
                   reads=[("pt", p), ("WK", fc)], writes=[("KT", fc)])
            P.emit("dve", "tensor_scalar_mul", KT[:, fc * 128 + 64:fc * 128 + 128],
                   pt[p][:, 64:128], WK[:, fc * 2 + 1:fc * 2 + 2],
                   reads=[("pt", p), ("WK", fc)], writes=[("KT", fc)])
            vt_dst = VT.ap()[:, fc * 130: fc * 130 + 130].rearrange(
                "p (s c) -> p s c", s=2, c=65)[:, :, 0:64]
            P.emit("dve", "tensor_copy", vt_dst,
                   pt[p][:, 128:256].rearrange("p (s c) -> p s c", s=2, c=64),
                   reads=[("pt", p)], writes=[("VT", fc)])
        for fc in range(FC):
            p = fc % 2
            bAe, bAo = 2 * p, 2 * p + 1
            qb = BL[0 + fc]
            qk = ("B", 0 + fc)
            kb = BL[8 + fc]
            kk = ("B", 8 + fc)
            P.emit("pe", "matmul", pb[bAe][:, 0:128], kb[0:64, ts:ts + 128],
                   qb[0:64, ts:ts + 128], start=True, stop=True,
                   reads=[kk, qk], writes=[("pb", bAe)])
            P.emit("pe", "matmul", pb[bAo][:, 0:128], kb[64:128, ts:ts + 128],
                   qb[64:128, ts:ts + 128], start=True, stop=True,
                   reads=[kk, qk], writes=[("pb", bAo)])
            P.emit("dve", "scalar_tensor_tensor", ATS[:, (2 * fc) * 128:(2 * fc + 1) * 128],
                   pb[bAe][:, 0:128], WK[:, fc * 2:fc * 2 + 1], TRIU,
                   ALU.mult, ALU.mult,
                   reads=[("pb", bAe), ("WK", fc), ("CF",)], writes=[("ATS", 2 * fc)])
            P.emit("dve", "scalar_tensor_tensor", ATS[:, (2 * fc + 1) * 128:(2 * fc + 2) * 128],
                   pb[bAo][:, 0:128], WK[:, fc * 2 + 1:fc * 2 + 2], TRIU,
                   ALU.mult, ALU.mult,
                   reads=[("pb", bAo), ("WK", fc), ("CF",)], writes=[("ATS", 2 * fc + 1)])
        for fc in range(FC):
            p = fc % 2
            bO = 4 + p
            qb = BL[0 + fc]
            qk = ("B", 0 + fc)
            P.emit("pe", "matmul", pb[bO][:, 0:65],
                   qb[0:64, ts:ts + 128], S_bf[0:64, fc * 65:(fc + 1) * 65],
                   start=True, stop=False, reads=[qk, ("Sbf",)],
                   writes=[("pb", bO)])
            P.emit("pe", "matmul", pb[bO][:, 0:65],
                   ATS[:, (2 * fc) * 128:(2 * fc + 1) * 128],
                   VT[:, fc * 130: fc * 130 + 65],
                   start=False, stop=True,
                   reads=[("ATS", 2 * fc), ("VT", fc), ("VTones", fc)],
                   writes=[("pb", bO)])
            P.emit("pe", "matmul", pb[bO][:, 65:130],
                   qb[64:128, ts:ts + 128], S_bf[64:128, fc * 65:(fc + 1) * 65],
                   start=True, stop=False, reads=[qk, ("Sbf",)],
                   writes=[("pb", bO)])
            P.emit("pe", "matmul", pb[bO][:, 65:130],
                   ATS[:, (2 * fc + 1) * 128:(2 * fc + 2) * 128],
                   VT[:, fc * 130 + 65: fc * 130 + 130],
                   start=False, stop=True,
                   reads=[("ATS", 2 * fc + 1), ("VT", fc), ("VTones2", fc)],
                   writes=[("pb", bO)])
            dens = bass.AP(pbt[bO], 64, [[512, 128], [65, 2]])
            P.emit("act", "activation", RD2[:, fc * 4: fc * 4 + 2], dens,
                   AF.Identity, bias=ATTN_EPS, reads=[("pb", bO)],
                   writes=[("RDr", fc)])
            P.emit("dve", "reciprocal", RD2[:, fc * 4 + 2: fc * 4 + 4],
                   RD2[:, fc * 4: fc * 4 + 2], reads=[("RDr", fc)],
                   writes=[("RD", fc)])
            P.emit("act", "activation", ATM[:, fc * 128: fc * 128 + 64],
                   pb[bO][:, 0:64], AF.Identity,
                   scale=RD2[:, fc * 4 + 2: fc * 4 + 3],
                   reads=[("pb", bO), ("RD", fc)], writes=[("ATM", fc)])
            P.emit("act", "activation", ATM[:, fc * 128 + 64: fc * 128 + 128],
                   pb[bO][:, 65:129], AF.Identity,
                   scale=RD2[:, fc * 4 + 3: fc * 4 + 4],
                   reads=[("pb", bO), ("RD", fc)], writes=[("ATM", fc)])
        for fc in range(FC):
            p = fc % 2
            bAo = 2 * p + 1
            P.emit("pe", "transpose", pt[p][:, 256:384],
                   ATM[:, fc * 128:(fc + 1) * 128], IDENT,
                   reads=[("ATM", fc), ("CB",)], writes=[("pt", p)])
            P.emit("dve", "tensor_copy", BL[24 + fc][:, ts:ts + 128],
                   pt[p][:, 256:384], reads=[("pt", p)], writes=[("B", 24 + fc)])
            P.emit("pe", "matmul", pb[bAo][0:64, 128:193],
                   KT[:, fc * 128: fc * 128 + 64],
                   VT[:, fc * 130: fc * 130 + 65], start=True, stop=True,
                   reads=[("KT", fc), ("VT", fc), ("VTones", fc)],
                   writes=[("pb", bAo)])
            P.emit("pe", "matmul", pb[bAo][64:128, 128:193],
                   KT[:, fc * 128 + 64: fc * 128 + 128],
                   VT[:, fc * 130 + 65: fc * 130 + 130], start=True, stop=True,
                   tile_position=(0, 64),
                   reads=[("KT", fc), ("VT", fc), ("VTones2", fc)],
                   writes=[("pb", bAo)])
            P.emit("dve", "tensor_add", S_sb[:, lb + fc * 65:lb + (fc + 1) * 65],
                   S_sb[:, lb + fc * 65:lb + (fc + 1) * 65], pb[bAo][:, 128:193],
                   reads=[("S", l), ("pb", bAo)], writes=[("S", l)])

    def emit_attn(l):
        for (wname, base, scale, bias_nm, bias_dv) in (
                ("wq", 0, DHS, "bqs", True), ("wk", 8, DHS, "bks", True),
                ("wv", 16, 1.0, "bv", False)):
            for mi0 in range(0, FC, 2):
                ps = [next_stack(wname, l, mi0, D), next_stack(wname, l, mi0 + 1, D)]
                bs2 = [bank(), bank()]
                bundle(ps, bs2)
                for j, mi in enumerate((mi0, mi0 + 1)):
                    matmul_stack(wname, l, mi, D, RBL,
                                 [("RB", k) for k in range(FC)], bs2[j],
                                 pieces=ps[j])
                    b_ap = (dv_ap(bias_nm, l, mi) if bias_dv
                            else sv_ap(bias_nm, l, mi))
                    P.emit("act", "activation", BL[base + mi], pb[bs2[j]][:, :],
                           AF.Identity, scale=scale, bias=b_ap,
                           reads=[("pb", bs2[j]), ("DV",) if bias_dv else ("SV",)],
                           writes=[("B", base + mi)])
        emit_favor(l, 0, "q")
        emit_favor(l, 8, "k")
        for tc in range(TC):
            emit_la_chunk(l, tc)
        for mi in range(FC):
            bi = bank()
            matmul_stack("wo", l, mi, D, [BL[24 + k] for k in range(FC)],
                         [("B", 24 + k) for k in range(FC)], bi)
            r = mi % 2
            t32 = T32[:, r * 512:(r + 1) * 512]
            P.emit("act", "activation", t32, pb[bi][:, :], AF.Identity,
                   bias=sv_ap("bo", l, mi), reads=[("pb", bi), ("SV",)],
                   writes=[("T32", r)])
            P.emit("dve", "tensor_add", HSL[mi], RSL[mi], t32,
                   reads=[("RS", mi), ("T32", r)], writes=[("HS", mi)])
            if mi >= 2:
                ln_stats_fc(mi - 2, HSL[mi - 2], ("HS", mi - 2))
        for mi in (FC - 2, FC - 1):
            ln_stats_fc(mi, HSL[mi], ("HS", mi))
        if DEV_STAGE == f"wo{l}":
            return
        emit_ln(HSL, [("HS", k) for k in range(FC)], gamma=("g1", l),
                beta=("b1", l), out_f32=HSL,
                out_f32_keys=[("HS", k) for k in range(FC)],
                out_bf=HBL, out_bf_keys=[("HB", k) for k in range(FC)],
                stats_done=True)
        if DEV_STAGE == f"ln1_{l}":
            return
        for mi0 in range(0, MCF, 2):
            ps = [next_stack("wff1", l, mi0, D), next_stack("wff1", l, mi0 + 1, D)]
            bs2 = [bank(), bank()]
            bundle(ps, bs2)
            for j, mi in enumerate((mi0, mi0 + 1)):
                matmul_stack("wff1", l, mi, D, HBL,
                             [("HB", k) for k in range(FC)], bs2[j], pieces=ps[j])
                P.emit("act", "activation", AL[mi], pb[bs2[j]][:, :], AF.Relu,
                       bias=sv_ap("bf1", l, mi), reads=[("pb", bs2[j]), ("SV",)],
                       writes=[("A", mi)])
        for mi in range(FC):
            bi = bank()
            matmul_stack("wff2", l, mi, DFF, AL, [("A", k) for k in range(MCF)], bi)
            r = mi % 2
            t32 = T32[:, r * 512:(r + 1) * 512]
            P.emit("act", "activation", t32, pb[bi][:, :], AF.Identity,
                   bias=sv_ap("bf2", l, mi), reads=[("pb", bi), ("SV",)],
                   writes=[("T32", r)])
            P.emit("dve", "tensor_add", HSL[mi], HSL[mi], t32,
                   reads=[("HS", mi), ("T32", r)], writes=[("HS", mi)])
            if mi >= 2:
                ln_stats_fc(mi - 2, HSL[mi - 2], ("HS", mi - 2))
        for mi in (FC - 2, FC - 1):
            ln_stats_fc(mi, HSL[mi], ("HS", mi))
        if DEV_STAGE == f"y{l}":
            return
        emit_ln(HSL, [("HS", k) for k in range(FC)], gamma=("g2", l),
                beta=("b2", l), out_f32=RSL,
                out_f32_keys=[("RS", k) for k in range(FC)],
                out_bf=RBL, out_bf_keys=[("RB", k) for k in range(FC)],
                stats_done=True)

    for fc in range(FC):
        P.emit("pool", "dma_start", XSL[fc], x_ap(fc), dma="xl", commit=False)
    P.commit_dma("xl", writes=[("XT", k) for k in range(FC)])

    def store(slices, keys):
        for fc in range(FC):
            P.emit("pool", "dma_start", x_ap(fc, out=True), slices[fc],
                   dma="yo", reads=[keys[fc]], writes=[("yo", fc)])

    done = False
    while True:
        emit_ln(XSL, [("XT", k) for k in range(FC)], out_bf=RBL,
                out_bf_keys=[("RB", k) for k in range(FC)])
        emit_ffn(ln_hook=(lambda mi: ln_stats_fc(mi, XSL[mi], ("XT", mi)))
                 if DEV_STAGE not in ("ffn1",) else None)
        if DEV_STAGE == "ffn1":
            store(XSL, [("XT", k) for k in range(FC)]); break
        emit_ln(XSL, [("XT", k) for k in range(FC)], out_f32=RSL,
                out_f32_keys=[("RS", k) for k in range(FC)], out_bf=RBL,
                out_bf_keys=[("RB", k) for k in range(FC)], stats_done=True)
        if DEV_STAGE == "preln":
            store(RSL, [("RS", k) for k in range(FC)]); break
        brk = False
        for l in range(LAY):
            emit_attn(l)
            if DEV_STAGE in (f"l{l}", f"wo{l}", f"ln1_{l}", f"y{l}"):
                brk = True
                break
        if brk:
            if DEV_STAGE.startswith("l") and not DEV_STAGE.startswith("ln1") and not DEV_STAGE.startswith("y"):
                store(RSL, [("RS", k) for k in range(FC)])
            else:
                store(HSL, [("HS", k) for k in range(FC)])
            break
        for fc in range(FC):
            P.emit("dve", "tensor_add", XSL[fc], XSL[fc], RSL[fc],
                   reads=[("XT", fc), ("RS", fc)], writes=[("XT", fc)])
            if fc >= 2:
                ln_stats_fc(fc - 2, XSL[fc - 2], ("XT", fc - 2))
        for fc in (FC - 2, FC - 1):
            ln_stats_fc(fc, XSL[fc], ("XT", fc))
        if DEV_STAGE == "xadd":
            store(XSL, [("XT", k) for k in range(FC)]); break
        emit_ln(XSL, [("XT", k) for k in range(FC)], out_bf=RBL,
                out_bf_keys=[("RB", k) for k in range(FC)], stats_done=True)
        emit_ffn()
        store(XSL, [("XT", k) for k in range(FC)])
        done = True
        break

    if done:
        assert st["spos"] == len(seq) * CH, st["spos"]
    P.end_loop()
    P.finalize(nc, fences={"pool": FP[:, :], "act": FA[0:1, 0:1],
                           "dve": FD[0:1, 0:1]})
    n_ops = sum(len(P.ops[e]) for e in ENGINES)
    return nc, n_ops, P.n_waits


def host_prepare(inputs):
    f = np.ascontiguousarray

    def svblock(v):
        v = np.asarray(v)
        return f(v.reshape(-1, 128).T)

    sv = np.zeros((128, SV_N), np.float32)
    for l in range(LAY):
        for nm, src in (("bq", inputs["L0_attn_bq"][l]),
                        ("bk", inputs["L0_attn_bk"][l]),
                        ("bv", inputs["L0_attn_bv"][l]),
                        ("bo", inputs["L0_attn_bo"][l]),
                        ("g1", inputs["L0_attn_ln1_g"][l]),
                        ("b1", inputs["L0_attn_ln1_b"][l]),
                        ("g2", inputs["L0_attn_ln2_g"][l]),
                        ("b2", inputs["L0_attn_ln2_b"][l]),
                        ("bf1", inputs["L0_attn_bf1"][l]),
                        ("bf2", inputs["L0_attn_bf2"][l])):
            blk = svblock(src)
            sv[:, SVC[(nm, l)]:SVC[(nm, l)] + blk.shape[1]] = blk
    for nm, src in (("s1", inputs["L0_ffn_w1_s"]), ("s2", inputs["L0_ffn_w2_s"]),
                    ("sh1", inputs["L0_ffn_h1_s"]), ("sh2", inputs["L0_ffn_h2_s"]),
                    ("s3", inputs["L0_ffn_w3_s"])):
        blk = svblock(src)
        sv[:, SVC[(nm, None)]:SVC[(nm, None)] + blk.shape[1]] = blk

    obd = np.zeros((128, 64 * LAY), np.float32)
    om = np.asarray(inputs["L0_attn_omega"])
    for l in range(LAY):
        obd[0:64, l * 64:l * 64 + 32] = om[l]
        obd[64:128, l * 64 + 32:l * 64 + 64] = om[l]

    cf = np.zeros((128, 272), np.float32)
    cf[:, 0:128] = np.triu(np.ones((128, 128), np.float32))
    cf[0, 128:192] = 1.0
    cf[1, 192:256] = 1.0
    cf[:, 256:256 + len(CONSTVALS)] = np.asarray(CONSTVALS, np.float32)[None, :]
    cb = np.zeros((128, 131), np.float32)
    cb[:, 0:128] = np.eye(128, dtype=np.float32)
    cb[:, 128] = 1.0
    cb[0:64, 129] = 1.0
    cb[64:128, 130] = 1.0
    cb = cb.astype(ml_dtypes.bfloat16)

    shared = dict(
        w1t=f(np.asarray(inputs["L0_ffn_w1_w"]).T),
        w2t=f(np.asarray(inputs["L0_ffn_w2_w"]).T),
        h1t=f(np.asarray(inputs["L0_ffn_h1_w"]).T),
        h2t=f(np.asarray(inputs["L0_ffn_h2_w"]).T),
        w3t=f(np.asarray(inputs["L0_ffn_w3_w"]).T),
        wq=f(np.asarray(inputs["L0_attn_Wq"])),
        wk=f(np.asarray(inputs["L0_attn_Wk"])),
        wv=f(np.asarray(inputs["L0_attn_Wv"])),
        wo=f(np.asarray(inputs["L0_attn_Wo"])),
        wff1=f(np.asarray(inputs["L0_attn_W1"])),
        wff2=f(np.asarray(inputs["L0_attn_W2"])),
        svin=sv, obdin=obd, cfin=cf, cbin=cb,
    )
    x = np.asarray(inputs["L0_attn_x"])
    xts = [f(x[c].T).reshape(-1) for c in range(B)]
    return shared, xts


_CACHE = {}


def kernel(**inputs):
    shared, xts = host_prepare(inputs)
    if "nc" not in _CACHE:
        nc, n_ops, n_waits = build_nc(NTILE)
        _CACHE["nc"] = nc
    nc = _CACHE["nc"]
    in_maps = [dict(shared, xt=xts[c]) for c in range(B)]
    res = run_bass_kernel_spmd(nc, in_maps, core_ids=list(range(B)))
    out = np.empty((B, S, D), np.float32)
    for c in range(B):
        out[c] = res.results[c]["yt"].reshape(D, S).T
    return out


if __name__ == "__main__":
    nc, n_ops, n_waits = build_nc(1)
    print(f"ops={n_ops} waits={n_waits}")


# version 06
# speedup vs baseline: 1.1037x; 1.0525x over previous
import numpy as np
import ml_dtypes
import concourse.bass as bass
import concourse.mybir as mybir
from concourse.bass_utils import run_bass_kernel_spmd

F32 = mybir.dt.float32
BF16 = mybir.dt.bfloat16
AF = mybir.ActivationFunctionType
ALU = mybir.AluOpType

D = 1024
DFF = 4096
S = 8192
B = 8
T = 512
NTILE = S // T
FC = D // 128
MCF = DFF // 128
TC = T // 128
NH = 16
DH = 64
LAY = 3
DHS = float(DH) ** -0.25
LNC = float(np.log(2 * 32) * -0.5)
LN_EPS = 1e-5
ATTN_EPS = 1e-6
NSLOT = 4
CH = 2048
CONSTVALS = (1.0 / D, LN_EPS, -0.5, 0.5, -LNC, LNC, ATTN_EPS, DHS, -1.0, 1.0,
             0.0, 2.0, 0.25)

ENGINES = ("pe", "act", "dve", "pool", "sp")
PARANOID = frozenset(("act", "dve", "pool"))


class Ctx:
    def __init__(self):
        self.i = None


class Prog:
    def __init__(self):
        self.ops = {e: [] for e in ENGINES}
        self.waits = {e: [] for e in ENGINES}
        self.opclock = {e: [] for e in ENGINES}
        self.dmaclock = {}
        self.clock = {e: {} for e in ENGINES}
        self.group_count = {}
        self.group_engine = {}
        self.need_inc = {e: set() for e in ENGINES}
        self.res = {}
        self.n_waits = 0
        self.seg = 0
        self.seg_start = {e: [0] for e in ENGINES}
        self.loop = None
        self.group_final = []
        self._seen_groups = set()

    def _need(self, eng, clk, needed, p):
        if p is None:
            return
        if p[0] == "e":
            if p[1] == eng and eng not in PARANOID:
                return
            sk, v = ("e", p[1]), p[2]
        else:
            sk, v = ("d", p[1]), p[2]
        if clk.get(sk, -1) >= v:
            return
        if needed.get(sk, -1) < v:
            needed[sk] = v

    def emit(self, eng, fn, *args, reads=(), writes=(), dma=None, commit=True,
             force_inc=False, **kwargs):
        pos = len(self.ops[eng])
        clk = self.clock[eng]
        needed = {}
        for k in reads:
            st = self.res.get(k)
            if st is not None:
                self._need(eng, clk, needed, st[0])
        for k in writes:
            st = self.res.get(k)
            if st is not None:
                self._need(eng, clk, needed, st[0])
                for r in st[1]:
                    self._need(eng, clk, needed, r)
        wl = []
        for sk, v in sorted(needed.items(), key=repr):
            if sk[0] == "e":
                self.need_inc[sk[1]].add((sk[1], v))
                snap = self.opclock[sk[1]][v]
            else:
                snap = self.dmaclock[(sk[1], v)]
            for k2, v2 in snap.items():
                if clk.get(k2, -1) < v2:
                    clk[k2] = v2
            if clk.get(sk, -1) < v:
                clk[sk] = v
            wl.append((sk, v))
        self.n_waits += len(wl)
        self.waits[eng].append(wl)
        self.ops[eng].append((fn, args, kwargs, dma, self.seg))
        if force_inc and dma is None:
            self.need_inc[eng].add((eng, pos))
        if dma is not None:
            assert self.group_engine.setdefault(dma, eng) == eng
            cnt = self.group_count.get(dma, 0) + 1
            self.group_count[dma] = cnt
            me = ("d", dma, cnt)
            self.dmaclock[(dma, cnt)] = dict(clk)
            self.opclock[eng].append(dict(clk))
        else:
            me = ("e", eng, pos)
            snap = dict(clk)
            snap[("e", eng)] = pos
            self.opclock[eng].append(snap)
        if commit:
            self._register(me, reads, writes)
        return me

    def commit_dma(self, group, reads=(), writes=()):
        cnt = self.group_count[group]
        self._register(("d", group, cnt), reads, writes)

    def _register(self, me, reads, writes):
        for k in writes:
            self.res[k] = [me, []]
        for k in reads:
            st = self.res.get(k)
            if st is None:
                st = [None, []]
                self.res[k] = st
            st[1].append(me)

    def _cut_segment(self, marker):
        for g in self.group_count:
            assert g not in self._seen_groups, f"group {g} in two segments"
        self._seen_groups |= set(self.group_count)
        self.group_final.append(dict(self.group_count))
        for e in ENGINES:
            self.ops[e].append((marker, None, None, None, self.seg))
            self.waits[e].append([])
            self.opclock[e].append({})
        self.seg += 1
        for e in ENGINES:
            self.seg_start[e].append(len(self.ops[e]))
            self.clock[e] = {}
        self.group_count = {}
        self.dmaclock = {}
        self.res = {k: [None, []] for k in self.res}

    def begin_loop(self, n_iters):
        assert self.loop is None
        self._cut_segment("@LOOP")
        self.loop = n_iters

    def end_loop(self):
        self._cut_segment("@ENDLOOP")

    def finalize(self, nc, fences=None):
        if fences is None:
            fences = {}
        self._cut_segment("@END")
        sems = {}
        for e in ENGINES:
            sems[("e", e)] = nc.alloc_semaphore(f"sem_{e}")
        for g in self.group_engine:
            nm = "_".join(map(str, g)) if isinstance(g, tuple) else str(g)
            sems[("d", g)] = nc.alloc_semaphore(f"dg_{nm}")
        bar = nc.alloc_semaphore("bar")
        bar2 = nc.alloc_semaphore("bar2")
        clear_ids = [sems[("e", e)] for e in ENGINES]
        body_per_iter = (self.group_final[1]
                         if self.loop is not None and len(self.group_final) > 1
                         else {})

        def seg_of(e, pos):
            lo = 0
            for si, st in enumerate(self.seg_start[e]):
                if st <= pos:
                    lo = si
            return lo

        inc_rank = {e: {} for e in ENGINES}
        inc_totals = {e: {} for e in ENGINES}
        for e in ENGINES:
            for p in sorted(p for (_, p) in self.need_inc[e]):
                s = seg_of(e, p)
                inc_totals[e][s] = inc_totals[e].get(s, 0) + 1
                inc_rank[e][p] = inc_totals[e][s]
        markedset = {e: set(p for (_, p) in self.need_inc[e]) for e in ENGINES}
        n_eng = len(ENGINES)
        loop_n = self.loop
        engmap = dict(pe="tensor", act="scalar", dve="vector", pool="gpsimd",
                      sp="sync")
        enumap = dict(pe=mybir.EngineType.PE, act=mybir.EngineType.Activation,
                      dve=mybir.EngineType.DVE, pool=mybir.EngineType.Pool,
                      sp=mybir.EngineType.SP)

        dyn_pairs = {e: [] for e in ENGINES}
        if loop_n is not None:
            for e in ENGINES:
                seen = dict()
                lo = self.seg_start[e][1]
                hi = self.seg_start[e][2] - 1
                for i in range(lo, hi):
                    for sk, v in self.waits[e][i]:
                        if sk[0] == "d" and sk[1] not in seen:
                            seen[sk[1]] = True
                for g, cnt in self.group_final[1].items():
                    if self.group_engine[g] == e and cnt and g not in seen:
                        seen[g] = True
                dyn_pairs[e] = list(seen)
        waitregs = {}
        for e in ENGINES:
            for g in dyn_pairs[e]:
                rh = nc.alloc_registers(f"wr_{e}_{g}", engines=[enumap[e]])
                waitregs[(e, g)] = list(rh)[0]
        cur_target = {}

        def dyn_wait(engine, e, g, target):
            reg = waitregs[(e, g)]
            cur = cur_target.get((e, g), 0)
            delta = target - cur
            assert delta >= 0, (e, g, target, cur)
            if delta:
                engine.reg_alu(reg, reg, delta, mybir.AluOpType.add)
            engine.wait_ge(sems[("d", g)], reg)
            cur_target[(e, g)] = target

        def emit_barrier(engine, e, ctx, seg_idx, in_loop):
            gf = self.group_final[seg_idx]
            for g, cnt in sorted(gf.items(), key=repr):
                if self.group_engine[g] == e and cnt:
                    if in_loop:
                        dyn_wait(engine, e, g, 16 * cnt)
                    else:
                        engine.wait_ge(sems[("d", g)], 16 * cnt)
            if in_loop:
                per_map = self.group_final[1]
                for g in dyn_pairs[e]:
                    per = per_map.get(g, 0)
                    tgt = 16 * per
                    cur = cur_target.get((e, g), 0)
                    if tgt != cur:
                        engine.reg_alu(waitregs[(e, g)], waitregs[(e, g)],
                                       tgt - cur, mybir.AluOpType.add)
                    cur_target[(e, g)] = 0
            for e2 in ENGINES:
                tot = inc_totals[e2].get(seg_idx, 0)
                if tot:
                    engine.wait_ge(sems[("e", e2)], tot)
            if e in ("pe", "sp") or fences.get(e) is None:
                engine.nop().then_inc(bar, 1)
            elif e == "act":
                engine.copy(fences[e], fences[e]).then_inc(bar, 1)
            else:
                engine.memset(fences[e], 0.0).then_inc(bar, 1)
            nbar = seg_idx + 1
            if in_loop:
                barv = n_eng * nbar + n_eng * ctx.i
                bar2v = nbar + ctx.i
            else:
                barv = n_eng * nbar
                bar2v = nbar
            if e == "pool":
                engine.wait_ge(bar, barv)
                for s in clear_ids:
                    engine.sem_clear(s)
                engine.nop().then_inc(bar2, 1)
            else:
                engine.wait_ge(bar2, bar2v)

        with nc.Block() as blk:
            for e in ENGINES:
                def body(engine, e=e):
                    ctx = Ctx()
                    esem = sems[("e", e)]

                    def run_ops(lo, hi):
                        for i in range(lo, hi):
                            fn, args, kwargs, dma, _seg = self.ops[e][i]
                            for sk, v in self.waits[e][i]:
                                if sk[0] == "e":
                                    engine.wait_ge(sems[sk], inc_rank[sk[1]][v])
                                elif ctx.i is not None:
                                    dyn_wait(engine, e, sk[1], 16 * v)
                                else:
                                    engine.wait_ge(sems[sk], 16 * v)
                            rargs = [a(ctx) if callable(a) else a for a in args]
                            inst = getattr(engine, fn)(*rargs, **kwargs)
                            if dma is not None:
                                inst.then_inc(sems[("d", dma)], 16)
                            elif i in markedset[e]:
                                inst.then_inc(esem, 1)

                    marks = [i for i, op in enumerate(self.ops[e])
                             if op[0] in ("@LOOP", "@ENDLOOP", "@END")]
                    assert loop_n is not None
                    mloop, mend, mfin = marks
                    run_ops(0, mloop)
                    emit_barrier(engine, e, ctx, self.ops[e][mloop][4], False)
                    for g in dyn_pairs[e]:
                        engine.reg_mov(waitregs[(e, g)], 0)
                    with engine.Fori(0, loop_n) as iv:
                        ctx.i = iv
                        run_ops(mloop + 1, mend)
                        emit_barrier(engine, e, ctx, self.ops[e][mend][4], True)
                    ctx.i = None
                    run_ops(mend + 1, mfin)

                getattr(blk, engmap[e])(body)
        return sems


FFN_SZ = 2 * 32 * 1024 + 2 * 32 * 4096 + 8 * 4096
ATT_SZ = 4 * 8 * 1024 + 32 * 1024 + 8 * 4096
CACHE_PP = FFN_SZ + LAY * ATT_SZ


def stack_off(w, l, mi):
    if w == "w1":
        return mi * 2048
    if w == "w2":
        return mi * 2048 + 1024
    if w == "h1":
        return 65536 + mi * 4096
    if w == "h2":
        return 196608 + mi * 4096
    if w == "w3":
        return 327680 + mi * 4096
    base = FFN_SZ + l * ATT_SZ
    if w == "wq":
        return base + mi * 1024
    if w == "wk":
        return base + 8192 + mi * 1024
    if w == "wv":
        return base + 16384 + mi * 1024
    if w == "wo":
        return base + 24576 + mi * 1024
    if w == "wff1":
        return base + 32768 + mi * 1024
    if w == "wff2":
        return base + 65536 + mi * 4096
    raise KeyError(w)


def sv_cols():
    cols = {}
    c = 0
    for l in range(LAY):
        for nm, n in (("bq", 8), ("bk", 8), ("bv", 8), ("bo", 8),
                      ("g1", 8), ("b1", 8), ("g2", 8), ("b2", 8),
                      ("bf1", 32), ("bf2", 8)):
            cols[(nm, l)] = c
            c += n
    for nm, n in (("s1", 32), ("s2", 32), ("sh1", 32), ("sh2", 32), ("s3", 8)):
        cols[(nm, None)] = c
        c += n
    return cols, c


SVC, SV_N = sv_cols()
DVC = {}
_c = 0
for l in range(LAY):
    DVC[("bqs", l)] = _c; _c += 8
    DVC[("bks", l)] = _c; _c += 8
DVC[("s3h", None)] = _c; _c += 8
DV_N = _c


import os as _os
DEV_STAGE = _os.environ.get("K_STAGE", "")


def build_nc(NT=NTILE):
    nc = bass.Bass("TRN2", target_bir_lowering=False)

    xt = nc.dram_tensor("xt", [D * S], F32, kind="ExternalInput")
    yt = nc.dram_tensor("yt", [D * S], F32, kind="ExternalOutput")
    w1t = nc.dram_tensor("w1t", [D, DFF], F32, kind="ExternalInput").ap()
    w2t = nc.dram_tensor("w2t", [D, DFF], F32, kind="ExternalInput").ap()
    h1t = nc.dram_tensor("h1t", [DFF, DFF], F32, kind="ExternalInput").ap()
    h2t = nc.dram_tensor("h2t", [DFF, DFF], F32, kind="ExternalInput").ap()
    w3t = nc.dram_tensor("w3t", [DFF, D], F32, kind="ExternalInput").ap()
    wq = nc.dram_tensor("wq", [LAY, D, D], F32, kind="ExternalInput").ap()
    wk = nc.dram_tensor("wk", [LAY, D, D], F32, kind="ExternalInput").ap()
    wv = nc.dram_tensor("wv", [LAY, D, D], F32, kind="ExternalInput").ap()
    wo = nc.dram_tensor("wo", [LAY, D, D], F32, kind="ExternalInput").ap()
    wff1 = nc.dram_tensor("wff1", [LAY, D, DFF], F32, kind="ExternalInput").ap()
    wff2 = nc.dram_tensor("wff2", [LAY, DFF, D], F32, kind="ExternalInput").ap()
    svin = nc.dram_tensor("svin", [128, SV_N], F32, kind="ExternalInput").ap()
    obdin = nc.dram_tensor("obdin", [128, 64 * LAY], F32, kind="ExternalInput").ap()
    cfin = nc.dram_tensor("cfin", [128, 272], F32, kind="ExternalInput").ap()
    cbin = nc.dram_tensor("cbin", [128, 131], BF16, kind="ExternalInput").ap()
    cache = nc.dram_tensor("wcache", [128 * CACHE_PP], BF16, kind="Internal")

    XT = nc.alloc_sbuf_tensor("XT", [128, FC * 512], F32)
    RS = nc.alloc_sbuf_tensor("RS", [128, FC * 512], F32)
    HS = nc.alloc_sbuf_tensor("HS", [128, FC * 512], F32)
    RB = nc.alloc_sbuf_tensor("RB", [128, FC * 512], BF16)
    HB = nc.alloc_sbuf_tensor("HB", [128, FC * 512], BF16)
    BIGA = nc.alloc_sbuf_tensor("BIGA", [128, MCF * 512], BF16)
    BIGB = nc.alloc_sbuf_tensor("BIGB", [128, MCF * 512], BF16)
    W = [nc.alloc_sbuf_tensor(f"W{i}", [128, CH], BF16) for i in range(NSLOT)]
    KT = nc.alloc_sbuf_tensor("KT", [128, FC * 128], BF16)
    VT = nc.alloc_sbuf_tensor("VT", [128, FC * 130], BF16)
    S_sb = nc.alloc_sbuf_tensor("S_sb", [128, LAY * FC * 65], F32)
    S_bf = nc.alloc_sbuf_tensor("S_bf", [128, FC * 65], BF16)
    ATS = nc.alloc_sbuf_tensor("ATS", [128, 16 * 128], BF16)
    SQA = nc.alloc_sbuf_tensor("SQA", [128, 4 * 512], BF16)
    T32 = nc.alloc_sbuf_tensor("T32", [128, 2 * 512], F32)
    RSB = nc.alloc_sbuf_tensor("RSB", [128, 512], F32)
    ROWS = nc.alloc_sbuf_tensor("ROWS", [1, 1536], F32)
    WROW = nc.alloc_sbuf_tensor("WROW", [2, FC * 512], BF16)
    WK = nc.alloc_sbuf_tensor("WK", [128, FC * 2], F32)
    ATM = nc.alloc_sbuf_tensor("ATM", [128, FC * 128], BF16)
    RD2 = nc.alloc_sbuf_tensor("RD2", [128, FC * 4], F32)
    SV = nc.alloc_sbuf_tensor("SV", [128, SV_N], F32)
    DV = nc.alloc_sbuf_tensor("DV", [128, DV_N], F32)
    OBD = nc.alloc_sbuf_tensor("OBD", [128, 64 * LAY], F32)
    OM2 = nc.alloc_sbuf_tensor("OM2", [128, 128 * LAY], BF16)
    CF = nc.alloc_sbuf_tensor("CF", [128, 272], F32)
    CB = nc.alloc_sbuf_tensor("CB", [128, 131], BF16)
    CAST = [nc.alloc_sbuf_tensor(f"CAST{i}", [128, 1024], BF16) for i in range(4)]
    FP = nc.alloc_sbuf_tensor("FP", [128, 1], F32)
    FA = nc.alloc_sbuf_tensor("FA", [1, 2], F32)
    FD = nc.alloc_sbuf_tensor("FD", [1, 2], F32)

    pbt = [nc.alloc_psum_tensor(f"pb{i}", [128, 512], F32) for i in range(6)]
    pb = pbt
    pt = [nc.alloc_psum_tensor(f"pt{i}", [128, 1024], BF16) for i in range(2)]

    P = Prog()

    def sv_ap(nm, l, mi, n=1):
        return SV[:, SVC[(nm, l)] + mi: SVC[(nm, l)] + mi + n]

    def dv_ap(nm, l, mi):
        return DV[:, DVC[(nm, l)] + mi: DVC[(nm, l)] + mi + 1]

    for j, v in enumerate(CONSTVALS):
        nc.const_aps.register(F32, v, CF[:, 256 + j:257 + j])

    ONES_F = CF[0:1, 0:128]
    TRIU = CF[:, 0:128]
    E2 = CF[0:2, 128:256]
    IDENT = CB[:, 0:128]
    ONESCOL = CB[:, 128:129]
    BDONES = CB[:, 129:131]

    P.emit("pool", "dma_start", SV[:, :], svin, dma="cl", commit=False)
    P.emit("pool", "dma_start", OBD[:, :], obdin, dma="cl", commit=False)
    P.emit("pool", "dma_start", CF[:, :], cfin, dma="cl", commit=False)
    P.emit("pool", "dma_start", CB[:, :], cbin, dma="cl", commit=False)
    P.commit_dma("cl", writes=[("SV",), ("OBD",), ("CF",), ("CB",)])

    for l in range(LAY):
        P.emit("dve", "tensor_scalar_mul", DV[:, DVC[("bqs", l)]:DVC[("bqs", l)] + 8],
               sv_ap("bq", l, 0, 8), DHS, reads=[("SV",)], writes=[("DV",)],
               force_inc=True)
        P.emit("dve", "tensor_scalar_mul", DV[:, DVC[("bks", l)]:DVC[("bks", l)] + 8],
               sv_ap("bk", l, 0, 8), DHS, reads=[("SV",)], writes=[("DV",)],
               force_inc=True)
    P.emit("dve", "tensor_scalar_mul", DV[:, DVC[("s3h", None)]:DVC[("s3h", None)] + 8],
           sv_ap("s3", None, 0, 8), 0.5, reads=[("SV",)], writes=[("DV",)],
           force_inc=True)

    for l in range(LAY):
        ob = OBD[:, l * 64: l * 64 + 64]
        o2 = 128 * l
        P.emit("act", "activation", OM2[:, o2:o2 + 32], ob[:, 0:32], AF.Identity,
               reads=[("OBD",)], writes=[("OM2",)], force_inc=True)
        P.emit("act", "activation", OM2[:, o2 + 32:o2 + 64], ob[:, 0:32], AF.Identity,
               scale=-1.0, reads=[("OBD",)], writes=[("OM2",)], force_inc=True)
        P.emit("act", "activation", OM2[:, o2 + 64:o2 + 96], ob[:, 32:64], AF.Identity,
               reads=[("OBD",)], writes=[("OM2",)], force_inc=True)
        P.emit("act", "activation", OM2[:, o2 + 96:o2 + 128], ob[:, 32:64], AF.Identity,
               scale=-1.0, reads=[("OBD",)], writes=[("OM2",)], force_inc=True)

    P.emit("dve", "memset", S_sb[:, :], 0.0, writes=[("S", 0), ("S", 1), ("S", 2)], force_inc=True)
    for fc in range(FC):
        P.emit("dve", "memset", VT[:, fc * 130 + 64: fc * 130 + 65], 1.0,
               writes=[("VTones", fc)], force_inc=True)
        P.emit("dve", "memset", VT[:, fc * 130 + 129: fc * 130 + 130], 1.0,
               writes=[("VTones2", fc)], force_inc=True)
    P.emit("act", "memzero", FA[:, :], force_inc=True)
    P.emit("dve", "memset", FD[:, :], 0.0, force_inc=True)
    P.emit("pool", "memset", FP[:, :], 0.0, force_inc=True)

    def prepass():
        cnt = [0]

        def one(w, l, K, M, src3):
            pitch = {"w1": 2048, "w2": 2048}.get(w, K)
            for ki in range(K // 128):
                for mb in range(0, M, 1024):
                    slot = cnt[0] % 4
                    cnt[0] += 1
                    if src3 is None:
                        blk_src = {"w1": w1t, "w2": w2t, "h1": h1t,
                                   "h2": h2t, "w3": w3t}[w][
                                       ki * 128:(ki + 1) * 128, mb:mb + 1024]
                    else:
                        blk_src = src3[l, ki * 128:(ki + 1) * 128, mb:mb + 1024]
                    P.emit("pool", "dma_start", CAST[slot][:, 0:1024], blk_src,
                           dma=("pl", slot), writes=[("CAST", slot)])
                    nmi = min(8, M // 128 - mb // 128)
                    out = bass.AP(cache,
                                  stack_off(w, l, mb // 128) + ki * 128,
                                  [[CACHE_PP, 128], [pitch, nmi], [1, 128]])
                    src = CAST[slot].ap()[:, 0:1024].rearrange(
                        "p (m e) -> p m e", m=8)[:, 0:nmi, :]
                    P.emit("sp", "dma_start", out, src, dma=("ps", slot),
                           reads=[("CAST", slot)])

        one("w1", None, D, DFF, None)
        one("w2", None, D, DFF, None)
        one("h1", None, DFF, DFF, None)
        one("h2", None, DFF, DFF, None)
        one("w3", None, DFF, D, None)
        for l in range(LAY):
            one("wq", l, D, D, wq)
            one("wk", l, D, D, wk)
            one("wv", l, D, D, wv)
            one("wo", l, D, D, wo)
            one("wff1", l, D, DFF, wff1)
            one("wff2", l, DFF, D, wff2)

    prepass()

    P.begin_loop(NT)

    seq = (list(range(0, 176)) + list(range(176, 320)) + list(range(0, 176)))
    st = {"spos": 0, "issued": -1, "slot_chunk": [None] * NSLOT}

    def ensure(tpos):
        tgt = min(tpos + NSLOT - 2, len(seq) - 1)
        while st["issued"] < tgt:
            c = st["issued"] + 1
            slot = c % NSLOT
            cid = seq[c]
            src = bass.AP(cache, cid * CH, [[CACHE_PP, 128], [1, CH]])
            P.emit("sp", "dma_start", W[slot][:, :], src, dma=("rg", slot),
                   writes=[("W", slot)])
            st["issued"] = c
            st["slot_chunk"][slot] = c

    def next_stack(w, l, mi, K):
        assert st["spos"] % 1024 == 0
        exp = stack_off(w, l, mi)
        pieces = []
        for ki in range(K // 128):
            pos = st["spos"] + ki * 128
            tpos = pos // CH
            ensure(tpos)
            slot = tpos % NSLOT
            off = pos % CH
            assert seq[tpos] * CH + off == exp + ki * 128, (w, l, mi, ki)
            assert st["slot_chunk"][slot] == tpos,                 ("slot retargeted before consumption", w, l, mi, ki)
            pieces.append((W[slot], ("W", slot), off))
        st["spos"] += K
        return pieces

    rot = [0]

    def bank():
        b = rot[0] % 4
        rot[0] += 1
        return b

    def x_ap(fc, out=False):
        tns = yt if out else xt
        def mk(ctx, fc=fc, tns=tns):
            return bass.AP(tns, ctx.i * 512 + fc * 128 * S, [[S, 128], [1, 512]])
        return mk

    XSL = [XT[:, fc * 512:(fc + 1) * 512] for fc in range(FC)]
    RSL = [RS[:, fc * 512:(fc + 1) * 512] for fc in range(FC)]
    HSL = [HS[:, fc * 512:(fc + 1) * 512] for fc in range(FC)]
    RBL = [RB[:, fc * 512:(fc + 1) * 512] for fc in range(FC)]
    HBL = [HB[:, fc * 512:(fc + 1) * 512] for fc in range(FC)]
    AL = [BIGA[:, m * 512:(m + 1) * 512] for m in range(MCF)]
    BL = [BIGB[:, m * 512:(m + 1) * 512] for m in range(MCF)]

    def matmul_stack(w, l, mi, K, rhs_slices, rhs_keys, bi, extra_reads=()):
        pieces = next_stack(w, l, mi, K)
        n = len(pieces)
        for ki, (wt, wkey, off) in enumerate(pieces):
            P.emit("pe", "matmul", pb[bi][:, :], wt[:, off:off + 128],
                   rhs_slices[ki], start=(ki == 0), stop=(ki == n - 1),
                   reads=[wkey, rhs_keys[ki]] + list(extra_reads),
                   writes=[("pb", bi)])

    def emit_ln(src_slices, src_keys, gamma=None, beta=None,
                out_f32=None, out_f32_keys=None, out_bf=None, out_bf_keys=None):
        for fc in range(FC):
            r = fc % 2
            P.emit("act", "copy", SQA[:, r * 512:(r + 1) * 512], src_slices[fc],
                   reads=[src_keys[fc]], writes=[("SQA", r)])
            P.emit("dve", "tensor_mul", SQA[:, (2 + r) * 512:(3 + r) * 512],
                   SQA[:, r * 512:(r + 1) * 512], SQA[:, r * 512:(r + 1) * 512],
                   reads=[("SQA", r)], writes=[("SQB", r)])
            P.emit("pe", "matmul", pb[4][0:1, 0:512], ONESCOL,
                   SQA[:, r * 512:(r + 1) * 512], start=(fc == 0), stop=(fc == 7),
                   reads=[("SQA", r), ("CB",)], writes=[("pb", 4)])
            P.emit("pe", "matmul", pb[5][0:1, 0:512], ONESCOL,
                   SQA[:, (2 + r) * 512:(3 + r) * 512], start=(fc == 0),
                   stop=(fc == 7), reads=[("SQB", r), ("CB",)], writes=[("pb", 5)])
        mu = ROWS[0:1, 0:512]
        ms = ROWS[0:1, 512:1024]
        tmp = ROWS[0:1, 1024:1536]
        P.emit("act", "activation", mu, pb[4][0:1, 0:512], AF.Identity, scale=1.0 / D,
               reads=[("pb", 4)], writes=[("mu",)])
        P.emit("act", "activation", ms, pb[5][0:1, 0:512], AF.Identity, scale=1.0 / D,
               reads=[("pb", 5)], writes=[("ms",)])
        P.emit("dve", "tensor_mul", tmp, mu, mu, reads=[("mu",)], writes=[("tmp",)])
        P.emit("dve", "tensor_sub", ms, ms, tmp, reads=[("ms",), ("tmp",)],
               writes=[("ms",)])
        P.emit("act", "activation", tmp, ms, AF.Ln, bias=LN_EPS,
               reads=[("ms",)], writes=[("tmp",)])
        P.emit("act", "activation", tmp, tmp, AF.Exp, scale=-0.5,
               reads=[("tmp",)], writes=[("tmp",)])
        P.emit("pe", "matmul", pb[4][:, :], ONES_F, mu, start=True, stop=True,
               reads=[("mu",), ("CF",)], writes=[("pb", 4)])
        P.emit("pe", "matmul", pb[5][:, :], ONES_F, tmp, start=True, stop=True,
               reads=[("tmp",), ("CF",)], writes=[("pb", 5)])
        P.emit("act", "copy", RSB[:, :], pb[5][:, :], reads=[("pb", 5)],
               writes=[("RSB",)])
        for fc in range(FC):
            r = fc % 2
            t32 = T32[:, r * 512:(r + 1) * 512]
            P.emit("dve", "tensor_sub", t32, src_slices[fc], pb[4][:, :],
                   reads=[src_keys[fc], ("pb", 4)], writes=[("T32", r)])
            if gamma is None:
                if out_f32 is not None:
                    P.emit("dve", "tensor_mul", out_f32[fc], t32, RSB[:, :],
                           reads=[("T32", r), ("RSB",)], writes=[out_f32_keys[fc]])
                    P.emit("act", "copy", out_bf[fc], out_f32[fc],
                           reads=[out_f32_keys[fc]], writes=[out_bf_keys[fc]])
                else:
                    P.emit("dve", "tensor_mul", out_bf[fc], t32, RSB[:, :],
                           reads=[("T32", r), ("RSB",)], writes=[out_bf_keys[fc]])
            else:
                P.emit("dve", "tensor_mul", t32, t32, RSB[:, :],
                       reads=[("T32", r), ("RSB",)], writes=[("T32", r)])
                P.emit("act", "activation", out_f32[fc], t32, AF.Identity,
                       scale=sv_ap(gamma[0], gamma[1], fc),
                       bias=sv_ap(beta[0], beta[1], fc),
                       reads=[("T32", r), ("SV",)], writes=[out_f32_keys[fc]])
                P.emit("act", "copy", out_bf[fc], out_f32[fc],
                       reads=[out_f32_keys[fc]], writes=[out_bf_keys[fc]])

    def emit_ffn():
        for mi in range(MCF):
            b1 = bank()
            matmul_stack("w1", None, mi, D, RBL, [("RB", k) for k in range(FC)], b1)
            b2 = bank()
            matmul_stack("w2", None, mi, D, RBL, [("RB", k) for k in range(FC)], b2)
            P.emit("act", "activation", AL[mi], pb[b1][:, :], AF.Silu,
                   scale=sv_ap("s1", None, mi), reads=[("pb", b1), ("SV",)],
                   writes=[("A", mi)])
            P.emit("act", "activation", BL[mi], pb[b2][:, :], AF.Identity,
                   scale=sv_ap("s2", None, mi), reads=[("pb", b2), ("SV",)],
                   writes=[("B", mi)])
            P.emit("dve", "tensor_mul", AL[mi], AL[mi], BL[mi],
                   reads=[("A", mi), ("B", mi)], writes=[("A", mi)])
        for mi in range(MCF):
            bi = bank()
            matmul_stack("h1", None, mi, DFF, AL, [("A", k) for k in range(MCF)], bi)
            P.emit("act", "activation", BL[mi], pb[bi][:, :], AF.Gelu,
                   scale=sv_ap("sh1", None, mi), reads=[("pb", bi), ("SV",)],
                   writes=[("B", mi)])
        for mi in range(MCF):
            bi = bank()
            matmul_stack("h2", None, mi, DFF, BL, [("B", k) for k in range(MCF)], bi)
            P.emit("act", "activation", AL[mi], pb[bi][:, :], AF.Gelu,
                   scale=sv_ap("sh2", None, mi), reads=[("pb", bi), ("SV",)],
                   writes=[("A", mi)])
        for mi in range(FC):
            bi = bank()
            matmul_stack("w3", None, mi, DFF, AL, [("A", k) for k in range(MCF)], bi)
            r = mi % 2
            t32 = T32[:, r * 512:(r + 1) * 512]
            P.emit("act", "activation", t32, pb[bi][:, :], AF.Identity,
                   scale=dv_ap("s3h", None, mi), reads=[("pb", bi), ("DV",)],
                   writes=[("T32", r)])
            P.emit("dve", "tensor_add", XSL[mi], XSL[mi], t32,
                   reads=[("XT", mi), ("T32", r)], writes=[("XT", mi)])

    def emit_favor(l, base, side):
        if side == "k":
            for fc in range(FC):
                r = fc % 4
                P.emit("act", "activation", SQA[:, r * 512:(r + 1) * 512],
                       BL[base + fc], AF.Square,
                       reads=[("B", base + fc)], writes=[("SQA", r)])
                if fc >= 3:
                    f2 = fc - 3
                    r2 = f2 % 4
                    P.emit("pe", "matmul", pb[4 + f2 % 2][0:2, 0:512], BDONES,
                           SQA[:, r2 * 512:(r2 + 1) * 512], start=True, stop=True,
                           reads=[("SQA", r2), ("CB",)], writes=[("pb", 4 + f2 % 2)])
                    P.emit("act", "activation",
                           WROW[0:2, f2 * 512:(f2 + 1) * 512],
                           pb[4 + f2 % 2][0:2, 0:512], AF.Exp, scale=-0.5, bias=LNC,
                           reads=[("pb", 4 + f2 % 2)], writes=[("WR", f2)])
            for f2 in range(FC - 3, FC):
                r2 = f2 % 4
                P.emit("pe", "matmul", pb[4 + f2 % 2][0:2, 0:512], BDONES,
                       SQA[:, r2 * 512:(r2 + 1) * 512], start=True, stop=True,
                       reads=[("SQA", r2), ("CB",)], writes=[("pb", 4 + f2 % 2)])
                P.emit("act", "activation", WROW[0:2, f2 * 512:(f2 + 1) * 512],
                       pb[4 + f2 % 2][0:2, 0:512], AF.Exp, scale=-0.5, bias=LNC,
                       reads=[("pb", 4 + f2 % 2)], writes=[("WR", f2)])
        for fc in range(FC):
            bu = fc % 4
            P.emit("pe", "matmul", pb[bu][:, :], OM2[:, l * 128:(l + 1) * 128],
                   BL[base + fc], start=True, stop=True,
                   reads=[("B", base + fc), ("OM2",)], writes=[("pb", bu)])
            if fc >= 3:
                f2 = fc - 3
                P.emit("act", "activation", BL[base + f2], pb[f2 % 4][:, :],
                       AF.Exp, reads=[("pb", f2 % 4)], writes=[("B", base + f2)])
        for f2 in range(FC - 3, FC):
            P.emit("act", "activation", BL[base + f2], pb[f2 % 4][:, :],
                   AF.Exp, reads=[("pb", f2 % 4)], writes=[("B", base + f2)])

    def emit_la_chunk(l, tc):
        ts = tc * 128
        lb = l * FC * 65
        P.emit("act", "copy", S_bf[:, :], S_sb[:, lb:lb + FC * 65],
               reads=[("S", l)], writes=[("Sbf",)])
        for fc in range(FC):
            p = fc % 2
            kbuf = BL[8 + fc]
            vbuf = BL[16 + fc]
            P.emit("pe", "transpose", pt[p][:, 384:386],
                   WROW[0:2, fc * 512 + ts: fc * 512 + ts + 128], IDENT[0:2, 0:2],
                   reads=[("WR", fc), ("CB",)], writes=[("pt", p)])
            P.emit("pe", "transpose", pt[p][:, 0:128], kbuf[:, ts:ts + 128],
                   IDENT, reads=[("B", 8 + fc), ("CB",)], writes=[("pt", p)])
            P.emit("pe", "transpose", pt[p][:, 128:256], vbuf[:, ts:ts + 128],
                   IDENT, reads=[("B", 16 + fc), ("CB",)], writes=[("pt", p)])
            P.emit("dve", "tensor_copy", WK[:, fc * 2:(fc + 1) * 2],
                   pt[p][:, 384:386], reads=[("pt", p)], writes=[("WK", fc)])
            P.emit("dve", "tensor_scalar_mul", KT[:, fc * 128:fc * 128 + 64],
                   pt[p][:, 0:64], WK[:, fc * 2:fc * 2 + 1],
                   reads=[("pt", p), ("WK", fc)], writes=[("KT", fc)])
            P.emit("dve", "tensor_scalar_mul", KT[:, fc * 128 + 64:fc * 128 + 128],
                   pt[p][:, 64:128], WK[:, fc * 2 + 1:fc * 2 + 2],
                   reads=[("pt", p), ("WK", fc)], writes=[("KT", fc)])
            vt_dst = VT.ap()[:, fc * 130: fc * 130 + 130].rearrange(
                "p (s c) -> p s c", s=2, c=65)[:, :, 0:64]
            P.emit("dve", "tensor_copy", vt_dst,
                   pt[p][:, 128:256].rearrange("p (s c) -> p s c", s=2, c=64),
                   reads=[("pt", p)], writes=[("VT", fc)])
        for fc in range(FC):
            p = fc % 2
            bAe, bAo = 2 * p, 2 * p + 1
            qb = BL[0 + fc]
            qk = ("B", 0 + fc)
            kb = BL[8 + fc]
            kk = ("B", 8 + fc)
            P.emit("pe", "matmul", pb[bAe][:, 0:128], kb[0:64, ts:ts + 128],
                   qb[0:64, ts:ts + 128], start=True, stop=True,
                   reads=[kk, qk], writes=[("pb", bAe)])
            P.emit("pe", "matmul", pb[bAo][:, 0:128], kb[64:128, ts:ts + 128],
                   qb[64:128, ts:ts + 128], start=True, stop=True,
                   reads=[kk, qk], writes=[("pb", bAo)])
            P.emit("dve", "scalar_tensor_tensor", ATS[:, (2 * fc) * 128:(2 * fc + 1) * 128],
                   pb[bAe][:, 0:128], WK[:, fc * 2:fc * 2 + 1], TRIU,
                   ALU.mult, ALU.mult,
                   reads=[("pb", bAe), ("WK", fc), ("CF",)], writes=[("ATS", 2 * fc)])
            P.emit("dve", "scalar_tensor_tensor", ATS[:, (2 * fc + 1) * 128:(2 * fc + 2) * 128],
                   pb[bAo][:, 0:128], WK[:, fc * 2 + 1:fc * 2 + 2], TRIU,
                   ALU.mult, ALU.mult,
                   reads=[("pb", bAo), ("WK", fc), ("CF",)], writes=[("ATS", 2 * fc + 1)])
        for fc in range(FC):
            p = fc % 2
            bO = 4 + p
            qb = BL[0 + fc]
            qk = ("B", 0 + fc)
            P.emit("pe", "matmul", pb[bO][:, 0:65],
                   qb[0:64, ts:ts + 128], S_bf[0:64, fc * 65:(fc + 1) * 65],
                   start=True, stop=False, reads=[qk, ("Sbf",)],
                   writes=[("pb", bO)])
            P.emit("pe", "matmul", pb[bO][:, 0:65],
                   ATS[:, (2 * fc) * 128:(2 * fc + 1) * 128],
                   VT[:, fc * 130: fc * 130 + 65],
                   start=False, stop=True,
                   reads=[("ATS", 2 * fc), ("VT", fc), ("VTones", fc)],
                   writes=[("pb", bO)])
            P.emit("pe", "matmul", pb[bO][:, 65:130],
                   qb[64:128, ts:ts + 128], S_bf[64:128, fc * 65:(fc + 1) * 65],
                   start=True, stop=False, reads=[qk, ("Sbf",)],
                   writes=[("pb", bO)])
            P.emit("pe", "matmul", pb[bO][:, 65:130],
                   ATS[:, (2 * fc + 1) * 128:(2 * fc + 2) * 128],
                   VT[:, fc * 130 + 65: fc * 130 + 130],
                   start=False, stop=True,
                   reads=[("ATS", 2 * fc + 1), ("VT", fc), ("VTones2", fc)],
                   writes=[("pb", bO)])
            dens = bass.AP(pbt[bO], 64, [[512, 128], [65, 2]])
            P.emit("act", "activation", RD2[:, fc * 4: fc * 4 + 2], dens,
                   AF.Identity, bias=ATTN_EPS, reads=[("pb", bO)],
                   writes=[("RDr", fc)])
            P.emit("dve", "reciprocal", RD2[:, fc * 4 + 2: fc * 4 + 4],
                   RD2[:, fc * 4: fc * 4 + 2], reads=[("RDr", fc)],
                   writes=[("RD", fc)])
            P.emit("act", "activation", ATM[:, fc * 128: fc * 128 + 64],
                   pb[bO][:, 0:64], AF.Identity,
                   scale=RD2[:, fc * 4 + 2: fc * 4 + 3],
                   reads=[("pb", bO), ("RD", fc)], writes=[("ATM", fc)])
            P.emit("act", "activation", ATM[:, fc * 128 + 64: fc * 128 + 128],
                   pb[bO][:, 65:129], AF.Identity,
                   scale=RD2[:, fc * 4 + 3: fc * 4 + 4],
                   reads=[("pb", bO), ("RD", fc)], writes=[("ATM", fc)])
        for fc in range(FC):
            p = fc % 2
            bAo = 2 * p + 1
            P.emit("pe", "transpose", pt[p][:, 256:384],
                   ATM[:, fc * 128:(fc + 1) * 128], IDENT,
                   reads=[("ATM", fc), ("CB",)], writes=[("pt", p)])
            P.emit("dve", "tensor_copy", BL[24 + fc][:, ts:ts + 128],
                   pt[p][:, 256:384], reads=[("pt", p)], writes=[("B", 24 + fc)])
            P.emit("pe", "matmul", pb[bAo][0:64, 128:193],
                   KT[:, fc * 128: fc * 128 + 64],
                   VT[:, fc * 130: fc * 130 + 65], start=True, stop=True,
                   reads=[("KT", fc), ("VT", fc), ("VTones", fc)],
                   writes=[("pb", bAo)])
            P.emit("pe", "matmul", pb[bAo][64:128, 128:193],
                   KT[:, fc * 128 + 64: fc * 128 + 128],
                   VT[:, fc * 130 + 65: fc * 130 + 130], start=True, stop=True,
                   tile_position=(0, 64),
                   reads=[("KT", fc), ("VT", fc), ("VTones2", fc)],
                   writes=[("pb", bAo)])
            P.emit("dve", "tensor_add", S_sb[:, lb + fc * 65:lb + (fc + 1) * 65],
                   S_sb[:, lb + fc * 65:lb + (fc + 1) * 65], pb[bAo][:, 128:193],
                   reads=[("S", l), ("pb", bAo)], writes=[("S", l)])

    def emit_attn(l):
        for (wname, base, scale, bias_nm, bias_dv) in (
                ("wq", 0, DHS, "bqs", True), ("wk", 8, DHS, "bks", True),
                ("wv", 16, 1.0, "bv", False)):
            for mi in range(FC):
                bi = bank()
                matmul_stack(wname, l, mi, D, RBL,
                             [("RB", k) for k in range(FC)], bi)
                b_ap = (dv_ap(bias_nm, l, mi) if bias_dv
                        else sv_ap(bias_nm, l, mi))
                P.emit("act", "activation", BL[base + mi], pb[bi][:, :],
                       AF.Identity, scale=scale, bias=b_ap,
                       reads=[("pb", bi), ("DV",) if bias_dv else ("SV",)],
                       writes=[("B", base + mi)])
        emit_favor(l, 0, "q")
        emit_favor(l, 8, "k")
        for tc in range(TC):
            emit_la_chunk(l, tc)
        for mi in range(FC):
            bi = bank()
            matmul_stack("wo", l, mi, D, [BL[24 + k] for k in range(FC)],
                         [("B", 24 + k) for k in range(FC)], bi)
            r = mi % 2
            t32 = T32[:, r * 512:(r + 1) * 512]
            P.emit("act", "activation", t32, pb[bi][:, :], AF.Identity,
                   bias=sv_ap("bo", l, mi), reads=[("pb", bi), ("SV",)],
                   writes=[("T32", r)])
            P.emit("dve", "tensor_add", HSL[mi], RSL[mi], t32,
                   reads=[("RS", mi), ("T32", r)], writes=[("HS", mi)])
        if DEV_STAGE == f"wo{l}":
            return
        emit_ln(HSL, [("HS", k) for k in range(FC)], gamma=("g1", l),
                beta=("b1", l), out_f32=HSL,
                out_f32_keys=[("HS", k) for k in range(FC)],
                out_bf=HBL, out_bf_keys=[("HB", k) for k in range(FC)])
        if DEV_STAGE == f"ln1_{l}":
            return
        for mi in range(MCF):
            bi = bank()
            matmul_stack("wff1", l, mi, D, HBL, [("HB", k) for k in range(FC)], bi)
            P.emit("act", "activation", AL[mi], pb[bi][:, :], AF.Relu,
                   bias=sv_ap("bf1", l, mi), reads=[("pb", bi), ("SV",)],
                   writes=[("A", mi)])
        for mi in range(FC):
            bi = bank()
            matmul_stack("wff2", l, mi, DFF, AL, [("A", k) for k in range(MCF)], bi)
            r = mi % 2
            t32 = T32[:, r * 512:(r + 1) * 512]
            P.emit("act", "activation", t32, pb[bi][:, :], AF.Identity,
                   bias=sv_ap("bf2", l, mi), reads=[("pb", bi), ("SV",)],
                   writes=[("T32", r)])
            P.emit("dve", "tensor_add", HSL[mi], HSL[mi], t32,
                   reads=[("HS", mi), ("T32", r)], writes=[("HS", mi)])
        if DEV_STAGE == f"y{l}":
            return
        emit_ln(HSL, [("HS", k) for k in range(FC)], gamma=("g2", l),
                beta=("b2", l), out_f32=RSL,
                out_f32_keys=[("RS", k) for k in range(FC)],
                out_bf=RBL, out_bf_keys=[("RB", k) for k in range(FC)])

    for fc in range(FC):
        P.emit("pool", "dma_start", XSL[fc], x_ap(fc), dma=("xl", fc),
               writes=[("XT", fc)])

    def store(slices, keys):
        for fc in range(FC):
            P.emit("pool", "dma_start", x_ap(fc, out=True), slices[fc],
                   dma="yo", reads=[keys[fc]], writes=[("yo", fc)])

    done = False
    while True:
        emit_ln(XSL, [("XT", k) for k in range(FC)], out_bf=RBL,
                out_bf_keys=[("RB", k) for k in range(FC)])
        emit_ffn()
        if DEV_STAGE == "ffn1":
            store(XSL, [("XT", k) for k in range(FC)]); break
        emit_ln(XSL, [("XT", k) for k in range(FC)], out_f32=RSL,
                out_f32_keys=[("RS", k) for k in range(FC)], out_bf=RBL,
                out_bf_keys=[("RB", k) for k in range(FC)])
        if DEV_STAGE == "preln":
            store(RSL, [("RS", k) for k in range(FC)]); break
        brk = False
        for l in range(LAY):
            emit_attn(l)
            if DEV_STAGE in (f"l{l}", f"wo{l}", f"ln1_{l}", f"y{l}"):
                brk = True
                break
        if brk:
            if DEV_STAGE.startswith("l") and not DEV_STAGE.startswith("ln1") and not DEV_STAGE.startswith("y"):
                store(RSL, [("RS", k) for k in range(FC)])
            else:
                store(HSL, [("HS", k) for k in range(FC)])
            break
        for fc in range(FC):
            P.emit("dve", "tensor_add", XSL[fc], XSL[fc], RSL[fc],
                   reads=[("XT", fc), ("RS", fc)], writes=[("XT", fc)])
        if DEV_STAGE == "xadd":
            store(XSL, [("XT", k) for k in range(FC)]); break
        emit_ln(XSL, [("XT", k) for k in range(FC)], out_bf=RBL,
                out_bf_keys=[("RB", k) for k in range(FC)])
        emit_ffn()
        store(XSL, [("XT", k) for k in range(FC)])
        done = True
        break

    if done:
        assert st["spos"] == len(seq) * CH, st["spos"]
    P.end_loop()
    P.finalize(nc, fences={"pool": FP[:, :], "act": FA[0:1, 0:1],
                           "dve": FD[0:1, 0:1]})
    n_ops = sum(len(P.ops[e]) for e in ENGINES)
    return nc, n_ops, P.n_waits


def host_prepare(inputs):
    f = np.ascontiguousarray

    def svblock(v):
        v = np.asarray(v)
        return f(v.reshape(-1, 128).T)

    sv = np.zeros((128, SV_N), np.float32)
    for l in range(LAY):
        for nm, src in (("bq", inputs["L0_attn_bq"][l]),
                        ("bk", inputs["L0_attn_bk"][l]),
                        ("bv", inputs["L0_attn_bv"][l]),
                        ("bo", inputs["L0_attn_bo"][l]),
                        ("g1", inputs["L0_attn_ln1_g"][l]),
                        ("b1", inputs["L0_attn_ln1_b"][l]),
                        ("g2", inputs["L0_attn_ln2_g"][l]),
                        ("b2", inputs["L0_attn_ln2_b"][l]),
                        ("bf1", inputs["L0_attn_bf1"][l]),
                        ("bf2", inputs["L0_attn_bf2"][l])):
            blk = svblock(src)
            sv[:, SVC[(nm, l)]:SVC[(nm, l)] + blk.shape[1]] = blk
    for nm, src in (("s1", inputs["L0_ffn_w1_s"]), ("s2", inputs["L0_ffn_w2_s"]),
                    ("sh1", inputs["L0_ffn_h1_s"]), ("sh2", inputs["L0_ffn_h2_s"]),
                    ("s3", inputs["L0_ffn_w3_s"])):
        blk = svblock(src)
        sv[:, SVC[(nm, None)]:SVC[(nm, None)] + blk.shape[1]] = blk

    obd = np.zeros((128, 64 * LAY), np.float32)
    om = np.asarray(inputs["L0_attn_omega"])
    for l in range(LAY):
        obd[0:64, l * 64:l * 64 + 32] = om[l]
        obd[64:128, l * 64 + 32:l * 64 + 64] = om[l]

    cf = np.zeros((128, 272), np.float32)
    cf[:, 0:128] = np.triu(np.ones((128, 128), np.float32))
    cf[0, 128:192] = 1.0
    cf[1, 192:256] = 1.0
    cf[:, 256:256 + len(CONSTVALS)] = np.asarray(CONSTVALS, np.float32)[None, :]
    cb = np.zeros((128, 131), np.float32)
    cb[:, 0:128] = np.eye(128, dtype=np.float32)
    cb[:, 128] = 1.0
    cb[0:64, 129] = 1.0
    cb[64:128, 130] = 1.0
    cb = cb.astype(ml_dtypes.bfloat16)

    shared = dict(
        w1t=f(np.asarray(inputs["L0_ffn_w1_w"]).T),
        w2t=f(np.asarray(inputs["L0_ffn_w2_w"]).T),
        h1t=f(np.asarray(inputs["L0_ffn_h1_w"]).T),
        h2t=f(np.asarray(inputs["L0_ffn_h2_w"]).T),
        w3t=f(np.asarray(inputs["L0_ffn_w3_w"]).T),
        wq=f(np.asarray(inputs["L0_attn_Wq"])),
        wk=f(np.asarray(inputs["L0_attn_Wk"])),
        wv=f(np.asarray(inputs["L0_attn_Wv"])),
        wo=f(np.asarray(inputs["L0_attn_Wo"])),
        wff1=f(np.asarray(inputs["L0_attn_W1"])),
        wff2=f(np.asarray(inputs["L0_attn_W2"])),
        svin=sv, obdin=obd, cfin=cf, cbin=cb,
    )
    x = np.asarray(inputs["L0_attn_x"])
    xts = [f(x[c].T).reshape(-1) for c in range(B)]
    return shared, xts


_CACHE = {}


def kernel(**inputs):
    shared, xts = host_prepare(inputs)
    if "nc" not in _CACHE:
        nc, n_ops, n_waits = build_nc(NTILE)
        _CACHE["nc"] = nc
    nc = _CACHE["nc"]
    in_maps = [dict(shared, xt=xts[c]) for c in range(B)]
    res = run_bass_kernel_spmd(nc, in_maps, core_ids=list(range(B)))
    out = np.empty((B, S, D), np.float32)
    for c in range(B):
        out[c] = res.results[c]["yt"].reshape(D, S).T
    return out


if __name__ == "__main__":
    nc, n_ops, n_waits = build_nc(1)
    print(f"ops={n_ops} waits={n_waits}")


# version 07
# speedup vs baseline: 1.1055x; 1.0016x over previous
import numpy as np
import ml_dtypes
import concourse.bass as bass
import concourse.mybir as mybir
from concourse.bass_utils import run_bass_kernel_spmd

F32 = mybir.dt.float32
BF16 = mybir.dt.bfloat16
AF = mybir.ActivationFunctionType
ALU = mybir.AluOpType

D = 1024
DFF = 4096
S = 8192
B = 8
T = 512
NTILE = S // T
FC = D // 128
MCF = DFF // 128
TC = T // 128
NH = 16
DH = 64
LAY = 3
DHS = float(DH) ** -0.25
LNC = float(np.log(2 * 32) * -0.5)
LN_EPS = 1e-5
ATTN_EPS = 1e-6
NSLOT = 4
CH = 2048
CONSTVALS = (1.0 / D, LN_EPS, -0.5, 0.5, -LNC, LNC, ATTN_EPS, DHS, -1.0, 1.0,
             0.0, 2.0, 0.25)

ENGINES = ("pe", "act", "dve", "pool", "sp")
PARANOID = frozenset(("act", "dve", "pool"))


class Ctx:
    def __init__(self):
        self.i = None


class Prog:
    def __init__(self):
        self.ops = {e: [] for e in ENGINES}
        self.waits = {e: [] for e in ENGINES}
        self.opclock = {e: [] for e in ENGINES}
        self.dmaclock = {}
        self.clock = {e: {} for e in ENGINES}
        self.group_count = {}
        self.group_engine = {}
        self.need_inc = {e: set() for e in ENGINES}
        self.res = {}
        self.n_waits = 0
        self.seg = 0
        self.seg_start = {e: [0] for e in ENGINES}
        self.loop = None
        self.group_final = []
        self._seen_groups = set()

    def _need(self, eng, clk, needed, p):
        if p is None:
            return
        if p[0] == "e":
            if p[1] == eng and eng not in PARANOID:
                return
            sk, v = ("e", p[1]), p[2]
        else:
            sk, v = ("d", p[1]), p[2]
        if clk.get(sk, -1) >= v:
            return
        if needed.get(sk, -1) < v:
            needed[sk] = v

    def emit(self, eng, fn, *args, reads=(), writes=(), dma=None, commit=True,
             force_inc=False, **kwargs):
        pos = len(self.ops[eng])
        clk = self.clock[eng]
        needed = {}
        for k in reads:
            st = self.res.get(k)
            if st is not None:
                self._need(eng, clk, needed, st[0])
        for k in writes:
            st = self.res.get(k)
            if st is not None:
                self._need(eng, clk, needed, st[0])
                for r in st[1]:
                    self._need(eng, clk, needed, r)
        wl = []
        for sk, v in sorted(needed.items(), key=repr):
            if sk[0] == "e":
                self.need_inc[sk[1]].add((sk[1], v))
                snap = self.opclock[sk[1]][v]
            else:
                snap = self.dmaclock.get((sk[1], v), {})
            for k2, v2 in snap.items():
                if clk.get(k2, -1) < v2:
                    clk[k2] = v2
            if clk.get(sk, -1) < v:
                clk[sk] = v
            wl.append((sk, v))
        self.n_waits += len(wl)
        self.waits[eng].append(wl)
        self.ops[eng].append((fn, args, kwargs, dma, self.seg))
        if force_inc and dma is None:
            self.need_inc[eng].add((eng, pos))
        if dma is not None:
            assert self.group_engine.setdefault(dma, eng) == eng
            cnt = self.group_count.get(dma, 0) + 1
            self.group_count[dma] = cnt
            me = ("d", dma, cnt)
            self.dmaclock[(dma, cnt)] = dict(clk)
            self.opclock[eng].append(dict(clk))
        else:
            me = ("e", eng, pos)
            snap = dict(clk)
            snap[("e", eng)] = pos
            self.opclock[eng].append(snap)
        if commit:
            self._register(me, reads, writes)
        return me

    def commit_dma(self, group, reads=(), writes=()):
        cnt = self.group_count[group]
        self._register(("d", group, cnt), reads, writes)

    def _register(self, me, reads, writes):
        for k in writes:
            self.res[k] = [me, []]
        for k in reads:
            st = self.res.get(k)
            if st is None:
                st = [None, []]
                self.res[k] = st
            st[1].append(me)

    def _cut_segment(self, marker):
        for g in self.group_count:
            assert g not in self._seen_groups, f"group {g} in two segments"
        self._seen_groups |= set(self.group_count)
        self.group_final.append(dict(self.group_count))
        for e in ENGINES:
            self.ops[e].append((marker, None, None, None, self.seg))
            self.waits[e].append([])
            self.opclock[e].append({})
        self.seg += 1
        for e in ENGINES:
            self.seg_start[e].append(len(self.ops[e]))
            self.clock[e] = {}
        self.group_count = {}
        self.dmaclock = {}
        self.res = {k: [None, []] for k in self.res}

    def begin_loop(self, n_iters):
        assert self.loop is None
        self._cut_segment("@LOOP")
        self.loop = n_iters

    def end_loop(self):
        self._cut_segment("@ENDLOOP")

    def finalize(self, nc, fences=None):
        if fences is None:
            fences = {}
        self._cut_segment("@END")
        sems = {}
        for e in ENGINES:
            sems[("e", e)] = nc.alloc_semaphore(f"sem_{e}")
        for g in self.group_engine:
            nm = "_".join(map(str, g)) if isinstance(g, tuple) else str(g)
            sems[("d", g)] = nc.alloc_semaphore(f"dg_{nm}")
        bar = nc.alloc_semaphore("bar")
        bar2 = nc.alloc_semaphore("bar2")
        clear_ids = [sems[("e", e)] for e in ENGINES]
        body_per_iter = (self.group_final[1]
                         if self.loop is not None and len(self.group_final) > 1
                         else {})

        def seg_of(e, pos):
            lo = 0
            for si, st in enumerate(self.seg_start[e]):
                if st <= pos:
                    lo = si
            return lo

        inc_rank = {e: {} for e in ENGINES}
        inc_totals = {e: {} for e in ENGINES}
        for e in ENGINES:
            for p in sorted(p for (_, p) in self.need_inc[e]):
                s = seg_of(e, p)
                inc_totals[e][s] = inc_totals[e].get(s, 0) + 1
                inc_rank[e][p] = inc_totals[e][s]
        markedset = {e: set(p for (_, p) in self.need_inc[e]) for e in ENGINES}
        n_eng = len(ENGINES)
        loop_n = self.loop
        engmap = dict(pe="tensor", act="scalar", dve="vector", pool="gpsimd",
                      sp="sync")
        enumap = dict(pe=mybir.EngineType.PE, act=mybir.EngineType.Activation,
                      dve=mybir.EngineType.DVE, pool=mybir.EngineType.Pool,
                      sp=mybir.EngineType.SP)

        dyn_pairs = {e: [] for e in ENGINES}
        if loop_n is not None:
            for e in ENGINES:
                seen = dict()
                lo = self.seg_start[e][1]
                hi = self.seg_start[e][2] - 1
                for i in range(lo, hi):
                    for sk, v in self.waits[e][i]:
                        if sk[0] == "d" and sk[1] not in seen:
                            seen[sk[1]] = True
                for g, cnt in self.group_final[1].items():
                    if self.group_engine[g] == e and cnt and g not in seen:
                        seen[g] = True
                dyn_pairs[e] = list(seen)
        waitregs = {}
        for e in ENGINES:
            for g in dyn_pairs[e]:
                rh = nc.alloc_registers(f"wr_{e}_{g}", engines=[enumap[e]])
                waitregs[(e, g)] = list(rh)[0]
        cur_target = {}

        def dyn_wait(engine, e, g, target):
            reg = waitregs[(e, g)]
            cur = cur_target.get((e, g), 0)
            delta = target - cur
            assert delta >= 0, (e, g, target, cur)
            if delta:
                engine.reg_alu(reg, reg, delta, mybir.AluOpType.add)
            engine.wait_ge(sems[("d", g)], reg)
            cur_target[(e, g)] = target

        def emit_barrier(engine, e, ctx, seg_idx, in_loop):
            gf = self.group_final[seg_idx]
            for g, cnt in sorted(gf.items(), key=repr):
                if self.group_engine[g] == e and cnt:
                    if in_loop:
                        dyn_wait(engine, e, g, 16 * cnt)
                    elif not (isinstance(g, tuple) and g[0] in ("pl", "ps")):
                        engine.wait_ge(sems[("d", g)], 16 * cnt)
            if in_loop:
                per_map = self.group_final[1]
                for g in dyn_pairs[e]:
                    per = per_map.get(g, 0)
                    tgt = 16 * per
                    cur = cur_target.get((e, g), 0)
                    if tgt != cur:
                        engine.reg_alu(waitregs[(e, g)], waitregs[(e, g)],
                                       tgt - cur, mybir.AluOpType.add)
                    cur_target[(e, g)] = 0
            for e2 in ENGINES:
                tot = inc_totals[e2].get(seg_idx, 0)
                if tot:
                    engine.wait_ge(sems[("e", e2)], tot)
            if e in ("pe", "sp") or fences.get(e) is None:
                engine.nop().then_inc(bar, 1)
            elif e == "act":
                engine.copy(fences[e], fences[e]).then_inc(bar, 1)
            else:
                engine.memset(fences[e], 0.0).then_inc(bar, 1)
            nbar = seg_idx + 1
            if in_loop:
                barv = n_eng * nbar + n_eng * ctx.i
                bar2v = nbar + ctx.i
            else:
                barv = n_eng * nbar
                bar2v = nbar
            if e == "pool":
                engine.wait_ge(bar, barv)
                for s in clear_ids:
                    engine.sem_clear(s)
                engine.nop().then_inc(bar2, 1)
            else:
                engine.wait_ge(bar2, bar2v)

        with nc.Block() as blk:
            for e in ENGINES:
                def body(engine, e=e):
                    ctx = Ctx()
                    esem = sems[("e", e)]

                    def run_ops(lo, hi):
                        for i in range(lo, hi):
                            fn, args, kwargs, dma, _seg = self.ops[e][i]
                            for sk, v in self.waits[e][i]:
                                if sk[0] == "e":
                                    engine.wait_ge(sems[sk], inc_rank[sk[1]][v])
                                elif ctx.i is not None:
                                    if body_per_iter.get(sk[1], 0):
                                        dyn_wait(engine, e, sk[1], 16 * v)
                                    else:
                                        engine.wait_ge(sems[sk], 16 * v)
                                else:
                                    engine.wait_ge(sems[sk], 16 * v)
                            rargs = [a(ctx) if callable(a) else a for a in args]
                            inst = getattr(engine, fn)(*rargs, **kwargs)
                            if dma is not None:
                                inst.then_inc(sems[("d", dma)], 16)
                            elif i in markedset[e]:
                                inst.then_inc(esem, 1)

                    marks = [i for i, op in enumerate(self.ops[e])
                             if op[0] in ("@LOOP", "@ENDLOOP", "@END")]
                    assert loop_n is not None
                    mloop, mend, mfin = marks
                    run_ops(0, mloop)
                    emit_barrier(engine, e, ctx, self.ops[e][mloop][4], False)
                    for g in dyn_pairs[e]:
                        engine.reg_mov(waitregs[(e, g)], 0)
                    with engine.Fori(0, loop_n) as iv:
                        ctx.i = iv
                        run_ops(mloop + 1, mend)
                        emit_barrier(engine, e, ctx, self.ops[e][mend][4], True)
                    ctx.i = None
                    run_ops(mend + 1, mfin)

                getattr(blk, engmap[e])(body)
        return sems


FFN_SZ = 2 * 32 * 1024 + 2 * 32 * 4096 + 8 * 4096
ATT_SZ = 4 * 8 * 1024 + 32 * 1024 + 8 * 4096
CACHE_PP = FFN_SZ + LAY * ATT_SZ


def stack_off(w, l, mi):
    if w == "w1":
        return mi * 2048
    if w == "w2":
        return mi * 2048 + 1024
    if w == "h1":
        return 65536 + mi * 4096
    if w == "h2":
        return 196608 + mi * 4096
    if w == "w3":
        return 327680 + mi * 4096
    base = FFN_SZ + l * ATT_SZ
    if w == "wq":
        return base + mi * 1024
    if w == "wk":
        return base + 8192 + mi * 1024
    if w == "wv":
        return base + 16384 + mi * 1024
    if w == "wo":
        return base + 24576 + mi * 1024
    if w == "wff1":
        return base + 32768 + mi * 1024
    if w == "wff2":
        return base + 65536 + mi * 4096
    raise KeyError(w)


def sv_cols():
    cols = {}
    c = 0
    for l in range(LAY):
        for nm, n in (("bq", 8), ("bk", 8), ("bv", 8), ("bo", 8),
                      ("g1", 8), ("b1", 8), ("g2", 8), ("b2", 8),
                      ("bf1", 32), ("bf2", 8)):
            cols[(nm, l)] = c
            c += n
    for nm, n in (("s1", 32), ("s2", 32), ("sh1", 32), ("sh2", 32), ("s3", 8)):
        cols[(nm, None)] = c
        c += n
    return cols, c


SVC, SV_N = sv_cols()
DVC = {}
_c = 0
for l in range(LAY):
    DVC[("bqs", l)] = _c; _c += 8
    DVC[("bks", l)] = _c; _c += 8
DVC[("s3h", None)] = _c; _c += 8
DV_N = _c


import os as _os
DEV_STAGE = _os.environ.get("K_STAGE", "")


def build_nc(NT=NTILE):
    nc = bass.Bass("TRN2", target_bir_lowering=False)

    xt = nc.dram_tensor("xt", [D * S], F32, kind="ExternalInput")
    yt = nc.dram_tensor("yt", [D * S], F32, kind="ExternalOutput")
    w1t = nc.dram_tensor("w1t", [D, DFF], F32, kind="ExternalInput").ap()
    w2t = nc.dram_tensor("w2t", [D, DFF], F32, kind="ExternalInput").ap()
    h1t = nc.dram_tensor("h1t", [DFF, DFF], F32, kind="ExternalInput").ap()
    h2t = nc.dram_tensor("h2t", [DFF, DFF], F32, kind="ExternalInput").ap()
    w3t = nc.dram_tensor("w3t", [DFF, D], F32, kind="ExternalInput").ap()
    wq = nc.dram_tensor("wq", [LAY, D, D], F32, kind="ExternalInput").ap()
    wk = nc.dram_tensor("wk", [LAY, D, D], F32, kind="ExternalInput").ap()
    wv = nc.dram_tensor("wv", [LAY, D, D], F32, kind="ExternalInput").ap()
    wo = nc.dram_tensor("wo", [LAY, D, D], F32, kind="ExternalInput").ap()
    wff1 = nc.dram_tensor("wff1", [LAY, D, DFF], F32, kind="ExternalInput").ap()
    wff2 = nc.dram_tensor("wff2", [LAY, DFF, D], F32, kind="ExternalInput").ap()
    svin = nc.dram_tensor("svin", [128, SV_N], F32, kind="ExternalInput").ap()
    obdin = nc.dram_tensor("obdin", [128, 64 * LAY], F32, kind="ExternalInput").ap()
    cfin = nc.dram_tensor("cfin", [128, 272], F32, kind="ExternalInput").ap()
    cbin = nc.dram_tensor("cbin", [128, 131], BF16, kind="ExternalInput").ap()
    cache = nc.dram_tensor("wcache", [128 * CACHE_PP], BF16, kind="Internal")

    XT = nc.alloc_sbuf_tensor("XT", [128, FC * 512], F32)
    RS = nc.alloc_sbuf_tensor("RS", [128, FC * 512], F32)
    HS = nc.alloc_sbuf_tensor("HS", [128, FC * 512], F32)
    RB = nc.alloc_sbuf_tensor("RB", [128, FC * 512], BF16)
    HB = nc.alloc_sbuf_tensor("HB", [128, FC * 512], BF16)
    BIGA = nc.alloc_sbuf_tensor("BIGA", [128, MCF * 512], BF16)
    BIGB = nc.alloc_sbuf_tensor("BIGB", [128, MCF * 512], BF16)
    W = [nc.alloc_sbuf_tensor(f"W{i}", [128, CH], BF16) for i in range(NSLOT)]
    KT = nc.alloc_sbuf_tensor("KT", [128, FC * 128], BF16)
    VT = nc.alloc_sbuf_tensor("VT", [128, FC * 130], BF16)
    S_sb = nc.alloc_sbuf_tensor("S_sb", [128, LAY * FC * 65], F32)
    S_bf = nc.alloc_sbuf_tensor("S_bf", [128, FC * 65], BF16)
    ATS = nc.alloc_sbuf_tensor("ATS", [128, 16 * 128], BF16)
    SQA = nc.alloc_sbuf_tensor("SQA", [128, 4 * 512], BF16)
    T32 = nc.alloc_sbuf_tensor("T32", [128, 2 * 512], F32)
    RSB = nc.alloc_sbuf_tensor("RSB", [128, 512], F32)
    ROWS = nc.alloc_sbuf_tensor("ROWS", [1, 1536], F32)
    WROW = nc.alloc_sbuf_tensor("WROW", [2, FC * 512], BF16)
    WK = nc.alloc_sbuf_tensor("WK", [128, FC * 2], F32)
    ATM = nc.alloc_sbuf_tensor("ATM", [128, FC * 128], BF16)
    RD2 = nc.alloc_sbuf_tensor("RD2", [128, FC * 4], F32)
    SV = nc.alloc_sbuf_tensor("SV", [128, SV_N], F32)
    DV = nc.alloc_sbuf_tensor("DV", [128, DV_N], F32)
    OBD = nc.alloc_sbuf_tensor("OBD", [128, 64 * LAY], F32)
    OM2 = nc.alloc_sbuf_tensor("OM2", [128, 128 * LAY], BF16)
    CF = nc.alloc_sbuf_tensor("CF", [128, 272], F32)
    CB = nc.alloc_sbuf_tensor("CB", [128, 131], BF16)
    CAST = [nc.alloc_sbuf_tensor(f"CAST{i}", [128, 1024], BF16) for i in range(4)]
    FP = nc.alloc_sbuf_tensor("FP", [128, 1], F32)
    FA = nc.alloc_sbuf_tensor("FA", [1, 2], F32)
    FD = nc.alloc_sbuf_tensor("FD", [1, 2], F32)

    pbt = [nc.alloc_psum_tensor(f"pb{i}", [128, 512], F32) for i in range(6)]
    pb = pbt
    pt = [nc.alloc_psum_tensor(f"pt{i}", [128, 1024], BF16) for i in range(2)]

    P = Prog()

    def sv_ap(nm, l, mi, n=1):
        return SV[:, SVC[(nm, l)] + mi: SVC[(nm, l)] + mi + n]

    def dv_ap(nm, l, mi):
        return DV[:, DVC[(nm, l)] + mi: DVC[(nm, l)] + mi + 1]

    for j, v in enumerate(CONSTVALS):
        nc.const_aps.register(F32, v, CF[:, 256 + j:257 + j])

    ONES_F = CF[0:1, 0:128]
    TRIU = CF[:, 0:128]
    E2 = CF[0:2, 128:256]
    IDENT = CB[:, 0:128]
    ONESCOL = CB[:, 128:129]
    BDONES = CB[:, 129:131]

    P.emit("pool", "dma_start", SV[:, :], svin, dma="cl", commit=False)
    P.emit("pool", "dma_start", OBD[:, :], obdin, dma="cl", commit=False)
    P.emit("pool", "dma_start", CF[:, :], cfin, dma="cl", commit=False)
    P.emit("pool", "dma_start", CB[:, :], cbin, dma="cl", commit=False)
    P.commit_dma("cl", writes=[("SV",), ("OBD",), ("CF",), ("CB",)])

    for l in range(LAY):
        P.emit("dve", "tensor_scalar_mul", DV[:, DVC[("bqs", l)]:DVC[("bqs", l)] + 8],
               sv_ap("bq", l, 0, 8), DHS, reads=[("SV",)], writes=[("DV",)],
               force_inc=True)
        P.emit("dve", "tensor_scalar_mul", DV[:, DVC[("bks", l)]:DVC[("bks", l)] + 8],
               sv_ap("bk", l, 0, 8), DHS, reads=[("SV",)], writes=[("DV",)],
               force_inc=True)
    P.emit("dve", "tensor_scalar_mul", DV[:, DVC[("s3h", None)]:DVC[("s3h", None)] + 8],
           sv_ap("s3", None, 0, 8), 0.5, reads=[("SV",)], writes=[("DV",)],
           force_inc=True)

    for l in range(LAY):
        ob = OBD[:, l * 64: l * 64 + 64]
        o2 = 128 * l
        P.emit("act", "activation", OM2[:, o2:o2 + 32], ob[:, 0:32], AF.Identity,
               reads=[("OBD",)], writes=[("OM2",)], force_inc=True)
        P.emit("act", "activation", OM2[:, o2 + 32:o2 + 64], ob[:, 0:32], AF.Identity,
               scale=-1.0, reads=[("OBD",)], writes=[("OM2",)], force_inc=True)
        P.emit("act", "activation", OM2[:, o2 + 64:o2 + 96], ob[:, 32:64], AF.Identity,
               reads=[("OBD",)], writes=[("OM2",)], force_inc=True)
        P.emit("act", "activation", OM2[:, o2 + 96:o2 + 128], ob[:, 32:64], AF.Identity,
               scale=-1.0, reads=[("OBD",)], writes=[("OM2",)], force_inc=True)

    P.emit("dve", "memset", S_sb[:, :], 0.0, writes=[("S", 0), ("S", 1), ("S", 2)], force_inc=True)
    for fc in range(FC):
        P.emit("dve", "memset", VT[:, fc * 130 + 64: fc * 130 + 65], 1.0,
               writes=[("VTones", fc)], force_inc=True)
        P.emit("dve", "memset", VT[:, fc * 130 + 129: fc * 130 + 130], 1.0,
               writes=[("VTones2", fc)], force_inc=True)
    P.emit("act", "memzero", FA[:, :], force_inc=True)
    P.emit("dve", "memset", FD[:, :], 0.0, force_inc=True)
    P.emit("pool", "memset", FP[:, :], 0.0, force_inc=True)

    cache_chunk_store = {}
    store_log = []

    def prepass():
        cnt = [0]

        def one(w, l, K, M, src3):
            pitch = {"w1": 2048, "w2": 2048}.get(w, K)
            for ki in range(K // 128):
                for mb in range(0, M, 1024):
                    slot = cnt[0] % 4
                    cnt[0] += 1
                    if src3 is None:
                        blk_src = {"w1": w1t, "w2": w2t, "h1": h1t,
                                   "h2": h2t, "w3": w3t}[w][
                                       ki * 128:(ki + 1) * 128, mb:mb + 1024]
                    else:
                        blk_src = src3[l, ki * 128:(ki + 1) * 128, mb:mb + 1024]
                    P.emit("pool", "dma_start", CAST[slot][:, 0:1024], blk_src,
                           dma=("pl", slot), writes=[("CAST", slot)])
                    P.emit("pool", "nop", dma="prog")
                    nmi = min(8, M // 128 - mb // 128)
                    out = bass.AP(cache,
                                  stack_off(w, l, mb // 128) + ki * 128,
                                  [[CACHE_PP, 128], [pitch, nmi], [1, 128]])
                    src = CAST[slot].ap()[:, 0:1024].rearrange(
                        "p (m e) -> p m e", m=8)[:, 0:nmi, :]
                    me = P.emit("sp", "dma_start", out, src, dma=("ps", slot),
                                reads=[("CAST", slot)])
                    store_log.append((slot, me, cnt[0]))
                    for mi2 in range(mb // 128, mb // 128 + nmi):
                        base = stack_off(w, l, mi2)
                        for c in range(base // CH, (base + K - 1) // CH + 1):
                            cache_chunk_store[c] = len(store_log) - 1

        one("w1", None, D, DFF, None)
        one("w2", None, D, DFF, None)
        one("h1", None, DFF, DFF, None)
        one("h2", None, DFF, DFF, None)
        one("w3", None, DFF, D, None)
        for l in range(LAY):
            one("wq", l, D, D, wq)
            one("wk", l, D, D, wk)
            one("wv", l, D, D, wv)
            one("wo", l, D, D, wo)
            one("wff1", l, D, DFF, wff1)
            one("wff2", l, DFF, D, wff2)

    prepass()

    P.begin_loop(NT)

    seq = (list(range(0, 176)) + list(range(176, 320)) + list(range(0, 176)))
    st = {"spos": 0, "issued": -1, "slot_chunk": [None] * NSLOT}

    n_store = len(store_log)
    last_of_slot = {}
    for i, (slot, me, _) in enumerate(store_log):
        last_of_slot[slot] = i
    for s, i in last_of_slot.items():
        P.res[("psfin", s)] = [store_log[i][1], []]
    cache_reads = {}
    for c, i in cache_chunk_store.items():
        if i + 5 > n_store:
            cache_reads[c] = [("psfin", s) for s in sorted(last_of_slot)]
        else:
            P.res[("cache", c)] = [("d", "prog", i + 5), []]
            cache_reads[c] = [("cache", c)]

    def ensure(tpos):
        tgt = min(tpos + NSLOT - 2, len(seq) - 1)
        while st["issued"] < tgt:
            c = st["issued"] + 1
            slot = c % NSLOT
            cid = seq[c]
            src = bass.AP(cache, cid * CH, [[CACHE_PP, 128], [1, CH]])
            P.emit("sp", "dma_start", W[slot][:, :], src, dma=("rg", slot),
                   reads=cache_reads[cid], writes=[("W", slot)])
            st["issued"] = c
            st["slot_chunk"][slot] = c

    def next_stack(w, l, mi, K):
        assert st["spos"] % 1024 == 0
        exp = stack_off(w, l, mi)
        pieces = []
        for ki in range(K // 128):
            pos = st["spos"] + ki * 128
            tpos = pos // CH
            ensure(tpos)
            slot = tpos % NSLOT
            off = pos % CH
            assert seq[tpos] * CH + off == exp + ki * 128, (w, l, mi, ki)
            assert st["slot_chunk"][slot] == tpos,                 ("slot retargeted before consumption", w, l, mi, ki)
            pieces.append((W[slot], ("W", slot), off))
        st["spos"] += K
        return pieces

    rot = [0]

    def bank():
        b = rot[0] % 6
        rot[0] += 1
        return b

    def x_ap(fc, out=False):
        tns = yt if out else xt
        def mk(ctx, fc=fc, tns=tns):
            return bass.AP(tns, ctx.i * 512 + fc * 128 * S, [[S, 128], [1, 512]])
        return mk

    XSL = [XT[:, fc * 512:(fc + 1) * 512] for fc in range(FC)]
    RSL = [RS[:, fc * 512:(fc + 1) * 512] for fc in range(FC)]
    HSL = [HS[:, fc * 512:(fc + 1) * 512] for fc in range(FC)]
    RBL = [RB[:, fc * 512:(fc + 1) * 512] for fc in range(FC)]
    HBL = [HB[:, fc * 512:(fc + 1) * 512] for fc in range(FC)]
    AL = [BIGA[:, m * 512:(m + 1) * 512] for m in range(MCF)]
    BL = [BIGB[:, m * 512:(m + 1) * 512] for m in range(MCF)]

    def matmul_stack(w, l, mi, K, rhs_slices, rhs_keys, bi, extra_reads=()):
        pieces = next_stack(w, l, mi, K)
        n = len(pieces)
        for ki, (wt, wkey, off) in enumerate(pieces):
            P.emit("pe", "matmul", pb[bi][:, :], wt[:, off:off + 128],
                   rhs_slices[ki], start=(ki == 0), stop=(ki == n - 1),
                   reads=[wkey, rhs_keys[ki]] + list(extra_reads),
                   writes=[("pb", bi)])

    def emit_ln(src_slices, src_keys, gamma=None, beta=None,
                out_f32=None, out_f32_keys=None, out_bf=None, out_bf_keys=None):
        for fc in range(FC):
            r = fc % 2
            P.emit("act", "copy", SQA[:, r * 512:(r + 1) * 512], src_slices[fc],
                   reads=[src_keys[fc]], writes=[("SQA", r)])
            P.emit("dve", "tensor_mul", SQA[:, (2 + r) * 512:(3 + r) * 512],
                   SQA[:, r * 512:(r + 1) * 512], SQA[:, r * 512:(r + 1) * 512],
                   reads=[("SQA", r)], writes=[("SQB", r)])
            P.emit("pe", "matmul", pb[4][0:1, 0:512], ONESCOL,
                   SQA[:, r * 512:(r + 1) * 512], start=(fc == 0), stop=(fc == 7),
                   reads=[("SQA", r), ("CB",)], writes=[("pb", 4)])
            P.emit("pe", "matmul", pb[5][0:1, 0:512], ONESCOL,
                   SQA[:, (2 + r) * 512:(3 + r) * 512], start=(fc == 0),
                   stop=(fc == 7), reads=[("SQB", r), ("CB",)], writes=[("pb", 5)])
        mu = ROWS[0:1, 0:512]
        ms = ROWS[0:1, 512:1024]
        tmp = ROWS[0:1, 1024:1536]
        P.emit("act", "activation", mu, pb[4][0:1, 0:512], AF.Identity, scale=1.0 / D,
               reads=[("pb", 4)], writes=[("mu",)])
        P.emit("act", "activation", ms, pb[5][0:1, 0:512], AF.Identity, scale=1.0 / D,
               reads=[("pb", 5)], writes=[("ms",)])
        P.emit("dve", "tensor_mul", tmp, mu, mu, reads=[("mu",)], writes=[("tmp",)])
        P.emit("dve", "tensor_sub", ms, ms, tmp, reads=[("ms",), ("tmp",)],
               writes=[("ms",)])
        P.emit("act", "activation", tmp, ms, AF.Ln, bias=LN_EPS,
               reads=[("ms",)], writes=[("tmp",)])
        P.emit("act", "activation", tmp, tmp, AF.Exp, scale=-0.5,
               reads=[("tmp",)], writes=[("tmp",)])
        P.emit("pe", "matmul", pb[4][:, :], ONES_F, mu, start=True, stop=True,
               reads=[("mu",), ("CF",)], writes=[("pb", 4)])
        P.emit("pe", "matmul", pb[5][:, :], ONES_F, tmp, start=True, stop=True,
               reads=[("tmp",), ("CF",)], writes=[("pb", 5)])
        P.emit("act", "copy", RSB[:, :], pb[5][:, :], reads=[("pb", 5)],
               writes=[("RSB",)])
        for fc in range(FC):
            r = fc % 2
            t32 = T32[:, r * 512:(r + 1) * 512]
            P.emit("dve", "tensor_sub", t32, src_slices[fc], pb[4][:, :],
                   reads=[src_keys[fc], ("pb", 4)], writes=[("T32", r)])
            if gamma is None:
                if out_f32 is not None:
                    P.emit("dve", "tensor_mul", out_f32[fc], t32, RSB[:, :],
                           reads=[("T32", r), ("RSB",)], writes=[out_f32_keys[fc]])
                    P.emit("act", "copy", out_bf[fc], out_f32[fc],
                           reads=[out_f32_keys[fc]], writes=[out_bf_keys[fc]])
                else:
                    P.emit("dve", "tensor_mul", out_bf[fc], t32, RSB[:, :],
                           reads=[("T32", r), ("RSB",)], writes=[out_bf_keys[fc]])
            else:
                P.emit("dve", "tensor_mul", t32, t32, RSB[:, :],
                       reads=[("T32", r), ("RSB",)], writes=[("T32", r)])
                P.emit("act", "activation", out_f32[fc], t32, AF.Identity,
                       scale=sv_ap(gamma[0], gamma[1], fc),
                       bias=sv_ap(beta[0], beta[1], fc),
                       reads=[("T32", r), ("SV",)], writes=[out_f32_keys[fc]])
                P.emit("act", "copy", out_bf[fc], out_f32[fc],
                       reads=[out_f32_keys[fc]], writes=[out_bf_keys[fc]])

    def emit_ffn():
        for mi in range(MCF):
            b1 = bank()
            matmul_stack("w1", None, mi, D, RBL, [("RB", k) for k in range(FC)], b1)
            b2 = bank()
            matmul_stack("w2", None, mi, D, RBL, [("RB", k) for k in range(FC)], b2)
            P.emit("act", "activation", AL[mi], pb[b1][:, :], AF.Silu,
                   scale=sv_ap("s1", None, mi), reads=[("pb", b1), ("SV",)],
                   writes=[("A", mi)])
            P.emit("act", "activation", BL[mi], pb[b2][:, :], AF.Identity,
                   scale=sv_ap("s2", None, mi), reads=[("pb", b2), ("SV",)],
                   writes=[("B", mi)])
            P.emit("dve", "tensor_mul", AL[mi], AL[mi], BL[mi],
                   reads=[("A", mi), ("B", mi)], writes=[("A", mi)])
        for mi in range(MCF):
            bi = bank()
            matmul_stack("h1", None, mi, DFF, AL, [("A", k) for k in range(MCF)], bi)
            P.emit("act", "activation", BL[mi], pb[bi][:, :], AF.Gelu,
                   scale=sv_ap("sh1", None, mi), reads=[("pb", bi), ("SV",)],
                   writes=[("B", mi)])
        for mi in range(MCF):
            bi = bank()
            matmul_stack("h2", None, mi, DFF, BL, [("B", k) for k in range(MCF)], bi)
            P.emit("act", "activation", AL[mi], pb[bi][:, :], AF.Gelu,
                   scale=sv_ap("sh2", None, mi), reads=[("pb", bi), ("SV",)],
                   writes=[("A", mi)])
        for mi in range(FC):
            bi = bank()
            matmul_stack("w3", None, mi, DFF, AL, [("A", k) for k in range(MCF)], bi)
            r = mi % 2
            t32 = T32[:, r * 512:(r + 1) * 512]
            P.emit("act", "activation", t32, pb[bi][:, :], AF.Identity,
                   scale=dv_ap("s3h", None, mi), reads=[("pb", bi), ("DV",)],
                   writes=[("T32", r)])
            P.emit("dve", "tensor_add", XSL[mi], XSL[mi], t32,
                   reads=[("XT", mi), ("T32", r)], writes=[("XT", mi)])

    def emit_favor(l, base, side):
        if side == "k":
            for fc in range(FC):
                r = fc % 4
                P.emit("act", "activation", SQA[:, r * 512:(r + 1) * 512],
                       BL[base + fc], AF.Square,
                       reads=[("B", base + fc)], writes=[("SQA", r)])
                if fc >= 3:
                    f2 = fc - 3
                    r2 = f2 % 4
                    P.emit("pe", "matmul", pb[4 + f2 % 2][0:2, 0:512], BDONES,
                           SQA[:, r2 * 512:(r2 + 1) * 512], start=True, stop=True,
                           reads=[("SQA", r2), ("CB",)], writes=[("pb", 4 + f2 % 2)])
                    P.emit("act", "activation",
                           WROW[0:2, f2 * 512:(f2 + 1) * 512],
                           pb[4 + f2 % 2][0:2, 0:512], AF.Exp, scale=-0.5, bias=LNC,
                           reads=[("pb", 4 + f2 % 2)], writes=[("WR", f2)])
            for f2 in range(FC - 3, FC):
                r2 = f2 % 4
                P.emit("pe", "matmul", pb[4 + f2 % 2][0:2, 0:512], BDONES,
                       SQA[:, r2 * 512:(r2 + 1) * 512], start=True, stop=True,
                       reads=[("SQA", r2), ("CB",)], writes=[("pb", 4 + f2 % 2)])
                P.emit("act", "activation", WROW[0:2, f2 * 512:(f2 + 1) * 512],
                       pb[4 + f2 % 2][0:2, 0:512], AF.Exp, scale=-0.5, bias=LNC,
                       reads=[("pb", 4 + f2 % 2)], writes=[("WR", f2)])
        for fc in range(FC):
            bu = fc % 4
            P.emit("pe", "matmul", pb[bu][:, :], OM2[:, l * 128:(l + 1) * 128],
                   BL[base + fc], start=True, stop=True,
                   reads=[("B", base + fc), ("OM2",)], writes=[("pb", bu)])
            if fc >= 3:
                f2 = fc - 3
                P.emit("act", "activation", BL[base + f2], pb[f2 % 4][:, :],
                       AF.Exp, reads=[("pb", f2 % 4)], writes=[("B", base + f2)])
        for f2 in range(FC - 3, FC):
            P.emit("act", "activation", BL[base + f2], pb[f2 % 4][:, :],
                   AF.Exp, reads=[("pb", f2 % 4)], writes=[("B", base + f2)])

    def emit_la_chunk(l, tc):
        ts = tc * 128
        lb = l * FC * 65
        P.emit("act", "copy", S_bf[:, :], S_sb[:, lb:lb + FC * 65],
               reads=[("S", l)], writes=[("Sbf",)])
        for fc in range(FC):
            p = fc % 2
            kbuf = BL[8 + fc]
            vbuf = BL[16 + fc]
            P.emit("pe", "transpose", pt[p][:, 384:386],
                   WROW[0:2, fc * 512 + ts: fc * 512 + ts + 128], IDENT[0:2, 0:2],
                   reads=[("WR", fc), ("CB",)], writes=[("pt", p)])
            P.emit("pe", "transpose", pt[p][:, 0:128], kbuf[:, ts:ts + 128],
                   IDENT, reads=[("B", 8 + fc), ("CB",)], writes=[("pt", p)])
            P.emit("pe", "transpose", pt[p][:, 128:256], vbuf[:, ts:ts + 128],
                   IDENT, reads=[("B", 16 + fc), ("CB",)], writes=[("pt", p)])
            P.emit("dve", "tensor_copy", WK[:, fc * 2:(fc + 1) * 2],
                   pt[p][:, 384:386], reads=[("pt", p)], writes=[("WK", fc)])
            P.emit("dve", "tensor_scalar_mul", KT[:, fc * 128:fc * 128 + 64],
                   pt[p][:, 0:64], WK[:, fc * 2:fc * 2 + 1],
                   reads=[("pt", p), ("WK", fc)], writes=[("KT", fc)])
            P.emit("dve", "tensor_scalar_mul", KT[:, fc * 128 + 64:fc * 128 + 128],
                   pt[p][:, 64:128], WK[:, fc * 2 + 1:fc * 2 + 2],
                   reads=[("pt", p), ("WK", fc)], writes=[("KT", fc)])
            vt_dst = VT.ap()[:, fc * 130: fc * 130 + 130].rearrange(
                "p (s c) -> p s c", s=2, c=65)[:, :, 0:64]
            P.emit("dve", "tensor_copy", vt_dst,
                   pt[p][:, 128:256].rearrange("p (s c) -> p s c", s=2, c=64),
                   reads=[("pt", p)], writes=[("VT", fc)])
        for fc in range(FC):
            p = fc % 2
            bAe, bAo = 2 * p, 2 * p + 1
            qb = BL[0 + fc]
            qk = ("B", 0 + fc)
            kb = BL[8 + fc]
            kk = ("B", 8 + fc)
            P.emit("pe", "matmul", pb[bAe][:, 0:128], kb[0:64, ts:ts + 128],
                   qb[0:64, ts:ts + 128], start=True, stop=True,
                   reads=[kk, qk], writes=[("pb", bAe)])
            P.emit("pe", "matmul", pb[bAo][:, 0:128], kb[64:128, ts:ts + 128],
                   qb[64:128, ts:ts + 128], start=True, stop=True,
                   reads=[kk, qk], writes=[("pb", bAo)])
            P.emit("dve", "scalar_tensor_tensor", ATS[:, (2 * fc) * 128:(2 * fc + 1) * 128],
                   pb[bAe][:, 0:128], WK[:, fc * 2:fc * 2 + 1], TRIU,
                   ALU.mult, ALU.mult,
                   reads=[("pb", bAe), ("WK", fc), ("CF",)], writes=[("ATS", 2 * fc)])
            P.emit("dve", "scalar_tensor_tensor", ATS[:, (2 * fc + 1) * 128:(2 * fc + 2) * 128],
                   pb[bAo][:, 0:128], WK[:, fc * 2 + 1:fc * 2 + 2], TRIU,
                   ALU.mult, ALU.mult,
                   reads=[("pb", bAo), ("WK", fc), ("CF",)], writes=[("ATS", 2 * fc + 1)])
        for fc in range(FC):
            p = fc % 2
            bO = 4 + p
            qb = BL[0 + fc]
            qk = ("B", 0 + fc)
            P.emit("pe", "matmul", pb[bO][:, 0:65],
                   qb[0:64, ts:ts + 128], S_bf[0:64, fc * 65:(fc + 1) * 65],
                   start=True, stop=False, reads=[qk, ("Sbf",)],
                   writes=[("pb", bO)])
            P.emit("pe", "matmul", pb[bO][:, 0:65],
                   ATS[:, (2 * fc) * 128:(2 * fc + 1) * 128],
                   VT[:, fc * 130: fc * 130 + 65],
                   start=False, stop=True,
                   reads=[("ATS", 2 * fc), ("VT", fc), ("VTones", fc)],
                   writes=[("pb", bO)])
            P.emit("pe", "matmul", pb[bO][:, 65:130],
                   qb[64:128, ts:ts + 128], S_bf[64:128, fc * 65:(fc + 1) * 65],
                   start=True, stop=False, reads=[qk, ("Sbf",)],
                   writes=[("pb", bO)])
            P.emit("pe", "matmul", pb[bO][:, 65:130],
                   ATS[:, (2 * fc + 1) * 128:(2 * fc + 2) * 128],
                   VT[:, fc * 130 + 65: fc * 130 + 130],
                   start=False, stop=True,
                   reads=[("ATS", 2 * fc + 1), ("VT", fc), ("VTones2", fc)],
                   writes=[("pb", bO)])
            dens = bass.AP(pbt[bO], 64, [[512, 128], [65, 2]])
            P.emit("act", "activation", RD2[:, fc * 4: fc * 4 + 2], dens,
                   AF.Identity, bias=ATTN_EPS, reads=[("pb", bO)],
                   writes=[("RDr", fc)])
            P.emit("dve", "reciprocal", RD2[:, fc * 4 + 2: fc * 4 + 4],
                   RD2[:, fc * 4: fc * 4 + 2], reads=[("RDr", fc)],
                   writes=[("RD", fc)])
            P.emit("act", "activation", ATM[:, fc * 128: fc * 128 + 64],
                   pb[bO][:, 0:64], AF.Identity,
                   scale=RD2[:, fc * 4 + 2: fc * 4 + 3],
                   reads=[("pb", bO), ("RD", fc)], writes=[("ATM", fc)])
            P.emit("act", "activation", ATM[:, fc * 128 + 64: fc * 128 + 128],
                   pb[bO][:, 65:129], AF.Identity,
                   scale=RD2[:, fc * 4 + 3: fc * 4 + 4],
                   reads=[("pb", bO), ("RD", fc)], writes=[("ATM", fc)])
        for fc in range(FC):
            p = fc % 2
            bAo = 2 * p + 1
            P.emit("pe", "transpose", pt[p][:, 256:384],
                   ATM[:, fc * 128:(fc + 1) * 128], IDENT,
                   reads=[("ATM", fc), ("CB",)], writes=[("pt", p)])
            P.emit("dve", "tensor_copy", BL[24 + fc][:, ts:ts + 128],
                   pt[p][:, 256:384], reads=[("pt", p)], writes=[("B", 24 + fc)])
            P.emit("pe", "matmul", pb[bAo][0:64, 128:193],
                   KT[:, fc * 128: fc * 128 + 64],
                   VT[:, fc * 130: fc * 130 + 65], start=True, stop=True,
                   reads=[("KT", fc), ("VT", fc), ("VTones", fc)],
                   writes=[("pb", bAo)])
            P.emit("pe", "matmul", pb[bAo][64:128, 128:193],
                   KT[:, fc * 128 + 64: fc * 128 + 128],
                   VT[:, fc * 130 + 65: fc * 130 + 130], start=True, stop=True,
                   tile_position=(0, 64),
                   reads=[("KT", fc), ("VT", fc), ("VTones2", fc)],
                   writes=[("pb", bAo)])
            P.emit("dve", "tensor_add", S_sb[:, lb + fc * 65:lb + (fc + 1) * 65],
                   S_sb[:, lb + fc * 65:lb + (fc + 1) * 65], pb[bAo][:, 128:193],
                   reads=[("S", l), ("pb", bAo)], writes=[("S", l)])

    def emit_attn(l):
        for (wname, base, scale, bias_nm, bias_dv) in (
                ("wq", 0, DHS, "bqs", True), ("wk", 8, DHS, "bks", True),
                ("wv", 16, 1.0, "bv", False)):
            for mi in range(FC):
                bi = bank()
                matmul_stack(wname, l, mi, D, RBL,
                             [("RB", k) for k in range(FC)], bi)
                b_ap = (dv_ap(bias_nm, l, mi) if bias_dv
                        else sv_ap(bias_nm, l, mi))
                P.emit("act", "activation", BL[base + mi], pb[bi][:, :],
                       AF.Identity, scale=scale, bias=b_ap,
                       reads=[("pb", bi), ("DV",) if bias_dv else ("SV",)],
                       writes=[("B", base + mi)])
        emit_favor(l, 0, "q")
        emit_favor(l, 8, "k")
        for tc in range(TC):
            emit_la_chunk(l, tc)
        for mi in range(FC):
            bi = bank()
            matmul_stack("wo", l, mi, D, [BL[24 + k] for k in range(FC)],
                         [("B", 24 + k) for k in range(FC)], bi)
            r = mi % 2
            t32 = T32[:, r * 512:(r + 1) * 512]
            P.emit("act", "activation", t32, pb[bi][:, :], AF.Identity,
                   bias=sv_ap("bo", l, mi), reads=[("pb", bi), ("SV",)],
                   writes=[("T32", r)])
            P.emit("dve", "tensor_add", HSL[mi], RSL[mi], t32,
                   reads=[("RS", mi), ("T32", r)], writes=[("HS", mi)])
        if DEV_STAGE == f"wo{l}":
            return
        emit_ln(HSL, [("HS", k) for k in range(FC)], gamma=("g1", l),
                beta=("b1", l), out_f32=HSL,
                out_f32_keys=[("HS", k) for k in range(FC)],
                out_bf=HBL, out_bf_keys=[("HB", k) for k in range(FC)])
        if DEV_STAGE == f"ln1_{l}":
            return
        for mi in range(MCF):
            bi = bank()
            matmul_stack("wff1", l, mi, D, HBL, [("HB", k) for k in range(FC)], bi)
            P.emit("act", "activation", AL[mi], pb[bi][:, :], AF.Relu,
                   bias=sv_ap("bf1", l, mi), reads=[("pb", bi), ("SV",)],
                   writes=[("A", mi)])
        for mi in range(FC):
            bi = bank()
            matmul_stack("wff2", l, mi, DFF, AL, [("A", k) for k in range(MCF)], bi)
            r = mi % 2
            t32 = T32[:, r * 512:(r + 1) * 512]
            P.emit("act", "activation", t32, pb[bi][:, :], AF.Identity,
                   bias=sv_ap("bf2", l, mi), reads=[("pb", bi), ("SV",)],
                   writes=[("T32", r)])
            P.emit("dve", "tensor_add", HSL[mi], HSL[mi], t32,
                   reads=[("HS", mi), ("T32", r)], writes=[("HS", mi)])
        if DEV_STAGE == f"y{l}":
            return
        emit_ln(HSL, [("HS", k) for k in range(FC)], gamma=("g2", l),
                beta=("b2", l), out_f32=RSL,
                out_f32_keys=[("RS", k) for k in range(FC)],
                out_bf=RBL, out_bf_keys=[("RB", k) for k in range(FC)])

    for fc in range(FC):
        P.emit("pool", "dma_start", XSL[fc], x_ap(fc), dma=("xl", fc),
               writes=[("XT", fc)])

    def store(slices, keys):
        for fc in range(FC):
            P.emit("pool", "dma_start", x_ap(fc, out=True), slices[fc],
                   dma="yo", reads=[keys[fc]], writes=[("yo", fc)])

    done = False
    while True:
        emit_ln(XSL, [("XT", k) for k in range(FC)], out_bf=RBL,
                out_bf_keys=[("RB", k) for k in range(FC)])
        emit_ffn()
        if DEV_STAGE == "ffn1":
            store(XSL, [("XT", k) for k in range(FC)]); break
        emit_ln(XSL, [("XT", k) for k in range(FC)], out_f32=RSL,
                out_f32_keys=[("RS", k) for k in range(FC)], out_bf=RBL,
                out_bf_keys=[("RB", k) for k in range(FC)])
        if DEV_STAGE == "preln":
            store(RSL, [("RS", k) for k in range(FC)]); break
        brk = False
        for l in range(LAY):
            emit_attn(l)
            if DEV_STAGE in (f"l{l}", f"wo{l}", f"ln1_{l}", f"y{l}"):
                brk = True
                break
        if brk:
            if DEV_STAGE.startswith("l") and not DEV_STAGE.startswith("ln1") and not DEV_STAGE.startswith("y"):
                store(RSL, [("RS", k) for k in range(FC)])
            else:
                store(HSL, [("HS", k) for k in range(FC)])
            break
        for fc in range(FC):
            P.emit("dve", "tensor_add", XSL[fc], XSL[fc], RSL[fc],
                   reads=[("XT", fc), ("RS", fc)], writes=[("XT", fc)])
        if DEV_STAGE == "xadd":
            store(XSL, [("XT", k) for k in range(FC)]); break
        emit_ln(XSL, [("XT", k) for k in range(FC)], out_bf=RBL,
                out_bf_keys=[("RB", k) for k in range(FC)])
        emit_ffn()
        store(XSL, [("XT", k) for k in range(FC)])
        done = True
        break

    if done:
        assert st["spos"] == len(seq) * CH, st["spos"]
    P.end_loop()
    P.finalize(nc, fences={"pool": FP[:, :], "act": FA[0:1, 0:1],
                           "dve": FD[0:1, 0:1]})
    n_ops = sum(len(P.ops[e]) for e in ENGINES)
    return nc, n_ops, P.n_waits


def host_prepare(inputs):
    f = np.ascontiguousarray

    def svblock(v):
        v = np.asarray(v)
        return f(v.reshape(-1, 128).T)

    sv = np.zeros((128, SV_N), np.float32)
    for l in range(LAY):
        for nm, src in (("bq", inputs["L0_attn_bq"][l]),
                        ("bk", inputs["L0_attn_bk"][l]),
                        ("bv", inputs["L0_attn_bv"][l]),
                        ("bo", inputs["L0_attn_bo"][l]),
                        ("g1", inputs["L0_attn_ln1_g"][l]),
                        ("b1", inputs["L0_attn_ln1_b"][l]),
                        ("g2", inputs["L0_attn_ln2_g"][l]),
                        ("b2", inputs["L0_attn_ln2_b"][l]),
                        ("bf1", inputs["L0_attn_bf1"][l]),
                        ("bf2", inputs["L0_attn_bf2"][l])):
            blk = svblock(src)
            sv[:, SVC[(nm, l)]:SVC[(nm, l)] + blk.shape[1]] = blk
    for nm, src in (("s1", inputs["L0_ffn_w1_s"]), ("s2", inputs["L0_ffn_w2_s"]),
                    ("sh1", inputs["L0_ffn_h1_s"]), ("sh2", inputs["L0_ffn_h2_s"]),
                    ("s3", inputs["L0_ffn_w3_s"])):
        blk = svblock(src)
        sv[:, SVC[(nm, None)]:SVC[(nm, None)] + blk.shape[1]] = blk

    obd = np.zeros((128, 64 * LAY), np.float32)
    om = np.asarray(inputs["L0_attn_omega"])
    for l in range(LAY):
        obd[0:64, l * 64:l * 64 + 32] = om[l]
        obd[64:128, l * 64 + 32:l * 64 + 64] = om[l]

    cf = np.zeros((128, 272), np.float32)
    cf[:, 0:128] = np.triu(np.ones((128, 128), np.float32))
    cf[0, 128:192] = 1.0
    cf[1, 192:256] = 1.0
    cf[:, 256:256 + len(CONSTVALS)] = np.asarray(CONSTVALS, np.float32)[None, :]
    cb = np.zeros((128, 131), np.float32)
    cb[:, 0:128] = np.eye(128, dtype=np.float32)
    cb[:, 128] = 1.0
    cb[0:64, 129] = 1.0
    cb[64:128, 130] = 1.0
    cb = cb.astype(ml_dtypes.bfloat16)

    shared = dict(
        w1t=f(np.asarray(inputs["L0_ffn_w1_w"]).T),
        w2t=f(np.asarray(inputs["L0_ffn_w2_w"]).T),
        h1t=f(np.asarray(inputs["L0_ffn_h1_w"]).T),
        h2t=f(np.asarray(inputs["L0_ffn_h2_w"]).T),
        w3t=f(np.asarray(inputs["L0_ffn_w3_w"]).T),
        wq=f(np.asarray(inputs["L0_attn_Wq"])),
        wk=f(np.asarray(inputs["L0_attn_Wk"])),
        wv=f(np.asarray(inputs["L0_attn_Wv"])),
        wo=f(np.asarray(inputs["L0_attn_Wo"])),
        wff1=f(np.asarray(inputs["L0_attn_W1"])),
        wff2=f(np.asarray(inputs["L0_attn_W2"])),
        svin=sv, obdin=obd, cfin=cf, cbin=cb,
    )
    x = np.asarray(inputs["L0_attn_x"])
    xts = [f(x[c].T).reshape(-1) for c in range(B)]
    return shared, xts


_CACHE = {}


def kernel(**inputs):
    shared, xts = host_prepare(inputs)
    if "nc" not in _CACHE:
        nc, n_ops, n_waits = build_nc(NTILE)
        _CACHE["nc"] = nc
    nc = _CACHE["nc"]
    in_maps = [dict(shared, xt=xts[c]) for c in range(B)]
    res = run_bass_kernel_spmd(nc, in_maps, core_ids=list(range(B)))
    out = np.empty((B, S, D), np.float32)
    for c in range(B):
        out[c] = res.results[c]["yt"].reshape(D, S).T
    return out


if __name__ == "__main__":
    nc, n_ops, n_waits = build_nc(1)
    print(f"ops={n_ops} waits={n_waits}")


# version 08 (final)
# speedup vs baseline: 1.1219x; 1.0148x over previous
import numpy as np
import ml_dtypes
import concourse.bass as bass
import concourse.mybir as mybir
from concourse.bass_utils import run_bass_kernel_spmd

F32 = mybir.dt.float32
BF16 = mybir.dt.bfloat16
AF = mybir.ActivationFunctionType
ALU = mybir.AluOpType

D = 1024
DFF = 4096
S = 8192
B = 8
T = 512
NTILE = S // T
FC = D // 128
MCF = DFF // 128
TC = T // 128
NH = 16
DH = 64
LAY = 3
DHS = float(DH) ** -0.25
LNC = float(np.log(2 * 32) * -0.5)
LN_EPS = 1e-5
ATTN_EPS = 1e-6
NSLOT = 4
CH = 2048
CONSTVALS = (1.0 / D, LN_EPS, -0.5, 0.5, -LNC, LNC, ATTN_EPS, DHS, -1.0, 1.0,
             0.0, 2.0, 0.25)

ENGINES = ("pe", "act", "dve", "pool", "sp")
PARANOID = frozenset(("act", "dve", "pool"))


class Ctx:
    def __init__(self):
        self.i = None


class Prog:
    def __init__(self):
        self.ops = {e: [] for e in ENGINES}
        self.waits = {e: [] for e in ENGINES}
        self.opclock = {e: [] for e in ENGINES}
        self.dmaclock = {}
        self.clock = {e: {} for e in ENGINES}
        self.group_count = {}
        self.group_engine = {}
        self.need_inc = {e: set() for e in ENGINES}
        self.res = {}
        self.n_waits = 0
        self.seg = 0
        self.seg_start = {e: [0] for e in ENGINES}
        self.loop = None
        self.group_final = []
        self._seen_groups = set()

    def _need(self, eng, clk, needed, p):
        if p is None:
            return
        if p[0] == "e":
            if p[1] == eng and eng not in PARANOID:
                return
            sk, v = ("e", p[1]), p[2]
        else:
            sk, v = ("d", p[1]), p[2]
        if clk.get(sk, -1) >= v:
            return
        if needed.get(sk, -1) < v:
            needed[sk] = v

    def emit(self, eng, fn, *args, reads=(), writes=(), dma=None, commit=True,
             force_inc=False, **kwargs):
        pos = len(self.ops[eng])
        clk = self.clock[eng]
        needed = {}
        for k in reads:
            st = self.res.get(k)
            if st is not None:
                self._need(eng, clk, needed, st[0])
        for k in writes:
            st = self.res.get(k)
            if st is not None:
                self._need(eng, clk, needed, st[0])
                for r in st[1]:
                    self._need(eng, clk, needed, r)
        wl = []
        for sk, v in sorted(needed.items(), key=repr):
            if sk[0] == "e":
                self.need_inc[sk[1]].add((sk[1], v))
                snap = self.opclock[sk[1]][v]
            else:
                snap = self.dmaclock[(sk[1], v)]
            for k2, v2 in snap.items():
                if clk.get(k2, -1) < v2:
                    clk[k2] = v2
            if clk.get(sk, -1) < v:
                clk[sk] = v
            wl.append((sk, v))
        self.n_waits += len(wl)
        self.waits[eng].append(wl)
        self.ops[eng].append((fn, args, kwargs, dma, self.seg))
        if force_inc and dma is None:
            self.need_inc[eng].add((eng, pos))
        if dma is not None:
            assert self.group_engine.setdefault(dma, eng) == eng
            cnt = self.group_count.get(dma, 0) + 1
            self.group_count[dma] = cnt
            me = ("d", dma, cnt)
            self.dmaclock[(dma, cnt)] = dict(clk)
            self.opclock[eng].append(dict(clk))
        else:
            me = ("e", eng, pos)
            snap = dict(clk)
            snap[("e", eng)] = pos
            self.opclock[eng].append(snap)
        if commit:
            self._register(me, reads, writes)
        return me

    def commit_dma(self, group, reads=(), writes=()):
        cnt = self.group_count[group]
        self._register(("d", group, cnt), reads, writes)

    def _register(self, me, reads, writes):
        for k in writes:
            self.res[k] = [me, []]
        for k in reads:
            st = self.res.get(k)
            if st is None:
                st = [None, []]
                self.res[k] = st
            st[1].append(me)

    def _cut_segment(self, marker):
        for g in self.group_count:
            assert g not in self._seen_groups, f"group {g} in two segments"
        self._seen_groups |= set(self.group_count)
        self.group_final.append(dict(self.group_count))
        for e in ENGINES:
            self.ops[e].append((marker, None, None, None, self.seg))
            self.waits[e].append([])
            self.opclock[e].append({})
        self.seg += 1
        for e in ENGINES:
            self.seg_start[e].append(len(self.ops[e]))
            self.clock[e] = {}
        self.group_count = {}
        self.dmaclock = {}
        self.res = {k: [None, []] for k in self.res}

    def begin_loop(self, n_iters):
        assert self.loop is None
        self._cut_segment("@LOOP")
        self.loop = n_iters

    def end_loop(self):
        self._cut_segment("@ENDLOOP")

    def finalize(self, nc, fences=None):
        if fences is None:
            fences = {}
        self._cut_segment("@END")
        sems = {}
        for e in ENGINES:
            sems[("e", e)] = nc.alloc_semaphore(f"sem_{e}")
        for g in self.group_engine:
            nm = "_".join(map(str, g)) if isinstance(g, tuple) else str(g)
            sems[("d", g)] = nc.alloc_semaphore(f"dg_{nm}")
        bar = nc.alloc_semaphore("bar")
        bar2 = nc.alloc_semaphore("bar2")
        clear_ids = [sems[("e", e)] for e in ENGINES]
        body_per_iter = (self.group_final[1]
                         if self.loop is not None and len(self.group_final) > 1
                         else {})

        def seg_of(e, pos):
            lo = 0
            for si, st in enumerate(self.seg_start[e]):
                if st <= pos:
                    lo = si
            return lo

        inc_rank = {e: {} for e in ENGINES}
        inc_totals = {e: {} for e in ENGINES}
        for e in ENGINES:
            for p in sorted(p for (_, p) in self.need_inc[e]):
                s = seg_of(e, p)
                inc_totals[e][s] = inc_totals[e].get(s, 0) + 1
                inc_rank[e][p] = inc_totals[e][s]
        markedset = {e: set(p for (_, p) in self.need_inc[e]) for e in ENGINES}
        n_eng = len(ENGINES)
        loop_n = self.loop
        engmap = dict(pe="tensor", act="scalar", dve="vector", pool="gpsimd",
                      sp="sync")
        enumap = dict(pe=mybir.EngineType.PE, act=mybir.EngineType.Activation,
                      dve=mybir.EngineType.DVE, pool=mybir.EngineType.Pool,
                      sp=mybir.EngineType.SP)

        dyn_pairs = {e: [] for e in ENGINES}
        if loop_n is not None:
            for e in ENGINES:
                seen = dict()
                lo = self.seg_start[e][1]
                hi = self.seg_start[e][2] - 1
                for i in range(lo, hi):
                    for sk, v in self.waits[e][i]:
                        if sk[0] == "d" and sk[1] not in seen:
                            seen[sk[1]] = True
                for g, cnt in self.group_final[1].items():
                    if self.group_engine[g] == e and cnt and g not in seen:
                        seen[g] = True
                dyn_pairs[e] = list(seen)
        waitregs = {}
        for e in ENGINES:
            for g in dyn_pairs[e]:
                rh = nc.alloc_registers(f"wr_{e}_{g}", engines=[enumap[e]])
                waitregs[(e, g)] = list(rh)[0]
        cur_target = {}

        def dyn_wait(engine, e, g, target):
            reg = waitregs[(e, g)]
            cur = cur_target.get((e, g), 0)
            delta = target - cur
            assert delta >= 0, (e, g, target, cur)
            if delta:
                engine.reg_alu(reg, reg, delta, mybir.AluOpType.add)
            engine.wait_ge(sems[("d", g)], reg)
            cur_target[(e, g)] = target

        def emit_barrier(engine, e, ctx, seg_idx, in_loop):
            gf = self.group_final[seg_idx]
            for g, cnt in sorted(gf.items(), key=repr):
                if self.group_engine[g] == e and cnt:
                    if in_loop:
                        dyn_wait(engine, e, g, 16 * cnt)
                    else:
                        engine.wait_ge(sems[("d", g)], 16 * cnt)
            if in_loop:
                per_map = self.group_final[1]
                for g in dyn_pairs[e]:
                    per = per_map.get(g, 0)
                    tgt = 16 * per
                    cur = cur_target.get((e, g), 0)
                    if tgt != cur:
                        engine.reg_alu(waitregs[(e, g)], waitregs[(e, g)],
                                       tgt - cur, mybir.AluOpType.add)
                    cur_target[(e, g)] = 0
            for e2 in ENGINES:
                tot = inc_totals[e2].get(seg_idx, 0)
                if tot:
                    engine.wait_ge(sems[("e", e2)], tot)
            if e in ("pe", "sp") or fences.get(e) is None:
                engine.nop().then_inc(bar, 1)
            elif e == "act":
                engine.copy(fences[e], fences[e]).then_inc(bar, 1)
            else:
                engine.memset(fences[e], 0.0).then_inc(bar, 1)
            nbar = seg_idx + 1
            if in_loop:
                barv = n_eng * nbar + n_eng * ctx.i
                bar2v = nbar + ctx.i
            else:
                barv = n_eng * nbar
                bar2v = nbar
            if e == "pool":
                engine.wait_ge(bar, barv)
                for s in clear_ids:
                    engine.sem_clear(s)
                engine.nop().then_inc(bar2, 1)
            else:
                engine.wait_ge(bar2, bar2v)

        with nc.Block() as blk:
            for e in ENGINES:
                def body(engine, e=e):
                    ctx = Ctx()
                    esem = sems[("e", e)]

                    def run_ops(lo, hi):
                        for i in range(lo, hi):
                            fn, args, kwargs, dma, _seg = self.ops[e][i]
                            for sk, v in self.waits[e][i]:
                                if sk[0] == "e":
                                    engine.wait_ge(sems[sk], inc_rank[sk[1]][v])
                                elif ctx.i is not None:
                                    dyn_wait(engine, e, sk[1], 16 * v)
                                else:
                                    engine.wait_ge(sems[sk], 16 * v)
                            rargs = [a(ctx) if callable(a) else a for a in args]
                            inst = getattr(engine, fn)(*rargs, **kwargs)
                            if dma is not None:
                                inst.then_inc(sems[("d", dma)], 16)
                            elif i in markedset[e]:
                                inst.then_inc(esem, 1)

                    marks = [i for i, op in enumerate(self.ops[e])
                             if op[0] in ("@LOOP", "@ENDLOOP", "@END")]
                    assert loop_n is not None
                    mloop, mend, mfin = marks
                    run_ops(0, mloop)
                    emit_barrier(engine, e, ctx, self.ops[e][mloop][4], False)
                    for g in dyn_pairs[e]:
                        engine.reg_mov(waitregs[(e, g)], 0)
                    with engine.Fori(0, loop_n) as iv:
                        ctx.i = iv
                        run_ops(mloop + 1, mend)
                        emit_barrier(engine, e, ctx, self.ops[e][mend][4], True)
                    ctx.i = None
                    run_ops(mend + 1, mfin)

                getattr(blk, engmap[e])(body)
        return sems


FFN_SZ = 2 * 32 * 1024 + 2 * 32 * 4096 + 8 * 4096
ATT_SZ = 4 * 8 * 1024 + 32 * 1024 + 8 * 4096
CACHE_PP = FFN_SZ + LAY * ATT_SZ


def stack_off(w, l, mi):
    if w == "w1":
        return mi * 2048
    if w == "w2":
        return mi * 2048 + 1024
    if w == "h1":
        return 65536 + mi * 4096
    if w == "h2":
        return 196608 + mi * 4096
    if w == "w3":
        return 327680 + mi * 4096
    base = FFN_SZ + l * ATT_SZ
    if w == "wq":
        return base + mi * 1024
    if w == "wk":
        return base + 8192 + mi * 1024
    if w == "wv":
        return base + 16384 + mi * 1024
    if w == "wo":
        return base + 24576 + mi * 1024
    if w == "wff1":
        return base + 32768 + mi * 1024
    if w == "wff2":
        return base + 65536 + mi * 4096
    raise KeyError(w)


def sv_cols():
    cols = {}
    c = 0
    for l in range(LAY):
        for nm, n in (("bq", 8), ("bk", 8), ("bv", 8), ("bo", 8),
                      ("g1", 8), ("b1", 8), ("g2", 8), ("b2", 8),
                      ("bf1", 32), ("bf2", 8)):
            cols[(nm, l)] = c
            c += n
    for nm, n in (("s1", 32), ("s2", 32), ("sh1", 32), ("sh2", 32), ("s3", 8)):
        cols[(nm, None)] = c
        c += n
    return cols, c


SVC, SV_N = sv_cols()
DVC = {}
_c = 0
for l in range(LAY):
    DVC[("bqs", l)] = _c; _c += 8
    DVC[("bks", l)] = _c; _c += 8
DVC[("s3h", None)] = _c; _c += 8
DV_N = _c


import os as _os
DEV_STAGE = _os.environ.get("K_STAGE", "")


def build_nc(NT=NTILE):
    nc = bass.Bass("TRN2", target_bir_lowering=False)

    xt = nc.dram_tensor("xt", [D * S], F32, kind="ExternalInput")
    yt = nc.dram_tensor("yt", [D * S], F32, kind="ExternalOutput")
    w1t = nc.dram_tensor("w1t", [D, DFF], F32, kind="ExternalInput").ap()
    w2t = nc.dram_tensor("w2t", [D, DFF], F32, kind="ExternalInput").ap()
    h1t = nc.dram_tensor("h1t", [DFF, DFF], F32, kind="ExternalInput").ap()
    h2t = nc.dram_tensor("h2t", [DFF, DFF], F32, kind="ExternalInput").ap()
    w3t = nc.dram_tensor("w3t", [DFF, D], F32, kind="ExternalInput").ap()
    wq = nc.dram_tensor("wq", [LAY, D, D], F32, kind="ExternalInput").ap()
    wk = nc.dram_tensor("wk", [LAY, D, D], F32, kind="ExternalInput").ap()
    wv = nc.dram_tensor("wv", [LAY, D, D], F32, kind="ExternalInput").ap()
    wo = nc.dram_tensor("wo", [LAY, D, D], F32, kind="ExternalInput").ap()
    wff1 = nc.dram_tensor("wff1", [LAY, D, DFF], F32, kind="ExternalInput").ap()
    wff2 = nc.dram_tensor("wff2", [LAY, DFF, D], F32, kind="ExternalInput").ap()
    svin = nc.dram_tensor("svin", [128, SV_N], F32, kind="ExternalInput").ap()
    obdin = nc.dram_tensor("obdin", [128, 64 * LAY], F32, kind="ExternalInput").ap()
    cfin = nc.dram_tensor("cfin", [128, 272], F32, kind="ExternalInput").ap()
    cbin = nc.dram_tensor("cbin", [128, 131], BF16, kind="ExternalInput").ap()
    cache = nc.dram_tensor("wcache", [128 * CACHE_PP], BF16, kind="Internal")

    XT = nc.alloc_sbuf_tensor("XT", [128, FC * 512], F32)
    RS = nc.alloc_sbuf_tensor("RS", [128, FC * 512], F32)
    HS = nc.alloc_sbuf_tensor("HS", [128, FC * 512], F32)
    RB = nc.alloc_sbuf_tensor("RB", [128, FC * 512], BF16)
    HB = nc.alloc_sbuf_tensor("HB", [128, FC * 512], BF16)
    BIGA = nc.alloc_sbuf_tensor("BIGA", [128, MCF * 512], BF16)
    BIGB = nc.alloc_sbuf_tensor("BIGB", [128, MCF * 512], BF16)
    W = [nc.alloc_sbuf_tensor(f"W{i}", [128, CH], BF16) for i in range(NSLOT)]
    KT = nc.alloc_sbuf_tensor("KT", [128, FC * 128], BF16)
    VT = nc.alloc_sbuf_tensor("VT", [128, FC * 130], BF16)
    S_sb = nc.alloc_sbuf_tensor("S_sb", [128, LAY * FC * 65], F32)
    S_bf = nc.alloc_sbuf_tensor("S_bf", [128, FC * 65], BF16)
    ATS = nc.alloc_sbuf_tensor("ATS", [128, 16 * 128], BF16)
    SQA = nc.alloc_sbuf_tensor("SQA", [128, 4 * 512], BF16)
    T32 = nc.alloc_sbuf_tensor("T32", [128, 2 * 512], F32)
    RSB = nc.alloc_sbuf_tensor("RSB", [128, 512], F32)
    ROWS = nc.alloc_sbuf_tensor("ROWS", [1, 1536], F32)
    WROW = nc.alloc_sbuf_tensor("WROW", [2, FC * 512], BF16)
    WK = nc.alloc_sbuf_tensor("WK", [128, FC * 2], F32)
    ATM = nc.alloc_sbuf_tensor("ATM", [128, FC * 128], BF16)
    RD2 = nc.alloc_sbuf_tensor("RD2", [128, FC * 4], F32)
    SV = nc.alloc_sbuf_tensor("SV", [128, SV_N], F32)
    DV = nc.alloc_sbuf_tensor("DV", [128, DV_N], F32)
    OBD = nc.alloc_sbuf_tensor("OBD", [128, 64 * LAY], F32)
    OM2 = nc.alloc_sbuf_tensor("OM2", [128, 128 * LAY], BF16)
    CF = nc.alloc_sbuf_tensor("CF", [128, 272], F32)
    CB = nc.alloc_sbuf_tensor("CB", [128, 131], BF16)
    CAST = [nc.alloc_sbuf_tensor(f"CAST{i}", [128, 1024], BF16) for i in range(4)]
    FP = nc.alloc_sbuf_tensor("FP", [128, 1], F32)
    FA = nc.alloc_sbuf_tensor("FA", [1, 2], F32)
    FD = nc.alloc_sbuf_tensor("FD", [1, 2], F32)

    pbt = [nc.alloc_psum_tensor(f"pb{i}", [128, 512], F32) for i in range(6)]
    pb = pbt
    pt = [nc.alloc_psum_tensor(f"pt{i}", [128, 1024], BF16) for i in range(2)]

    P = Prog()

    def sv_ap(nm, l, mi, n=1):
        return SV[:, SVC[(nm, l)] + mi: SVC[(nm, l)] + mi + n]

    def dv_ap(nm, l, mi):
        return DV[:, DVC[(nm, l)] + mi: DVC[(nm, l)] + mi + 1]

    for j, v in enumerate(CONSTVALS):
        nc.const_aps.register(F32, v, CF[:, 256 + j:257 + j])

    ONES_F = CF[0:1, 0:128]
    TRIU = CF[:, 0:128]
    E2 = CF[0:2, 128:256]
    IDENT = CB[:, 0:128]
    ONESCOL = CB[:, 128:129]
    BDONES = CB[:, 129:131]

    P.emit("pool", "dma_start", SV[:, :], svin, dma="cl", commit=False)
    P.emit("pool", "dma_start", OBD[:, :], obdin, dma="cl", commit=False)
    P.emit("pool", "dma_start", CF[:, :], cfin, dma="cl", commit=False)
    P.emit("pool", "dma_start", CB[:, :], cbin, dma="cl", commit=False)
    P.commit_dma("cl", writes=[("SV",), ("OBD",), ("CF",), ("CB",)])

    for l in range(LAY):
        P.emit("dve", "tensor_scalar_mul", DV[:, DVC[("bqs", l)]:DVC[("bqs", l)] + 8],
               sv_ap("bq", l, 0, 8), DHS, reads=[("SV",)], writes=[("DV",)],
               force_inc=True)
        P.emit("dve", "tensor_scalar_mul", DV[:, DVC[("bks", l)]:DVC[("bks", l)] + 8],
               sv_ap("bk", l, 0, 8), DHS, reads=[("SV",)], writes=[("DV",)],
               force_inc=True)
    P.emit("dve", "tensor_scalar_mul", DV[:, DVC[("s3h", None)]:DVC[("s3h", None)] + 8],
           sv_ap("s3", None, 0, 8), 0.5, reads=[("SV",)], writes=[("DV",)],
           force_inc=True)

    for l in range(LAY):
        ob = OBD[:, l * 64: l * 64 + 64]
        o2 = 128 * l
        P.emit("act", "activation", OM2[:, o2:o2 + 32], ob[:, 0:32], AF.Identity,
               reads=[("OBD",)], writes=[("OM2",)], force_inc=True)
        P.emit("act", "activation", OM2[:, o2 + 32:o2 + 64], ob[:, 0:32], AF.Identity,
               scale=-1.0, reads=[("OBD",)], writes=[("OM2",)], force_inc=True)
        P.emit("act", "activation", OM2[:, o2 + 64:o2 + 96], ob[:, 32:64], AF.Identity,
               reads=[("OBD",)], writes=[("OM2",)], force_inc=True)
        P.emit("act", "activation", OM2[:, o2 + 96:o2 + 128], ob[:, 32:64], AF.Identity,
               scale=-1.0, reads=[("OBD",)], writes=[("OM2",)], force_inc=True)

    P.emit("dve", "memset", S_sb[:, :], 0.0, writes=[("S", 0), ("S", 1), ("S", 2)], force_inc=True)
    for fc in range(FC):
        P.emit("dve", "memset", VT[:, fc * 130 + 64: fc * 130 + 65], 1.0,
               writes=[("VTones", fc)], force_inc=True)
        P.emit("dve", "memset", VT[:, fc * 130 + 129: fc * 130 + 130], 1.0,
               writes=[("VTones2", fc)], force_inc=True)
    P.emit("act", "memzero", FA[:, :], force_inc=True)
    P.emit("dve", "memset", FD[:, :], 0.0, force_inc=True)
    P.emit("pool", "memset", FP[:, :], 0.0, force_inc=True)

    def prepass():
        cnt = [0]

        def one(w, l, K, M, src3):
            pitch = {"w1": 2048, "w2": 2048}.get(w, K)
            for ki in range(K // 128):
                for mb in range(0, M, 1024):
                    slot = cnt[0] % 4
                    cnt[0] += 1
                    if src3 is None:
                        blk_src = {"w1": w1t, "w2": w2t, "h1": h1t,
                                   "h2": h2t, "w3": w3t}[w][
                                       ki * 128:(ki + 1) * 128, mb:mb + 1024]
                    else:
                        blk_src = src3[l, ki * 128:(ki + 1) * 128, mb:mb + 1024]
                    P.emit("pool", "dma_start", CAST[slot][:, 0:1024], blk_src,
                           dma=("pl", slot), writes=[("CAST", slot)])
                    nmi = min(8, M // 128 - mb // 128)
                    out = bass.AP(cache,
                                  stack_off(w, l, mb // 128) + ki * 128,
                                  [[CACHE_PP, 128], [pitch, nmi], [1, 128]])
                    src = CAST[slot].ap()[:, 0:1024].rearrange(
                        "p (m e) -> p m e", m=8)[:, 0:nmi, :]
                    P.emit("sp", "dma_start", out, src, dma=("ps", slot),
                           reads=[("CAST", slot)])

        one("w1", None, D, DFF, None)
        one("w2", None, D, DFF, None)
        one("h1", None, DFF, DFF, None)
        one("h2", None, DFF, DFF, None)
        one("w3", None, DFF, D, None)
        for l in range(LAY):
            one("wq", l, D, D, wq)
            one("wk", l, D, D, wk)
            one("wv", l, D, D, wv)
            one("wo", l, D, D, wo)
            one("wff1", l, D, DFF, wff1)
            one("wff2", l, DFF, D, wff2)

    prepass()

    P.begin_loop(NT)

    seq = (list(range(0, 176)) + list(range(176, 320)) + list(range(0, 176)))
    st = {"spos": 0, "issued": -1, "slot_chunk": [None] * NSLOT}

    def ensure(tpos):
        tgt = min(tpos + NSLOT - 2, len(seq) - 1)
        while st["issued"] < tgt:
            c = st["issued"] + 1
            slot = c % NSLOT
            cid = seq[c]
            src = bass.AP(cache, cid * CH, [[CACHE_PP, 128], [1, CH]])
            P.emit("sp", "dma_start", W[slot][:, :], src, dma=("rg", slot),
                   writes=[("W", slot)])
            st["issued"] = c
            st["slot_chunk"][slot] = c

    def next_stack(w, l, mi, K):
        assert st["spos"] % 1024 == 0
        exp = stack_off(w, l, mi)
        pieces = []
        for ki in range(K // 128):
            pos = st["spos"] + ki * 128
            tpos = pos // CH
            ensure(tpos)
            slot = tpos % NSLOT
            off = pos % CH
            assert seq[tpos] * CH + off == exp + ki * 128, (w, l, mi, ki)
            assert st["slot_chunk"][slot] == tpos,                 ("slot retargeted before consumption", w, l, mi, ki)
            pieces.append((W[slot], ("W", slot), off))
        st["spos"] += K
        return pieces

    rot = [0]

    def bank():
        b = rot[0] % 4
        rot[0] += 1
        return b

    def x_ap(fc, out=False):
        tns = yt if out else xt
        def mk(ctx, fc=fc, tns=tns):
            return bass.AP(tns, ctx.i * 512 + fc * 128 * S, [[S, 128], [1, 512]])
        return mk

    XSL = [XT[:, fc * 512:(fc + 1) * 512] for fc in range(FC)]
    RSL = [RS[:, fc * 512:(fc + 1) * 512] for fc in range(FC)]
    HSL = [HS[:, fc * 512:(fc + 1) * 512] for fc in range(FC)]
    RBL = [RB[:, fc * 512:(fc + 1) * 512] for fc in range(FC)]
    HBL = [HB[:, fc * 512:(fc + 1) * 512] for fc in range(FC)]
    AL = [BIGA[:, m * 512:(m + 1) * 512] for m in range(MCF)]
    BL = [BIGB[:, m * 512:(m + 1) * 512] for m in range(MCF)]

    def matmul_stack(w, l, mi, K, rhs_slices, rhs_keys, bi, extra_reads=(),
                     pieces=None):
        if pieces is None:
            pieces = next_stack(w, l, mi, K)
        n = len(pieces)
        for ki, (wt, wkey, off) in enumerate(pieces):
            P.emit("pe", "matmul", pb[bi][:, :], wt[:, off:off + 128],
                   rhs_slices[ki], start=(ki == 0), stop=(ki == n - 1),
                   reads=[wkey, rhs_keys[ki]] + list(extra_reads),
                   writes=[("pb", bi)])

    def bundle(stacks, banks):
        rd = set()
        for pieces in stacks:
            for _, wkey, _ in pieces:
                rd.add(wkey)
        P.emit("pe", "nop", reads=sorted(rd, key=repr),
               writes=[("pb", b) for b in banks])

    def emit_ln(src_slices, src_keys, gamma=None, beta=None,
                out_f32=None, out_f32_keys=None, out_bf=None, out_bf_keys=None):
        for fc in range(FC):
            r = fc % 2
            P.emit("act", "copy", SQA[:, r * 512:(r + 1) * 512], src_slices[fc],
                   reads=[src_keys[fc]], writes=[("SQA", r)])
            P.emit("dve", "tensor_mul", SQA[:, (2 + r) * 512:(3 + r) * 512],
                   SQA[:, r * 512:(r + 1) * 512], SQA[:, r * 512:(r + 1) * 512],
                   reads=[("SQA", r)], writes=[("SQB", r)])
            P.emit("pe", "matmul", pb[4][0:1, 0:512], ONESCOL,
                   SQA[:, r * 512:(r + 1) * 512], start=(fc == 0), stop=(fc == 7),
                   reads=[("SQA", r), ("CB",)], writes=[("pb", 4)])
            P.emit("pe", "matmul", pb[5][0:1, 0:512], ONESCOL,
                   SQA[:, (2 + r) * 512:(3 + r) * 512], start=(fc == 0),
                   stop=(fc == 7), reads=[("SQB", r), ("CB",)], writes=[("pb", 5)])
        mu = ROWS[0:1, 0:512]
        ms = ROWS[0:1, 512:1024]
        tmp = ROWS[0:1, 1024:1536]
        P.emit("act", "activation", mu, pb[4][0:1, 0:512], AF.Identity, scale=1.0 / D,
               reads=[("pb", 4)], writes=[("mu",)])
        P.emit("act", "activation", ms, pb[5][0:1, 0:512], AF.Identity, scale=1.0 / D,
               reads=[("pb", 5)], writes=[("ms",)])
        P.emit("dve", "tensor_mul", tmp, mu, mu, reads=[("mu",)], writes=[("tmp",)])
        P.emit("dve", "tensor_sub", ms, ms, tmp, reads=[("ms",), ("tmp",)],
               writes=[("ms",)])
        P.emit("act", "activation", tmp, ms, AF.Ln, bias=LN_EPS,
               reads=[("ms",)], writes=[("tmp",)])
        P.emit("act", "activation", tmp, tmp, AF.Exp, scale=-0.5,
               reads=[("tmp",)], writes=[("tmp",)])
        P.emit("pe", "matmul", pb[4][:, :], ONES_F, mu, start=True, stop=True,
               reads=[("mu",), ("CF",)], writes=[("pb", 4)])
        P.emit("pe", "matmul", pb[5][:, :], ONES_F, tmp, start=True, stop=True,
               reads=[("tmp",), ("CF",)], writes=[("pb", 5)])
        P.emit("act", "copy", RSB[:, :], pb[5][:, :], reads=[("pb", 5)],
               writes=[("RSB",)])
        for fc in range(FC):
            r = fc % 2
            t32 = T32[:, r * 512:(r + 1) * 512]
            P.emit("dve", "tensor_sub", t32, src_slices[fc], pb[4][:, :],
                   reads=[src_keys[fc], ("pb", 4)], writes=[("T32", r)])
            if gamma is None:
                if out_f32 is not None:
                    P.emit("dve", "tensor_mul", out_f32[fc], t32, RSB[:, :],
                           reads=[("T32", r), ("RSB",)], writes=[out_f32_keys[fc]])
                    P.emit("act", "copy", out_bf[fc], out_f32[fc],
                           reads=[out_f32_keys[fc]], writes=[out_bf_keys[fc]])
                else:
                    P.emit("dve", "tensor_mul", out_bf[fc], t32, RSB[:, :],
                           reads=[("T32", r), ("RSB",)], writes=[out_bf_keys[fc]])
            else:
                P.emit("dve", "tensor_mul", t32, t32, RSB[:, :],
                       reads=[("T32", r), ("RSB",)], writes=[("T32", r)])
                P.emit("act", "activation", out_f32[fc], t32, AF.Identity,
                       scale=sv_ap(gamma[0], gamma[1], fc),
                       bias=sv_ap(beta[0], beta[1], fc),
                       reads=[("T32", r), ("SV",)], writes=[out_f32_keys[fc]])
                P.emit("act", "copy", out_bf[fc], out_f32[fc],
                       reads=[out_f32_keys[fc]], writes=[out_bf_keys[fc]])

    def emit_ffn():
        for mi in range(MCF):
            p1 = next_stack("w1", None, mi, D)
            p2 = next_stack("w2", None, mi, D)
            b1 = bank()
            b2 = bank()
            bundle([p1, p2], [b1, b2])
            matmul_stack("w1", None, mi, D, RBL, [("RB", k) for k in range(FC)], b1,
                         pieces=p1)
            matmul_stack("w2", None, mi, D, RBL, [("RB", k) for k in range(FC)], b2,
                         pieces=p2)
            P.emit("act", "activation", AL[mi], pb[b1][:, :], AF.Silu,
                   scale=sv_ap("s1", None, mi), reads=[("pb", b1), ("SV",)],
                   writes=[("A", mi)])
            P.emit("act", "activation", BL[mi], pb[b2][:, :], AF.Identity,
                   scale=sv_ap("s2", None, mi), reads=[("pb", b2), ("SV",)],
                   writes=[("B", mi)])
            P.emit("dve", "tensor_mul", AL[mi], AL[mi], BL[mi],
                   reads=[("A", mi), ("B", mi)], writes=[("A", mi)])
        for mi in range(MCF):
            bi = bank()
            matmul_stack("h1", None, mi, DFF, AL, [("A", k) for k in range(MCF)], bi)
            P.emit("act", "activation", BL[mi], pb[bi][:, :], AF.Gelu,
                   scale=sv_ap("sh1", None, mi), reads=[("pb", bi), ("SV",)],
                   writes=[("B", mi)])
        for mi in range(MCF):
            bi = bank()
            matmul_stack("h2", None, mi, DFF, BL, [("B", k) for k in range(MCF)], bi)
            P.emit("act", "activation", AL[mi], pb[bi][:, :], AF.Gelu,
                   scale=sv_ap("sh2", None, mi), reads=[("pb", bi), ("SV",)],
                   writes=[("A", mi)])
        for mi in range(FC):
            bi = bank()
            matmul_stack("w3", None, mi, DFF, AL, [("A", k) for k in range(MCF)], bi)
            r = mi % 2
            t32 = T32[:, r * 512:(r + 1) * 512]
            P.emit("act", "activation", t32, pb[bi][:, :], AF.Identity,
                   scale=dv_ap("s3h", None, mi), reads=[("pb", bi), ("DV",)],
                   writes=[("T32", r)])
            P.emit("dve", "tensor_add", XSL[mi], XSL[mi], t32,
                   reads=[("XT", mi), ("T32", r)], writes=[("XT", mi)])

    def emit_favor(l, base, side):
        if side == "k":
            for fc in range(FC):
                r = fc % 4
                P.emit("act", "activation", SQA[:, r * 512:(r + 1) * 512],
                       BL[base + fc], AF.Square,
                       reads=[("B", base + fc)], writes=[("SQA", r)])
                if fc >= 3:
                    f2 = fc - 3
                    r2 = f2 % 4
                    P.emit("pe", "matmul", pb[4 + f2 % 2][0:2, 0:512], BDONES,
                           SQA[:, r2 * 512:(r2 + 1) * 512], start=True, stop=True,
                           reads=[("SQA", r2), ("CB",)], writes=[("pb", 4 + f2 % 2)])
                    P.emit("act", "activation",
                           WROW[0:2, f2 * 512:(f2 + 1) * 512],
                           pb[4 + f2 % 2][0:2, 0:512], AF.Exp, scale=-0.5, bias=LNC,
                           reads=[("pb", 4 + f2 % 2)], writes=[("WR", f2)])
            for f2 in range(FC - 3, FC):
                r2 = f2 % 4
                P.emit("pe", "matmul", pb[4 + f2 % 2][0:2, 0:512], BDONES,
                       SQA[:, r2 * 512:(r2 + 1) * 512], start=True, stop=True,
                       reads=[("SQA", r2), ("CB",)], writes=[("pb", 4 + f2 % 2)])
                P.emit("act", "activation", WROW[0:2, f2 * 512:(f2 + 1) * 512],
                       pb[4 + f2 % 2][0:2, 0:512], AF.Exp, scale=-0.5, bias=LNC,
                       reads=[("pb", 4 + f2 % 2)], writes=[("WR", f2)])
        for fc in range(FC):
            bu = fc % 4
            P.emit("pe", "matmul", pb[bu][:, :], OM2[:, l * 128:(l + 1) * 128],
                   BL[base + fc], start=True, stop=True,
                   reads=[("B", base + fc), ("OM2",)], writes=[("pb", bu)])
            if fc >= 3:
                f2 = fc - 3
                P.emit("act", "activation", BL[base + f2], pb[f2 % 4][:, :],
                       AF.Exp, reads=[("pb", f2 % 4)], writes=[("B", base + f2)])
        for f2 in range(FC - 3, FC):
            P.emit("act", "activation", BL[base + f2], pb[f2 % 4][:, :],
                   AF.Exp, reads=[("pb", f2 % 4)], writes=[("B", base + f2)])

    def emit_la_chunk(l, tc):
        ts = tc * 128
        lb = l * FC * 65
        P.emit("act", "copy", S_bf[:, :], S_sb[:, lb:lb + FC * 65],
               reads=[("S", l)], writes=[("Sbf",)])
        for fc in range(FC):
            p = fc % 2
            kbuf = BL[8 + fc]
            vbuf = BL[16 + fc]
            P.emit("pe", "transpose", pt[p][:, 384:386],
                   WROW[0:2, fc * 512 + ts: fc * 512 + ts + 128], IDENT[0:2, 0:2],
                   reads=[("WR", fc), ("CB",)], writes=[("pt", p)])
            P.emit("pe", "transpose", pt[p][:, 0:128], kbuf[:, ts:ts + 128],
                   IDENT, reads=[("B", 8 + fc), ("CB",)], writes=[("pt", p)])
            P.emit("pe", "transpose", pt[p][:, 128:256], vbuf[:, ts:ts + 128],
                   IDENT, reads=[("B", 16 + fc), ("CB",)], writes=[("pt", p)])
            P.emit("dve", "tensor_copy", WK[:, fc * 2:(fc + 1) * 2],
                   pt[p][:, 384:386], reads=[("pt", p)], writes=[("WK", fc)])
            P.emit("dve", "tensor_scalar_mul", KT[:, fc * 128:fc * 128 + 64],
                   pt[p][:, 0:64], WK[:, fc * 2:fc * 2 + 1],
                   reads=[("pt", p), ("WK", fc)], writes=[("KT", fc)])
            P.emit("dve", "tensor_scalar_mul", KT[:, fc * 128 + 64:fc * 128 + 128],
                   pt[p][:, 64:128], WK[:, fc * 2 + 1:fc * 2 + 2],
                   reads=[("pt", p), ("WK", fc)], writes=[("KT", fc)])
            vt_dst = VT.ap()[:, fc * 130: fc * 130 + 130].rearrange(
                "p (s c) -> p s c", s=2, c=65)[:, :, 0:64]
            P.emit("dve", "tensor_copy", vt_dst,
                   pt[p][:, 128:256].rearrange("p (s c) -> p s c", s=2, c=64),
                   reads=[("pt", p)], writes=[("VT", fc)])
        for fc in range(FC):
            p = fc % 2
            bAe, bAo = 2 * p, 2 * p + 1
            qb = BL[0 + fc]
            qk = ("B", 0 + fc)
            kb = BL[8 + fc]
            kk = ("B", 8 + fc)
            P.emit("pe", "matmul", pb[bAe][:, 0:128], kb[0:64, ts:ts + 128],
                   qb[0:64, ts:ts + 128], start=True, stop=True,
                   reads=[kk, qk], writes=[("pb", bAe)])
            P.emit("pe", "matmul", pb[bAo][:, 0:128], kb[64:128, ts:ts + 128],
                   qb[64:128, ts:ts + 128], start=True, stop=True,
                   reads=[kk, qk], writes=[("pb", bAo)])
            P.emit("dve", "scalar_tensor_tensor", ATS[:, (2 * fc) * 128:(2 * fc + 1) * 128],
                   pb[bAe][:, 0:128], WK[:, fc * 2:fc * 2 + 1], TRIU,
                   ALU.mult, ALU.mult,
                   reads=[("pb", bAe), ("WK", fc), ("CF",)], writes=[("ATS", 2 * fc)])
            P.emit("dve", "scalar_tensor_tensor", ATS[:, (2 * fc + 1) * 128:(2 * fc + 2) * 128],
                   pb[bAo][:, 0:128], WK[:, fc * 2 + 1:fc * 2 + 2], TRIU,
                   ALU.mult, ALU.mult,
                   reads=[("pb", bAo), ("WK", fc), ("CF",)], writes=[("ATS", 2 * fc + 1)])
        for fc in range(FC):
            p = fc % 2
            bO = 4 + p
            qb = BL[0 + fc]
            qk = ("B", 0 + fc)
            P.emit("pe", "matmul", pb[bO][:, 0:65],
                   qb[0:64, ts:ts + 128], S_bf[0:64, fc * 65:(fc + 1) * 65],
                   start=True, stop=False, reads=[qk, ("Sbf",)],
                   writes=[("pb", bO)])
            P.emit("pe", "matmul", pb[bO][:, 0:65],
                   ATS[:, (2 * fc) * 128:(2 * fc + 1) * 128],
                   VT[:, fc * 130: fc * 130 + 65],
                   start=False, stop=True,
                   reads=[("ATS", 2 * fc), ("VT", fc), ("VTones", fc)],
                   writes=[("pb", bO)])
            P.emit("pe", "matmul", pb[bO][:, 65:130],
                   qb[64:128, ts:ts + 128], S_bf[64:128, fc * 65:(fc + 1) * 65],
                   start=True, stop=False, reads=[qk, ("Sbf",)],
                   writes=[("pb", bO)])
            P.emit("pe", "matmul", pb[bO][:, 65:130],
                   ATS[:, (2 * fc + 1) * 128:(2 * fc + 2) * 128],
                   VT[:, fc * 130 + 65: fc * 130 + 130],
                   start=False, stop=True,
                   reads=[("ATS", 2 * fc + 1), ("VT", fc), ("VTones2", fc)],
                   writes=[("pb", bO)])
            dens = bass.AP(pbt[bO], 64, [[512, 128], [65, 2]])
            P.emit("act", "activation", RD2[:, fc * 4: fc * 4 + 2], dens,
                   AF.Identity, bias=ATTN_EPS, reads=[("pb", bO)],
                   writes=[("RDr", fc)])
            P.emit("dve", "reciprocal", RD2[:, fc * 4 + 2: fc * 4 + 4],
                   RD2[:, fc * 4: fc * 4 + 2], reads=[("RDr", fc)],
                   writes=[("RD", fc)])
            P.emit("act", "activation", ATM[:, fc * 128: fc * 128 + 64],
                   pb[bO][:, 0:64], AF.Identity,
                   scale=RD2[:, fc * 4 + 2: fc * 4 + 3],
                   reads=[("pb", bO), ("RD", fc)], writes=[("ATM", fc)])
            P.emit("act", "activation", ATM[:, fc * 128 + 64: fc * 128 + 128],
                   pb[bO][:, 65:129], AF.Identity,
                   scale=RD2[:, fc * 4 + 3: fc * 4 + 4],
                   reads=[("pb", bO), ("RD", fc)], writes=[("ATM", fc)])
        for fc in range(FC):
            p = fc % 2
            bAo = 2 * p + 1
            P.emit("pe", "transpose", pt[p][:, 256:384],
                   ATM[:, fc * 128:(fc + 1) * 128], IDENT,
                   reads=[("ATM", fc), ("CB",)], writes=[("pt", p)])
            P.emit("dve", "tensor_copy", BL[24 + fc][:, ts:ts + 128],
                   pt[p][:, 256:384], reads=[("pt", p)], writes=[("B", 24 + fc)])
            P.emit("pe", "matmul", pb[bAo][0:64, 128:193],
                   KT[:, fc * 128: fc * 128 + 64],
                   VT[:, fc * 130: fc * 130 + 65], start=True, stop=True,
                   reads=[("KT", fc), ("VT", fc), ("VTones", fc)],
                   writes=[("pb", bAo)])
            P.emit("pe", "matmul", pb[bAo][64:128, 128:193],
                   KT[:, fc * 128 + 64: fc * 128 + 128],
                   VT[:, fc * 130 + 65: fc * 130 + 130], start=True, stop=True,
                   tile_position=(0, 64),
                   reads=[("KT", fc), ("VT", fc), ("VTones2", fc)],
                   writes=[("pb", bAo)])
            P.emit("dve", "tensor_add", S_sb[:, lb + fc * 65:lb + (fc + 1) * 65],
                   S_sb[:, lb + fc * 65:lb + (fc + 1) * 65], pb[bAo][:, 128:193],
                   reads=[("S", l), ("pb", bAo)], writes=[("S", l)])

    def emit_attn(l):
        for (wname, base, scale, bias_nm, bias_dv) in (
                ("wq", 0, DHS, "bqs", True), ("wk", 8, DHS, "bks", True),
                ("wv", 16, 1.0, "bv", False)):
            for mi0 in range(0, FC, 2):
                ps2 = [next_stack(wname, l, mi0, D),
                       next_stack(wname, l, mi0 + 1, D)]
                bs2 = [bank(), bank()]
                bundle(ps2, bs2)
                for j, mi in enumerate((mi0, mi0 + 1)):
                    matmul_stack(wname, l, mi, D, RBL,
                                 [("RB", k) for k in range(FC)], bs2[j],
                                 pieces=ps2[j])
                    b_ap = (dv_ap(bias_nm, l, mi) if bias_dv
                            else sv_ap(bias_nm, l, mi))
                    P.emit("act", "activation", BL[base + mi], pb[bs2[j]][:, :],
                           AF.Identity, scale=scale, bias=b_ap,
                           reads=[("pb", bs2[j]), ("DV",) if bias_dv else ("SV",)],
                           writes=[("B", base + mi)])
        emit_favor(l, 0, "q")
        emit_favor(l, 8, "k")
        for tc in range(TC):
            emit_la_chunk(l, tc)
        for mi in range(FC):
            bi = bank()
            matmul_stack("wo", l, mi, D, [BL[24 + k] for k in range(FC)],
                         [("B", 24 + k) for k in range(FC)], bi)
            r = mi % 2
            t32 = T32[:, r * 512:(r + 1) * 512]
            P.emit("act", "activation", t32, pb[bi][:, :], AF.Identity,
                   bias=sv_ap("bo", l, mi), reads=[("pb", bi), ("SV",)],
                   writes=[("T32", r)])
            P.emit("dve", "tensor_add", HSL[mi], RSL[mi], t32,
                   reads=[("RS", mi), ("T32", r)], writes=[("HS", mi)])
        if DEV_STAGE == f"wo{l}":
            return
        emit_ln(HSL, [("HS", k) for k in range(FC)], gamma=("g1", l),
                beta=("b1", l), out_f32=HSL,
                out_f32_keys=[("HS", k) for k in range(FC)],
                out_bf=HBL, out_bf_keys=[("HB", k) for k in range(FC)])
        if DEV_STAGE == f"ln1_{l}":
            return
        for mi0 in range(0, MCF, 2):
            ps2 = [next_stack("wff1", l, mi0, D),
                   next_stack("wff1", l, mi0 + 1, D)]
            bs2 = [bank(), bank()]
            bundle(ps2, bs2)
            for j, mi in enumerate((mi0, mi0 + 1)):
                matmul_stack("wff1", l, mi, D, HBL,
                             [("HB", k) for k in range(FC)], bs2[j], pieces=ps2[j])
                P.emit("act", "activation", AL[mi], pb[bs2[j]][:, :], AF.Relu,
                       bias=sv_ap("bf1", l, mi), reads=[("pb", bs2[j]), ("SV",)],
                       writes=[("A", mi)])
        for mi in range(FC):
            bi = bank()
            matmul_stack("wff2", l, mi, DFF, AL, [("A", k) for k in range(MCF)], bi)
            r = mi % 2
            t32 = T32[:, r * 512:(r + 1) * 512]
            P.emit("act", "activation", t32, pb[bi][:, :], AF.Identity,
                   bias=sv_ap("bf2", l, mi), reads=[("pb", bi), ("SV",)],
                   writes=[("T32", r)])
            P.emit("dve", "tensor_add", HSL[mi], HSL[mi], t32,
                   reads=[("HS", mi), ("T32", r)], writes=[("HS", mi)])
        if DEV_STAGE == f"y{l}":
            return
        emit_ln(HSL, [("HS", k) for k in range(FC)], gamma=("g2", l),
                beta=("b2", l), out_f32=RSL,
                out_f32_keys=[("RS", k) for k in range(FC)],
                out_bf=RBL, out_bf_keys=[("RB", k) for k in range(FC)])

    for fc in range(FC):
        P.emit("pool", "dma_start", XSL[fc], x_ap(fc), dma=("xl", fc),
               writes=[("XT", fc)])

    def store(slices, keys):
        for fc in range(FC):
            P.emit("pool", "dma_start", x_ap(fc, out=True), slices[fc],
                   dma="yo", reads=[keys[fc]], writes=[("yo", fc)])

    done = False
    while True:
        emit_ln(XSL, [("XT", k) for k in range(FC)], out_bf=RBL,
                out_bf_keys=[("RB", k) for k in range(FC)])
        emit_ffn()
        if DEV_STAGE == "ffn1":
            store(XSL, [("XT", k) for k in range(FC)]); break
        emit_ln(XSL, [("XT", k) for k in range(FC)], out_f32=RSL,
                out_f32_keys=[("RS", k) for k in range(FC)], out_bf=RBL,
                out_bf_keys=[("RB", k) for k in range(FC)])
        if DEV_STAGE == "preln":
            store(RSL, [("RS", k) for k in range(FC)]); break
        brk = False
        for l in range(LAY):
            emit_attn(l)
            if DEV_STAGE in (f"l{l}", f"wo{l}", f"ln1_{l}", f"y{l}"):
                brk = True
                break
        if brk:
            if DEV_STAGE.startswith("l") and not DEV_STAGE.startswith("ln1") and not DEV_STAGE.startswith("y"):
                store(RSL, [("RS", k) for k in range(FC)])
            else:
                store(HSL, [("HS", k) for k in range(FC)])
            break
        for fc in range(FC):
            P.emit("dve", "tensor_add", XSL[fc], XSL[fc], RSL[fc],
                   reads=[("XT", fc), ("RS", fc)], writes=[("XT", fc)])
        if DEV_STAGE == "xadd":
            store(XSL, [("XT", k) for k in range(FC)]); break
        emit_ln(XSL, [("XT", k) for k in range(FC)], out_bf=RBL,
                out_bf_keys=[("RB", k) for k in range(FC)])
        emit_ffn()
        store(XSL, [("XT", k) for k in range(FC)])
        done = True
        break

    if done:
        assert st["spos"] == len(seq) * CH, st["spos"]
    P.end_loop()
    P.finalize(nc, fences={"pool": FP[:, :], "act": FA[0:1, 0:1],
                           "dve": FD[0:1, 0:1]})
    n_ops = sum(len(P.ops[e]) for e in ENGINES)
    return nc, n_ops, P.n_waits


def host_prepare(inputs):
    f = np.ascontiguousarray

    def svblock(v):
        v = np.asarray(v)
        return f(v.reshape(-1, 128).T)

    sv = np.zeros((128, SV_N), np.float32)
    for l in range(LAY):
        for nm, src in (("bq", inputs["L0_attn_bq"][l]),
                        ("bk", inputs["L0_attn_bk"][l]),
                        ("bv", inputs["L0_attn_bv"][l]),
                        ("bo", inputs["L0_attn_bo"][l]),
                        ("g1", inputs["L0_attn_ln1_g"][l]),
                        ("b1", inputs["L0_attn_ln1_b"][l]),
                        ("g2", inputs["L0_attn_ln2_g"][l]),
                        ("b2", inputs["L0_attn_ln2_b"][l]),
                        ("bf1", inputs["L0_attn_bf1"][l]),
                        ("bf2", inputs["L0_attn_bf2"][l])):
            blk = svblock(src)
            sv[:, SVC[(nm, l)]:SVC[(nm, l)] + blk.shape[1]] = blk
    for nm, src in (("s1", inputs["L0_ffn_w1_s"]), ("s2", inputs["L0_ffn_w2_s"]),
                    ("sh1", inputs["L0_ffn_h1_s"]), ("sh2", inputs["L0_ffn_h2_s"]),
                    ("s3", inputs["L0_ffn_w3_s"])):
        blk = svblock(src)
        sv[:, SVC[(nm, None)]:SVC[(nm, None)] + blk.shape[1]] = blk

    obd = np.zeros((128, 64 * LAY), np.float32)
    om = np.asarray(inputs["L0_attn_omega"])
    for l in range(LAY):
        obd[0:64, l * 64:l * 64 + 32] = om[l]
        obd[64:128, l * 64 + 32:l * 64 + 64] = om[l]

    cf = np.zeros((128, 272), np.float32)
    cf[:, 0:128] = np.triu(np.ones((128, 128), np.float32))
    cf[0, 128:192] = 1.0
    cf[1, 192:256] = 1.0
    cf[:, 256:256 + len(CONSTVALS)] = np.asarray(CONSTVALS, np.float32)[None, :]
    cb = np.zeros((128, 131), np.float32)
    cb[:, 0:128] = np.eye(128, dtype=np.float32)
    cb[:, 128] = 1.0
    cb[0:64, 129] = 1.0
    cb[64:128, 130] = 1.0
    cb = cb.astype(ml_dtypes.bfloat16)

    shared = dict(
        w1t=f(np.asarray(inputs["L0_ffn_w1_w"]).T),
        w2t=f(np.asarray(inputs["L0_ffn_w2_w"]).T),
        h1t=f(np.asarray(inputs["L0_ffn_h1_w"]).T),
        h2t=f(np.asarray(inputs["L0_ffn_h2_w"]).T),
        w3t=f(np.asarray(inputs["L0_ffn_w3_w"]).T),
        wq=f(np.asarray(inputs["L0_attn_Wq"])),
        wk=f(np.asarray(inputs["L0_attn_Wk"])),
        wv=f(np.asarray(inputs["L0_attn_Wv"])),
        wo=f(np.asarray(inputs["L0_attn_Wo"])),
        wff1=f(np.asarray(inputs["L0_attn_W1"])),
        wff2=f(np.asarray(inputs["L0_attn_W2"])),
        svin=sv, obdin=obd, cfin=cf, cbin=cb,
    )
    x = np.asarray(inputs["L0_attn_x"])
    xts = [f(x[c].T).reshape(-1) for c in range(B)]
    return shared, xts


_CACHE = {}


def kernel(**inputs):
    shared, xts = host_prepare(inputs)
    if "nc" not in _CACHE:
        nc, n_ops, n_waits = build_nc(NTILE)
        _CACHE["nc"] = nc
    nc = _CACHE["nc"]
    in_maps = [dict(shared, xt=xts[c]) for c in range(B)]
    res = run_bass_kernel_spmd(nc, in_maps, core_ids=list(range(B)))
    out = np.empty((B, S, D), np.float32)
    for c in range(B):
        out[c] = res.results[c]["yt"].reshape(D, S).T
    return out


if __name__ == "__main__":
    nc, n_ops, n_waits = build_nc(1)
    print(f"ops={n_ops} waits={n_waits}")


# version 09
# speedup vs baseline: 1.1352x; 1.0119x over previous
import numpy as np
import ml_dtypes
import concourse.bass as bass
import concourse.mybir as mybir
from concourse.bass_utils import run_bass_kernel_spmd

F32 = mybir.dt.float32
BF16 = mybir.dt.bfloat16
AF = mybir.ActivationFunctionType
ALU = mybir.AluOpType

D = 1024
DFF = 4096
S = 8192
B = 8
T = 512
NTILE = S // T
FC = D // 128
MCF = DFF // 128
TC = T // 128
NH = 16
DH = 64
LAY = 3
DHS = float(DH) ** -0.25
LNC = float(np.log(2 * 32) * -0.5)
LN_EPS = 1e-5
ATTN_EPS = 1e-6
NSLOT = 4
CH = 2048
CONSTVALS = (1.0 / D, LN_EPS, -0.5, 0.5, -LNC, LNC, ATTN_EPS, DHS, -1.0, 1.0,
             0.0, 2.0, 0.25)

ENGINES = ("pe", "act", "dve", "pool", "sp")
PARANOID = frozenset(("act", "dve", "pool"))


class Ctx:
    def __init__(self):
        self.i = None


class Prog:
    def __init__(self):
        self.ops = {e: [] for e in ENGINES}
        self.waits = {e: [] for e in ENGINES}
        self.opclock = {e: [] for e in ENGINES}
        self.dmaclock = {}
        self.clock = {e: {} for e in ENGINES}
        self.group_count = {}
        self.group_engine = {}
        self.need_inc = {e: set() for e in ENGINES}
        self.res = {}
        self.n_waits = 0
        self.seg = 0
        self.seg_start = {e: [0] for e in ENGINES}
        self.loop = None
        self.group_final = []
        self._seen_groups = set()

    def _need(self, eng, clk, needed, p):
        if p is None:
            return
        if p[0] == "e":
            if p[1] == eng and eng not in PARANOID:
                return
            sk, v = ("e", p[1]), p[2]
        else:
            sk, v = ("d", p[1]), p[2]
        if clk.get(sk, -1) >= v:
            return
        if needed.get(sk, -1) < v:
            needed[sk] = v

    def emit(self, eng, fn, *args, reads=(), writes=(), dma=None, commit=True,
             force_inc=False, **kwargs):
        pos = len(self.ops[eng])
        clk = self.clock[eng]
        needed = {}
        for k in reads:
            st = self.res.get(k)
            if st is not None:
                self._need(eng, clk, needed, st[0])
        for k in writes:
            st = self.res.get(k)
            if st is not None:
                self._need(eng, clk, needed, st[0])
                for r in st[1]:
                    self._need(eng, clk, needed, r)
        wl = []
        for sk, v in sorted(needed.items(), key=repr):
            if sk[0] == "e":
                self.need_inc[sk[1]].add((sk[1], v))
                snap = self.opclock[sk[1]][v]
            else:
                snap = self.dmaclock[(sk[1], v)]
            for k2, v2 in snap.items():
                if clk.get(k2, -1) < v2:
                    clk[k2] = v2
            if clk.get(sk, -1) < v:
                clk[sk] = v
            wl.append((sk, v))
        self.n_waits += len(wl)
        self.waits[eng].append(wl)
        self.ops[eng].append((fn, args, kwargs, dma, self.seg))
        if force_inc and dma is None:
            self.need_inc[eng].add((eng, pos))
        if dma is not None:
            assert self.group_engine.setdefault(dma, eng) == eng
            cnt = self.group_count.get(dma, 0) + 1
            self.group_count[dma] = cnt
            me = ("d", dma, cnt)
            self.dmaclock[(dma, cnt)] = dict(clk)
            self.opclock[eng].append(dict(clk))
        else:
            me = ("e", eng, pos)
            snap = dict(clk)
            snap[("e", eng)] = pos
            self.opclock[eng].append(snap)
        if commit:
            self._register(me, reads, writes)
        return me

    def commit_dma(self, group, reads=(), writes=()):
        cnt = self.group_count[group]
        self._register(("d", group, cnt), reads, writes)

    def _register(self, me, reads, writes):
        for k in writes:
            self.res[k] = [me, []]
        for k in reads:
            st = self.res.get(k)
            if st is None:
                st = [None, []]
                self.res[k] = st
            st[1].append(me)

    def _cut_segment(self, marker):
        for g in self.group_count:
            assert g not in self._seen_groups, f"group {g} in two segments"
        self._seen_groups |= set(self.group_count)
        self.group_final.append(dict(self.group_count))
        for e in ENGINES:
            self.ops[e].append((marker, None, None, None, self.seg))
            self.waits[e].append([])
            self.opclock[e].append({})
        self.seg += 1
        for e in ENGINES:
            self.seg_start[e].append(len(self.ops[e]))
            self.clock[e] = {}
        self.group_count = {}
        self.dmaclock = {}
        self.res = {k: [None, []] for k in self.res}

    def begin_loop(self, n_iters):
        assert self.loop is None
        self._cut_segment("@LOOP")
        self.loop = n_iters

    def end_loop(self):
        self._cut_segment("@ENDLOOP")

    def finalize(self, nc, fences=None):
        if fences is None:
            fences = {}
        self._cut_segment("@END")
        sems = {}
        for e in ENGINES:
            sems[("e", e)] = nc.alloc_semaphore(f"sem_{e}")
        for g in self.group_engine:
            nm = "_".join(map(str, g)) if isinstance(g, tuple) else str(g)
            sems[("d", g)] = nc.alloc_semaphore(f"dg_{nm}")
        bar = nc.alloc_semaphore("bar")
        bar2 = nc.alloc_semaphore("bar2")
        clear_ids = [sems[("e", e)] for e in ENGINES]
        body_per_iter = (self.group_final[1]
                         if self.loop is not None and len(self.group_final) > 1
                         else {})

        def seg_of(e, pos):
            lo = 0
            for si, st in enumerate(self.seg_start[e]):
                if st <= pos:
                    lo = si
            return lo

        inc_rank = {e: {} for e in ENGINES}
        inc_totals = {e: {} for e in ENGINES}
        for e in ENGINES:
            for p in sorted(p for (_, p) in self.need_inc[e]):
                s = seg_of(e, p)
                inc_totals[e][s] = inc_totals[e].get(s, 0) + 1
                inc_rank[e][p] = inc_totals[e][s]
        markedset = {e: set(p for (_, p) in self.need_inc[e]) for e in ENGINES}
        n_eng = len(ENGINES)
        loop_n = self.loop
        engmap = dict(pe="tensor", act="scalar", dve="vector", pool="gpsimd",
                      sp="sync")
        enumap = dict(pe=mybir.EngineType.PE, act=mybir.EngineType.Activation,
                      dve=mybir.EngineType.DVE, pool=mybir.EngineType.Pool,
                      sp=mybir.EngineType.SP)

        dyn_pairs = {e: [] for e in ENGINES}
        if loop_n is not None:
            for e in ENGINES:
                seen = dict()
                lo = self.seg_start[e][1]
                hi = self.seg_start[e][2] - 1
                for i in range(lo, hi):
                    for sk, v in self.waits[e][i]:
                        if sk[0] == "d" and sk[1] not in seen:
                            seen[sk[1]] = True
                for g, cnt in self.group_final[1].items():
                    if self.group_engine[g] == e and cnt and g not in seen:
                        seen[g] = True
                dyn_pairs[e] = list(seen)
        waitregs = {}
        for e in ENGINES:
            for g in dyn_pairs[e]:
                rh = nc.alloc_registers(f"wr_{e}_{g}", engines=[enumap[e]])
                waitregs[(e, g)] = list(rh)[0]
        cur_target = {}

        def dyn_wait(engine, e, g, target):
            reg = waitregs[(e, g)]
            cur = cur_target.get((e, g), 0)
            delta = target - cur
            assert delta >= 0, (e, g, target, cur)
            if delta:
                engine.reg_alu(reg, reg, delta, mybir.AluOpType.add)
            engine.wait_ge(sems[("d", g)], reg)
            cur_target[(e, g)] = target

        def emit_barrier(engine, e, ctx, seg_idx, in_loop):
            gf = self.group_final[seg_idx]
            for g, cnt in sorted(gf.items(), key=repr):
                if self.group_engine[g] == e and cnt:
                    if in_loop:
                        dyn_wait(engine, e, g, 16 * cnt)
                    else:
                        engine.wait_ge(sems[("d", g)], 16 * cnt)
            if in_loop:
                per_map = self.group_final[1]
                for g in dyn_pairs[e]:
                    per = per_map.get(g, 0)
                    tgt = 16 * per
                    cur = cur_target.get((e, g), 0)
                    if tgt != cur:
                        engine.reg_alu(waitregs[(e, g)], waitregs[(e, g)],
                                       tgt - cur, mybir.AluOpType.add)
                    cur_target[(e, g)] = 0
            for e2 in ENGINES:
                tot = inc_totals[e2].get(seg_idx, 0)
                if tot:
                    engine.wait_ge(sems[("e", e2)], tot)
            if e in ("pe", "sp") or fences.get(e) is None:
                engine.nop().then_inc(bar, 1)
            elif e == "act":
                engine.copy(fences[e], fences[e]).then_inc(bar, 1)
            else:
                engine.memset(fences[e], 0.0).then_inc(bar, 1)
            nbar = seg_idx + 1
            if in_loop:
                barv = n_eng * nbar + n_eng * ctx.i
                bar2v = nbar + ctx.i
            else:
                barv = n_eng * nbar
                bar2v = nbar
            if e == "pool":
                engine.wait_ge(bar, barv)
                for s in clear_ids:
                    engine.sem_clear(s)
                engine.nop().then_inc(bar2, 1)
            else:
                engine.wait_ge(bar2, bar2v)

        with nc.Block() as blk:
            for e in ENGINES:
                def body(engine, e=e):
                    ctx = Ctx()
                    esem = sems[("e", e)]

                    def run_ops(lo, hi):
                        for i in range(lo, hi):
                            fn, args, kwargs, dma, _seg = self.ops[e][i]
                            for sk, v in self.waits[e][i]:
                                if sk[0] == "e":
                                    engine.wait_ge(sems[sk], inc_rank[sk[1]][v])
                                elif ctx.i is not None:
                                    dyn_wait(engine, e, sk[1], 16 * v)
                                else:
                                    engine.wait_ge(sems[sk], 16 * v)
                            rargs = [a(ctx) if callable(a) else a for a in args]
                            inst = getattr(engine, fn)(*rargs, **kwargs)
                            if dma is not None:
                                inst.then_inc(sems[("d", dma)], 16)
                            elif i in markedset[e]:
                                inst.then_inc(esem, 1)

                    marks = [i for i, op in enumerate(self.ops[e])
                             if op[0] in ("@LOOP", "@ENDLOOP", "@END")]
                    assert loop_n is not None
                    mloop, mend, mfin = marks
                    run_ops(0, mloop)
                    emit_barrier(engine, e, ctx, self.ops[e][mloop][4], False)
                    for g in dyn_pairs[e]:
                        engine.reg_mov(waitregs[(e, g)], 0)
                    with engine.Fori(0, loop_n) as iv:
                        ctx.i = iv
                        run_ops(mloop + 1, mend)
                        emit_barrier(engine, e, ctx, self.ops[e][mend][4], True)
                    ctx.i = None
                    run_ops(mend + 1, mfin)

                getattr(blk, engmap[e])(body)
        return sems


FFN_SZ = 2 * 32 * 1024 + 2 * 32 * 4096 + 8 * 4096
ATT_SZ = 4 * 8 * 1024 + 32 * 1024 + 8 * 4096
CACHE_PP = FFN_SZ + LAY * ATT_SZ


def stack_off(w, l, mi):
    if w == "w1":
        return mi * 2048
    if w == "w2":
        return mi * 2048 + 1024
    if w == "h1":
        return 65536 + mi * 4096
    if w == "h2":
        return 196608 + mi * 4096
    if w == "w3":
        return 327680 + mi * 4096
    base = FFN_SZ + l * ATT_SZ
    if w == "wq":
        return base + mi * 1024
    if w == "wk":
        return base + 8192 + mi * 1024
    if w == "wv":
        return base + 16384 + mi * 1024
    if w == "wo":
        return base + 24576 + mi * 1024
    if w == "wff1":
        return base + 32768 + mi * 1024
    if w == "wff2":
        return base + 65536 + mi * 4096
    raise KeyError(w)


def sv_cols():
    cols = {}
    c = 0
    for l in range(LAY):
        for nm, n in (("bq", 8), ("bk", 8), ("bv", 8), ("bo", 8),
                      ("g1", 8), ("b1", 8), ("g2", 8), ("b2", 8),
                      ("bf1", 32), ("bf2", 8)):
            cols[(nm, l)] = c
            c += n
    for nm, n in (("s1", 32), ("s2", 32), ("sh1", 32), ("sh2", 32), ("s3", 8)):
        cols[(nm, None)] = c
        c += n
    return cols, c


SVC, SV_N = sv_cols()
DVC = {}
_c = 0
for l in range(LAY):
    DVC[("bqs", l)] = _c; _c += 8
    DVC[("bks", l)] = _c; _c += 8
DVC[("s3h", None)] = _c; _c += 8
DV_N = _c


import os as _os
DEV_STAGE = _os.environ.get("K_STAGE", "")


def build_nc(NT=NTILE):
    nc = bass.Bass("TRN2", target_bir_lowering=False)

    xt = nc.dram_tensor("xt", [D * S], F32, kind="ExternalInput")
    yt = nc.dram_tensor("yt", [D * S], F32, kind="ExternalOutput")
    w1t = nc.dram_tensor("w1t", [D, DFF], F32, kind="ExternalInput").ap()
    w2t = nc.dram_tensor("w2t", [D, DFF], F32, kind="ExternalInput").ap()
    h1t = nc.dram_tensor("h1t", [DFF, DFF], F32, kind="ExternalInput").ap()
    h2t = nc.dram_tensor("h2t", [DFF, DFF], F32, kind="ExternalInput").ap()
    w3t = nc.dram_tensor("w3t", [DFF, D], F32, kind="ExternalInput").ap()
    wq = nc.dram_tensor("wq", [LAY, D, D], F32, kind="ExternalInput").ap()
    wk = nc.dram_tensor("wk", [LAY, D, D], F32, kind="ExternalInput").ap()
    wv = nc.dram_tensor("wv", [LAY, D, D], F32, kind="ExternalInput").ap()
    wo = nc.dram_tensor("wo", [LAY, D, D], F32, kind="ExternalInput").ap()
    wff1 = nc.dram_tensor("wff1", [LAY, D, DFF], F32, kind="ExternalInput").ap()
    wff2 = nc.dram_tensor("wff2", [LAY, DFF, D], F32, kind="ExternalInput").ap()
    svin = nc.dram_tensor("svin", [128, SV_N], F32, kind="ExternalInput").ap()
    obdin = nc.dram_tensor("obdin", [128, 64 * LAY], F32, kind="ExternalInput").ap()
    cfin = nc.dram_tensor("cfin", [128, 272], F32, kind="ExternalInput").ap()
    cbin = nc.dram_tensor("cbin", [128, 131], BF16, kind="ExternalInput").ap()
    cache = nc.dram_tensor("wcache", [128 * CACHE_PP], BF16, kind="Internal")

    XT = nc.alloc_sbuf_tensor("XT", [128, FC * 512], F32)
    RS = nc.alloc_sbuf_tensor("RS", [128, FC * 512], F32)
    HS = nc.alloc_sbuf_tensor("HS", [128, FC * 512], F32)
    RB = nc.alloc_sbuf_tensor("RB", [128, FC * 512], BF16)
    HB = nc.alloc_sbuf_tensor("HB", [128, FC * 512], BF16)
    BIGA = nc.alloc_sbuf_tensor("BIGA", [128, MCF * 512], BF16)
    BIGB = nc.alloc_sbuf_tensor("BIGB", [128, MCF * 512], BF16)
    W = [nc.alloc_sbuf_tensor(f"W{i}", [128, CH], BF16) for i in range(NSLOT)]
    KT = nc.alloc_sbuf_tensor("KT", [128, FC * 128], BF16)
    VT = nc.alloc_sbuf_tensor("VT", [128, FC * 130], BF16)
    S_sb = nc.alloc_sbuf_tensor("S_sb", [128, LAY * FC * 65], F32)
    S_bf = nc.alloc_sbuf_tensor("S_bf", [128, FC * 65], BF16)
    ATS = nc.alloc_sbuf_tensor("ATS", [128, 16 * 128], BF16)
    SQA = nc.alloc_sbuf_tensor("SQA", [128, 4 * 512], BF16)
    T32 = nc.alloc_sbuf_tensor("T32", [128, 2 * 512], F32)
    RSB = nc.alloc_sbuf_tensor("RSB", [128, 512], F32)
    ROWS = nc.alloc_sbuf_tensor("ROWS", [1, 1536], F32)
    WROW = nc.alloc_sbuf_tensor("WROW", [2, FC * 512], BF16)
    WK = nc.alloc_sbuf_tensor("WK", [128, FC * 2], F32)
    ATM = nc.alloc_sbuf_tensor("ATM", [128, FC * 128], BF16)
    RD2 = nc.alloc_sbuf_tensor("RD2", [128, FC * 4], F32)
    SV = nc.alloc_sbuf_tensor("SV", [128, SV_N], F32)
    DV = nc.alloc_sbuf_tensor("DV", [128, DV_N], F32)
    OBD = nc.alloc_sbuf_tensor("OBD", [128, 64 * LAY], F32)
    OM2 = nc.alloc_sbuf_tensor("OM2", [128, 128 * LAY], BF16)
    CF = nc.alloc_sbuf_tensor("CF", [128, 272], F32)
    CB = nc.alloc_sbuf_tensor("CB", [128, 131], BF16)
    CAST = [nc.alloc_sbuf_tensor(f"CAST{i}", [128, 1024], BF16) for i in range(4)]
    FP = nc.alloc_sbuf_tensor("FP", [128, 1], F32)
    FA = nc.alloc_sbuf_tensor("FA", [1, 2], F32)
    FD = nc.alloc_sbuf_tensor("FD", [1, 2], F32)

    pbt = [nc.alloc_psum_tensor(f"pb{i}", [128, 512], F32) for i in range(6)]
    pb = pbt
    pt = [nc.alloc_psum_tensor(f"pt{i}", [128, 1024], BF16) for i in range(2)]

    P = Prog()

    def sv_ap(nm, l, mi, n=1):
        return SV[:, SVC[(nm, l)] + mi: SVC[(nm, l)] + mi + n]

    def dv_ap(nm, l, mi):
        return DV[:, DVC[(nm, l)] + mi: DVC[(nm, l)] + mi + 1]

    for j, v in enumerate(CONSTVALS):
        nc.const_aps.register(F32, v, CF[:, 256 + j:257 + j])

    ONES_F = CF[0:1, 0:128]
    TRIU = CF[:, 0:128]
    E2 = CF[0:2, 128:256]
    IDENT = CB[:, 0:128]
    ONESCOL = CB[:, 128:129]
    BDONES = CB[:, 129:131]

    P.emit("pool", "dma_start", SV[:, :], svin, dma="cl", commit=False)
    P.emit("pool", "dma_start", OBD[:, :], obdin, dma="cl", commit=False)
    P.emit("pool", "dma_start", CF[:, :], cfin, dma="cl", commit=False)
    P.emit("pool", "dma_start", CB[:, :], cbin, dma="cl", commit=False)
    P.commit_dma("cl", writes=[("SV",), ("OBD",), ("CF",), ("CB",)])

    for l in range(LAY):
        P.emit("dve", "tensor_scalar_mul", DV[:, DVC[("bqs", l)]:DVC[("bqs", l)] + 8],
               sv_ap("bq", l, 0, 8), DHS, reads=[("SV",)], writes=[("DV",)],
               force_inc=True)
        P.emit("dve", "tensor_scalar_mul", DV[:, DVC[("bks", l)]:DVC[("bks", l)] + 8],
               sv_ap("bk", l, 0, 8), DHS, reads=[("SV",)], writes=[("DV",)],
               force_inc=True)
    P.emit("dve", "tensor_scalar_mul", DV[:, DVC[("s3h", None)]:DVC[("s3h", None)] + 8],
           sv_ap("s3", None, 0, 8), 0.5, reads=[("SV",)], writes=[("DV",)],
           force_inc=True)

    for l in range(LAY):
        ob = OBD[:, l * 64: l * 64 + 64]
        o2 = 128 * l
        P.emit("act", "activation", OM2[:, o2:o2 + 32], ob[:, 0:32], AF.Identity,
               reads=[("OBD",)], writes=[("OM2",)], force_inc=True)
        P.emit("act", "activation", OM2[:, o2 + 32:o2 + 64], ob[:, 0:32], AF.Identity,
               scale=-1.0, reads=[("OBD",)], writes=[("OM2",)], force_inc=True)
        P.emit("act", "activation", OM2[:, o2 + 64:o2 + 96], ob[:, 32:64], AF.Identity,
               reads=[("OBD",)], writes=[("OM2",)], force_inc=True)
        P.emit("act", "activation", OM2[:, o2 + 96:o2 + 128], ob[:, 32:64], AF.Identity,
               scale=-1.0, reads=[("OBD",)], writes=[("OM2",)], force_inc=True)

    P.emit("dve", "memset", S_sb[:, :], 0.0, writes=[("S", 0), ("S", 1), ("S", 2)], force_inc=True)
    for fc in range(FC):
        P.emit("dve", "memset", VT[:, fc * 130 + 64: fc * 130 + 65], 1.0,
               writes=[("VTones", fc)], force_inc=True)
        P.emit("dve", "memset", VT[:, fc * 130 + 129: fc * 130 + 130], 1.0,
               writes=[("VTones2", fc)], force_inc=True)
    P.emit("act", "memzero", FA[:, :], force_inc=True)
    P.emit("dve", "memset", FD[:, :], 0.0, force_inc=True)
    P.emit("pool", "memset", FP[:, :], 0.0, force_inc=True)

    def prepass():
        cnt = [0]

        def one(w, l, K, M, src3):
            pitch = {"w1": 2048, "w2": 2048}.get(w, K)
            for ki in range(K // 128):
                for mb in range(0, M, 1024):
                    slot = cnt[0] % 4
                    cnt[0] += 1
                    if src3 is None:
                        blk_src = {"w1": w1t, "w2": w2t, "h1": h1t,
                                   "h2": h2t, "w3": w3t}[w][
                                       ki * 128:(ki + 1) * 128, mb:mb + 1024]
                    else:
                        blk_src = src3[l, ki * 128:(ki + 1) * 128, mb:mb + 1024]
                    P.emit("pool", "dma_start", CAST[slot][:, 0:1024], blk_src,
                           dma=("pl", slot), writes=[("CAST", slot)])
                    nmi = min(8, M // 128 - mb // 128)
                    out = bass.AP(cache,
                                  stack_off(w, l, mb // 128) + ki * 128,
                                  [[CACHE_PP, 128], [pitch, nmi], [1, 128]])
                    src = CAST[slot].ap()[:, 0:1024].rearrange(
                        "p (m e) -> p m e", m=8)[:, 0:nmi, :]
                    P.emit("sp", "dma_start", out, src, dma=("ps", slot),
                           reads=[("CAST", slot)])

        one("w1", None, D, DFF, None)
        one("w2", None, D, DFF, None)
        one("h1", None, DFF, DFF, None)
        one("h2", None, DFF, DFF, None)
        one("w3", None, DFF, D, None)
        for l in range(LAY):
            one("wq", l, D, D, wq)
            one("wk", l, D, D, wk)
            one("wv", l, D, D, wv)
            one("wo", l, D, D, wo)
            one("wff1", l, D, DFF, wff1)
            one("wff2", l, DFF, D, wff2)

    prepass()

    P.begin_loop(NT)

    seq = (list(range(0, 176)) + list(range(176, 320)) + list(range(0, 176)))
    st = {"spos": 0, "issued": -1, "slot_chunk": [None] * NSLOT}

    def ensure(tpos):
        tgt = min(tpos + NSLOT - 2, len(seq) - 1)
        while st["issued"] < tgt:
            c = st["issued"] + 1
            slot = c % NSLOT
            cid = seq[c]
            src = bass.AP(cache, cid * CH, [[CACHE_PP, 128], [1, CH]])
            P.emit("sp", "dma_start", W[slot][:, :], src, dma=("rg", slot),
                   writes=[("W", slot)])
            st["issued"] = c
            st["slot_chunk"][slot] = c

    def next_stack(w, l, mi, K):
        assert st["spos"] % 1024 == 0
        exp = stack_off(w, l, mi)
        pieces = []
        for ki in range(K // 128):
            pos = st["spos"] + ki * 128
            tpos = pos // CH
            ensure(tpos)
            slot = tpos % NSLOT
            off = pos % CH
            assert seq[tpos] * CH + off == exp + ki * 128, (w, l, mi, ki)
            assert st["slot_chunk"][slot] == tpos,                 ("slot retargeted before consumption", w, l, mi, ki)
            pieces.append((W[slot], ("W", slot), off))
        st["spos"] += K
        return pieces

    rot = [0]

    def bank():
        b = rot[0] % 4
        rot[0] += 1
        return b

    def x_ap(fc, out=False):
        tns = yt if out else xt
        def mk(ctx, fc=fc, tns=tns):
            return bass.AP(tns, ctx.i * 512 + fc * 128 * S, [[S, 128], [1, 512]])
        return mk

    XSL = [XT[:, fc * 512:(fc + 1) * 512] for fc in range(FC)]
    RSL = [RS[:, fc * 512:(fc + 1) * 512] for fc in range(FC)]
    HSL = [HS[:, fc * 512:(fc + 1) * 512] for fc in range(FC)]
    RBL = [RB[:, fc * 512:(fc + 1) * 512] for fc in range(FC)]
    HBL = [HB[:, fc * 512:(fc + 1) * 512] for fc in range(FC)]
    AL = [BIGA[:, m * 512:(m + 1) * 512] for m in range(MCF)]
    BL = [BIGB[:, m * 512:(m + 1) * 512] for m in range(MCF)]

    def matmul_stack(w, l, mi, K, rhs_slices, rhs_keys, bi, extra_reads=(),
                     pieces=None):
        if pieces is None:
            pieces = next_stack(w, l, mi, K)
        n = len(pieces)
        for ki, (wt, wkey, off) in enumerate(pieces):
            P.emit("pe", "matmul", pb[bi][:, :], wt[:, off:off + 128],
                   rhs_slices[ki], start=(ki == 0), stop=(ki == n - 1),
                   reads=[wkey, rhs_keys[ki]] + list(extra_reads),
                   writes=[("pb", bi)])

    def bundle(stacks, banks):
        rd = set()
        for pieces in stacks:
            for _, wkey, _ in pieces:
                rd.add(wkey)
        P.emit("pe", "nop", reads=sorted(rd, key=repr),
               writes=[("pb", b) for b in banks])

    def emit_ln(src_slices, src_keys, gamma=None, beta=None,
                out_f32=None, out_f32_keys=None, out_bf=None, out_bf_keys=None):
        def _mms(fc):
            r = fc % 2
            P.emit("pe", "matmul", pb[4][0:1, 0:512], ONESCOL,
                   SQA[:, r * 512:(r + 1) * 512], start=(fc == 0), stop=(fc == 7),
                   reads=[("SQA", r), ("CB",)], writes=[("pb", 4)])
            P.emit("pe", "matmul", pb[5][0:1, 0:512], ONESCOL,
                   SQA[:, (2 + r) * 512:(3 + r) * 512], start=(fc == 0),
                   stop=(fc == 7), reads=[("SQB", r), ("CB",)], writes=[("pb", 5)])
        for fc in range(FC):
            r = fc % 2
            P.emit("act", "copy", SQA[:, r * 512:(r + 1) * 512], src_slices[fc],
                   reads=[src_keys[fc]], writes=[("SQA", r)])
            P.emit("dve", "tensor_mul", SQA[:, (2 + r) * 512:(3 + r) * 512],
                   SQA[:, r * 512:(r + 1) * 512], SQA[:, r * 512:(r + 1) * 512],
                   reads=[("SQA", r)], writes=[("SQB", r)])
            if fc >= 1:
                _mms(fc - 1)
        _mms(FC - 1)
        mu = ROWS[0:1, 0:512]
        ms = ROWS[0:1, 512:1024]
        tmp = ROWS[0:1, 1024:1536]
        P.emit("act", "activation", mu, pb[4][0:1, 0:512], AF.Identity, scale=1.0 / D,
               reads=[("pb", 4)], writes=[("mu",)])
        P.emit("act", "activation", ms, pb[5][0:1, 0:512], AF.Identity, scale=1.0 / D,
               reads=[("pb", 5)], writes=[("ms",)])
        P.emit("dve", "tensor_mul", tmp, mu, mu, reads=[("mu",)], writes=[("tmp",)])
        P.emit("dve", "tensor_sub", ms, ms, tmp, reads=[("ms",), ("tmp",)],
               writes=[("ms",)])
        P.emit("act", "activation", tmp, ms, AF.Ln, bias=LN_EPS,
               reads=[("ms",)], writes=[("tmp",)])
        P.emit("act", "activation", tmp, tmp, AF.Exp, scale=-0.5,
               reads=[("tmp",)], writes=[("tmp",)])
        P.emit("pe", "matmul", pb[4][:, :], ONES_F, mu, start=True, stop=True,
               reads=[("mu",), ("CF",)], writes=[("pb", 4)])
        P.emit("pe", "matmul", pb[5][:, :], ONES_F, tmp, start=True, stop=True,
               reads=[("tmp",), ("CF",)], writes=[("pb", 5)])
        P.emit("act", "copy", RSB[:, :], pb[5][:, :], reads=[("pb", 5)],
               writes=[("RSB",)])
        for fc in range(FC):
            r = fc % 2
            t32 = T32[:, r * 512:(r + 1) * 512]
            P.emit("dve", "tensor_sub", t32, src_slices[fc], pb[4][:, :],
                   reads=[src_keys[fc], ("pb", 4)], writes=[("T32", r)])
            if gamma is None:
                if out_f32 is not None:
                    P.emit("dve", "tensor_mul", out_f32[fc], t32, RSB[:, :],
                           reads=[("T32", r), ("RSB",)], writes=[out_f32_keys[fc]])
                    P.emit("act", "copy", out_bf[fc], out_f32[fc],
                           reads=[out_f32_keys[fc]], writes=[out_bf_keys[fc]])
                else:
                    P.emit("dve", "tensor_mul", out_bf[fc], t32, RSB[:, :],
                           reads=[("T32", r), ("RSB",)], writes=[out_bf_keys[fc]])
            else:
                P.emit("dve", "tensor_mul", t32, t32, RSB[:, :],
                       reads=[("T32", r), ("RSB",)], writes=[("T32", r)])
                P.emit("act", "activation", out_f32[fc], t32, AF.Identity,
                       scale=sv_ap(gamma[0], gamma[1], fc),
                       bias=sv_ap(beta[0], beta[1], fc),
                       reads=[("T32", r), ("SV",)], writes=[out_f32_keys[fc]])
                P.emit("act", "copy", out_bf[fc], out_f32[fc],
                       reads=[out_f32_keys[fc]], writes=[out_bf_keys[fc]])

    def emit_ffn():
        for mi in range(MCF):
            p1 = next_stack("w1", None, mi, D)
            p2 = next_stack("w2", None, mi, D)
            b1 = bank()
            b2 = bank()
            bundle([p1, p2], [b1, b2])
            matmul_stack("w1", None, mi, D, RBL, [("RB", k) for k in range(FC)], b1,
                         pieces=p1)
            matmul_stack("w2", None, mi, D, RBL, [("RB", k) for k in range(FC)], b2,
                         pieces=p2)
            P.emit("act", "activation", AL[mi], pb[b1][:, :], AF.Silu,
                   scale=sv_ap("s1", None, mi), reads=[("pb", b1), ("SV",)],
                   writes=[("A", mi)])
            P.emit("act", "activation", BL[mi], pb[b2][:, :], AF.Identity,
                   scale=sv_ap("s2", None, mi), reads=[("pb", b2), ("SV",)],
                   writes=[("B", mi)])
            P.emit("dve", "tensor_mul", AL[mi], AL[mi], BL[mi],
                   reads=[("A", mi), ("B", mi)], writes=[("A", mi)])
        for mi in range(MCF):
            pcs = next_stack("h1", None, mi, DFF)
            bi = bank()
            bundle([pcs], [bi])
            matmul_stack("h1", None, mi, DFF, AL, [("A", k) for k in range(MCF)], bi, pieces=pcs)
            P.emit("act", "activation", BL[mi], pb[bi][:, :], AF.Gelu,
                   scale=sv_ap("sh1", None, mi), reads=[("pb", bi), ("SV",)],
                   writes=[("B", mi)])
        for mi in range(MCF):
            pcs = next_stack("h2", None, mi, DFF)
            bi = bank()
            bundle([pcs], [bi])
            matmul_stack("h2", None, mi, DFF, BL, [("B", k) for k in range(MCF)], bi, pieces=pcs)
            P.emit("act", "activation", AL[mi], pb[bi][:, :], AF.Gelu,
                   scale=sv_ap("sh2", None, mi), reads=[("pb", bi), ("SV",)],
                   writes=[("A", mi)])
        for mi in range(FC):
            bi = bank()
            matmul_stack("w3", None, mi, DFF, AL, [("A", k) for k in range(MCF)], bi)
            r = mi % 2
            t32 = T32[:, r * 512:(r + 1) * 512]
            P.emit("act", "activation", t32, pb[bi][:, :], AF.Identity,
                   scale=dv_ap("s3h", None, mi), reads=[("pb", bi), ("DV",)],
                   writes=[("T32", r)])
            P.emit("dve", "tensor_add", XSL[mi], XSL[mi], t32,
                   reads=[("XT", mi), ("T32", r)], writes=[("XT", mi)])

    def emit_favor(l, base, side):
        if side == "k":
            for fc in range(FC):
                r = fc % 4
                P.emit("act", "activation", SQA[:, r * 512:(r + 1) * 512],
                       BL[base + fc], AF.Square,
                       reads=[("B", base + fc)], writes=[("SQA", r)])
                if fc >= 3:
                    f2 = fc - 3
                    r2 = f2 % 4
                    P.emit("pe", "matmul", pb[4 + f2 % 2][0:2, 0:512], BDONES,
                           SQA[:, r2 * 512:(r2 + 1) * 512], start=True, stop=True,
                           reads=[("SQA", r2), ("CB",)], writes=[("pb", 4 + f2 % 2)])
                    P.emit("act", "activation",
                           WROW[0:2, f2 * 512:(f2 + 1) * 512],
                           pb[4 + f2 % 2][0:2, 0:512], AF.Exp, scale=-0.5, bias=LNC,
                           reads=[("pb", 4 + f2 % 2)], writes=[("WR", f2)])
            for f2 in range(FC - 3, FC):
                r2 = f2 % 4
                P.emit("pe", "matmul", pb[4 + f2 % 2][0:2, 0:512], BDONES,
                       SQA[:, r2 * 512:(r2 + 1) * 512], start=True, stop=True,
                       reads=[("SQA", r2), ("CB",)], writes=[("pb", 4 + f2 % 2)])
                P.emit("act", "activation", WROW[0:2, f2 * 512:(f2 + 1) * 512],
                       pb[4 + f2 % 2][0:2, 0:512], AF.Exp, scale=-0.5, bias=LNC,
                       reads=[("pb", 4 + f2 % 2)], writes=[("WR", f2)])
        for fc in range(FC):
            bu = fc % 4
            P.emit("pe", "matmul", pb[bu][:, :], OM2[:, l * 128:(l + 1) * 128],
                   BL[base + fc], start=True, stop=True,
                   reads=[("B", base + fc), ("OM2",)], writes=[("pb", bu)])
            if fc >= 3:
                f2 = fc - 3
                P.emit("act", "activation", BL[base + f2], pb[f2 % 4][:, :],
                       AF.Exp, reads=[("pb", f2 % 4)], writes=[("B", base + f2)])
        for f2 in range(FC - 3, FC):
            P.emit("act", "activation", BL[base + f2], pb[f2 % 4][:, :],
                   AF.Exp, reads=[("pb", f2 % 4)], writes=[("B", base + f2)])

    def emit_la_chunk(l, tc):
        ts = tc * 128
        lb = l * FC * 65
        P.emit("act", "copy", S_bf[:, :], S_sb[:, lb:lb + FC * 65],
               reads=[("S", l)], writes=[("Sbf",)])
        for fc in range(FC):
            p = fc % 2
            kbuf = BL[8 + fc]
            vbuf = BL[16 + fc]
            P.emit("pe", "transpose", pt[p][:, 384:386],
                   WROW[0:2, fc * 512 + ts: fc * 512 + ts + 128], IDENT[0:2, 0:2],
                   reads=[("WR", fc), ("CB",)], writes=[("pt", p)])
            P.emit("pe", "transpose", pt[p][:, 0:128], kbuf[:, ts:ts + 128],
                   IDENT, reads=[("B", 8 + fc), ("CB",)], writes=[("pt", p)])
            P.emit("pe", "transpose", pt[p][:, 128:256], vbuf[:, ts:ts + 128],
                   IDENT, reads=[("B", 16 + fc), ("CB",)], writes=[("pt", p)])
            P.emit("dve", "tensor_copy", WK[:, fc * 2:(fc + 1) * 2],
                   pt[p][:, 384:386], reads=[("pt", p)], writes=[("WK", fc)])
            P.emit("dve", "tensor_scalar_mul", KT[:, fc * 128:fc * 128 + 64],
                   pt[p][:, 0:64], WK[:, fc * 2:fc * 2 + 1],
                   reads=[("pt", p), ("WK", fc)], writes=[("KT", fc)])
            P.emit("dve", "tensor_scalar_mul", KT[:, fc * 128 + 64:fc * 128 + 128],
                   pt[p][:, 64:128], WK[:, fc * 2 + 1:fc * 2 + 2],
                   reads=[("pt", p), ("WK", fc)], writes=[("KT", fc)])
            vt_dst = VT.ap()[:, fc * 130: fc * 130 + 130].rearrange(
                "p (s c) -> p s c", s=2, c=65)[:, :, 0:64]
            P.emit("dve", "tensor_copy", vt_dst,
                   pt[p][:, 128:256].rearrange("p (s c) -> p s c", s=2, c=64),
                   reads=[("pt", p)], writes=[("VT", fc)])
        for fc in range(FC):
            p = fc % 2
            bAe, bAo = 2 * p, 2 * p + 1
            qb = BL[0 + fc]
            qk = ("B", 0 + fc)
            kb = BL[8 + fc]
            kk = ("B", 8 + fc)
            P.emit("pe", "matmul", pb[bAe][:, 0:128], kb[0:64, ts:ts + 128],
                   qb[0:64, ts:ts + 128], start=True, stop=True,
                   reads=[kk, qk], writes=[("pb", bAe)])
            P.emit("pe", "matmul", pb[bAo][:, 0:128], kb[64:128, ts:ts + 128],
                   qb[64:128, ts:ts + 128], start=True, stop=True,
                   reads=[kk, qk], writes=[("pb", bAo)])
            P.emit("dve", "scalar_tensor_tensor", ATS[:, (2 * fc) * 128:(2 * fc + 1) * 128],
                   pb[bAe][:, 0:128], WK[:, fc * 2:fc * 2 + 1], TRIU,
                   ALU.mult, ALU.mult,
                   reads=[("pb", bAe), ("WK", fc), ("CF",)], writes=[("ATS", 2 * fc)])
            P.emit("dve", "scalar_tensor_tensor", ATS[:, (2 * fc + 1) * 128:(2 * fc + 2) * 128],
                   pb[bAo][:, 0:128], WK[:, fc * 2 + 1:fc * 2 + 2], TRIU,
                   ALU.mult, ALU.mult,
                   reads=[("pb", bAo), ("WK", fc), ("CF",)], writes=[("ATS", 2 * fc + 1)])
        for fc in range(FC):
            p = fc % 2
            bO = 4 + p
            qb = BL[0 + fc]
            qk = ("B", 0 + fc)
            P.emit("pe", "matmul", pb[bO][:, 0:65],
                   qb[0:64, ts:ts + 128], S_bf[0:64, fc * 65:(fc + 1) * 65],
                   start=True, stop=False, reads=[qk, ("Sbf",)],
                   writes=[("pb", bO)])
            P.emit("pe", "matmul", pb[bO][:, 0:65],
                   ATS[:, (2 * fc) * 128:(2 * fc + 1) * 128],
                   VT[:, fc * 130: fc * 130 + 65],
                   start=False, stop=True,
                   reads=[("ATS", 2 * fc), ("VT", fc), ("VTones", fc)],
                   writes=[("pb", bO)])
            P.emit("pe", "matmul", pb[bO][:, 65:130],
                   qb[64:128, ts:ts + 128], S_bf[64:128, fc * 65:(fc + 1) * 65],
                   start=True, stop=False, reads=[qk, ("Sbf",)],
                   writes=[("pb", bO)])
            P.emit("pe", "matmul", pb[bO][:, 65:130],
                   ATS[:, (2 * fc + 1) * 128:(2 * fc + 2) * 128],
                   VT[:, fc * 130 + 65: fc * 130 + 130],
                   start=False, stop=True,
                   reads=[("ATS", 2 * fc + 1), ("VT", fc), ("VTones2", fc)],
                   writes=[("pb", bO)])
            dens = bass.AP(pbt[bO], 64, [[512, 128], [65, 2]])
            P.emit("act", "activation", RD2[:, fc * 4: fc * 4 + 2], dens,
                   AF.Identity, bias=ATTN_EPS, reads=[("pb", bO)],
                   writes=[("RDr", fc)])
            P.emit("dve", "reciprocal", RD2[:, fc * 4 + 2: fc * 4 + 4],
                   RD2[:, fc * 4: fc * 4 + 2], reads=[("RDr", fc)],
                   writes=[("RD", fc)])
            P.emit("act", "activation", ATM[:, fc * 128: fc * 128 + 64],
                   pb[bO][:, 0:64], AF.Identity,
                   scale=RD2[:, fc * 4 + 2: fc * 4 + 3],
                   reads=[("pb", bO), ("RD", fc)], writes=[("ATM", fc)])
            P.emit("act", "activation", ATM[:, fc * 128 + 64: fc * 128 + 128],
                   pb[bO][:, 65:129], AF.Identity,
                   scale=RD2[:, fc * 4 + 3: fc * 4 + 4],
                   reads=[("pb", bO), ("RD", fc)], writes=[("ATM", fc)])
        for fc in range(FC):
            p = fc % 2
            bAo = 2 * p + 1
            P.emit("pe", "matmul", pb[bAo][0:64, 128:193],
                   KT[:, fc * 128: fc * 128 + 64],
                   VT[:, fc * 130: fc * 130 + 65], start=True, stop=True,
                   reads=[("KT", fc), ("VT", fc), ("VTones", fc)],
                   writes=[("pb", bAo)])
            P.emit("pe", "matmul", pb[bAo][64:128, 128:193],
                   KT[:, fc * 128 + 64: fc * 128 + 128],
                   VT[:, fc * 130 + 65: fc * 130 + 130], start=True, stop=True,
                   tile_position=(0, 64),
                   reads=[("KT", fc), ("VT", fc), ("VTones2", fc)],
                   writes=[("pb", bAo)])
            P.emit("dve", "tensor_add", S_sb[:, lb + fc * 65:lb + (fc + 1) * 65],
                   S_sb[:, lb + fc * 65:lb + (fc + 1) * 65], pb[bAo][:, 128:193],
                   reads=[("S", l), ("pb", bAo)], writes=[("S", l)])
            P.emit("pe", "transpose", pt[p][:, 256:384],
                   ATM[:, fc * 128:(fc + 1) * 128], IDENT,
                   reads=[("ATM", fc), ("CB",)], writes=[("pt", p)])
            P.emit("dve", "tensor_copy", BL[24 + fc][:, ts:ts + 128],
                   pt[p][:, 256:384], reads=[("pt", p)], writes=[("B", 24 + fc)])

    def emit_attn(l):
        for (wname, base, scale, bias_nm, bias_dv) in (
                ("wq", 0, DHS, "bqs", True), ("wk", 8, DHS, "bks", True),
                ("wv", 16, 1.0, "bv", False)):
            for mi0 in range(0, FC, 2):
                ps2 = [next_stack(wname, l, mi0, D),
                       next_stack(wname, l, mi0 + 1, D)]
                bs2 = [bank(), bank()]
                bundle(ps2, bs2)
                for j, mi in enumerate((mi0, mi0 + 1)):
                    matmul_stack(wname, l, mi, D, RBL,
                                 [("RB", k) for k in range(FC)], bs2[j],
                                 pieces=ps2[j])
                    b_ap = (dv_ap(bias_nm, l, mi) if bias_dv
                            else sv_ap(bias_nm, l, mi))
                    P.emit("act", "activation", BL[base + mi], pb[bs2[j]][:, :],
                           AF.Identity, scale=scale, bias=b_ap,
                           reads=[("pb", bs2[j]), ("DV",) if bias_dv else ("SV",)],
                           writes=[("B", base + mi)])
        emit_favor(l, 0, "q")
        emit_favor(l, 8, "k")
        for tc in range(TC):
            emit_la_chunk(l, tc)
        for mi in range(FC):
            pcs = next_stack("wo", l, mi, D)
            bi = bank()
            bundle([pcs], [bi])
            matmul_stack("wo", l, mi, D, [BL[24 + k] for k in range(FC)],
                         [("B", 24 + k) for k in range(FC)], bi, pieces=pcs)
            r = mi % 2
            t32 = T32[:, r * 512:(r + 1) * 512]
            P.emit("act", "activation", t32, pb[bi][:, :], AF.Identity,
                   bias=sv_ap("bo", l, mi), reads=[("pb", bi), ("SV",)],
                   writes=[("T32", r)])
            P.emit("dve", "tensor_add", HSL[mi], RSL[mi], t32,
                   reads=[("RS", mi), ("T32", r)], writes=[("HS", mi)])
        if DEV_STAGE == f"wo{l}":
            return
        emit_ln(HSL, [("HS", k) for k in range(FC)], gamma=("g1", l),
                beta=("b1", l), out_f32=HSL,
                out_f32_keys=[("HS", k) for k in range(FC)],
                out_bf=HBL, out_bf_keys=[("HB", k) for k in range(FC)])
        if DEV_STAGE == f"ln1_{l}":
            return
        for mi0 in range(0, MCF, 2):
            ps2 = [next_stack("wff1", l, mi0, D),
                   next_stack("wff1", l, mi0 + 1, D)]
            bs2 = [bank(), bank()]
            bundle(ps2, bs2)
            for j, mi in enumerate((mi0, mi0 + 1)):
                matmul_stack("wff1", l, mi, D, HBL,
                             [("HB", k) for k in range(FC)], bs2[j], pieces=ps2[j])
                P.emit("act", "activation", AL[mi], pb[bs2[j]][:, :], AF.Relu,
                       bias=sv_ap("bf1", l, mi), reads=[("pb", bs2[j]), ("SV",)],
                       writes=[("A", mi)])
        for mi in range(FC):
            bi = bank()
            matmul_stack("wff2", l, mi, DFF, AL, [("A", k) for k in range(MCF)], bi)
            r = mi % 2
            t32 = T32[:, r * 512:(r + 1) * 512]
            P.emit("act", "activation", t32, pb[bi][:, :], AF.Identity,
                   bias=sv_ap("bf2", l, mi), reads=[("pb", bi), ("SV",)],
                   writes=[("T32", r)])
            P.emit("dve", "tensor_add", HSL[mi], HSL[mi], t32,
                   reads=[("HS", mi), ("T32", r)], writes=[("HS", mi)])
        if DEV_STAGE == f"y{l}":
            return
        emit_ln(HSL, [("HS", k) for k in range(FC)], gamma=("g2", l),
                beta=("b2", l), out_f32=RSL,
                out_f32_keys=[("RS", k) for k in range(FC)],
                out_bf=RBL, out_bf_keys=[("RB", k) for k in range(FC)])

    for fc in range(FC):
        P.emit("pool", "dma_start", XSL[fc], x_ap(fc), dma=("xl", fc),
               writes=[("XT", fc)])

    def store(slices, keys):
        for fc in range(FC):
            P.emit("pool", "dma_start", x_ap(fc, out=True), slices[fc],
                   dma="yo", reads=[keys[fc]], writes=[("yo", fc)])

    done = False
    while True:
        emit_ln(XSL, [("XT", k) for k in range(FC)], out_bf=RBL,
                out_bf_keys=[("RB", k) for k in range(FC)])
        emit_ffn()
        if DEV_STAGE == "ffn1":
            store(XSL, [("XT", k) for k in range(FC)]); break
        emit_ln(XSL, [("XT", k) for k in range(FC)], out_f32=RSL,
                out_f32_keys=[("RS", k) for k in range(FC)], out_bf=RBL,
                out_bf_keys=[("RB", k) for k in range(FC)])
        if DEV_STAGE == "preln":
            store(RSL, [("RS", k) for k in range(FC)]); break
        brk = False
        for l in range(LAY):
            emit_attn(l)
            if DEV_STAGE in (f"l{l}", f"wo{l}", f"ln1_{l}", f"y{l}"):
                brk = True
                break
        if brk:
            if DEV_STAGE.startswith("l") and not DEV_STAGE.startswith("ln1") and not DEV_STAGE.startswith("y"):
                store(RSL, [("RS", k) for k in range(FC)])
            else:
                store(HSL, [("HS", k) for k in range(FC)])
            break
        for fc in range(FC):
            P.emit("dve", "tensor_add", XSL[fc], XSL[fc], RSL[fc],
                   reads=[("XT", fc), ("RS", fc)], writes=[("XT", fc)])
        if DEV_STAGE == "xadd":
            store(XSL, [("XT", k) for k in range(FC)]); break
        emit_ln(XSL, [("XT", k) for k in range(FC)], out_bf=RBL,
                out_bf_keys=[("RB", k) for k in range(FC)])
        emit_ffn()
        store(XSL, [("XT", k) for k in range(FC)])
        done = True
        break

    if done:
        assert st["spos"] == len(seq) * CH, st["spos"]
    P.end_loop()
    P.finalize(nc, fences={"pool": FP[:, :], "act": FA[0:1, 0:1],
                           "dve": FD[0:1, 0:1]})
    n_ops = sum(len(P.ops[e]) for e in ENGINES)
    return nc, n_ops, P.n_waits


def host_prepare(inputs):
    f = np.ascontiguousarray

    def svblock(v):
        v = np.asarray(v)
        return f(v.reshape(-1, 128).T)

    sv = np.zeros((128, SV_N), np.float32)
    for l in range(LAY):
        for nm, src in (("bq", inputs["L0_attn_bq"][l]),
                        ("bk", inputs["L0_attn_bk"][l]),
                        ("bv", inputs["L0_attn_bv"][l]),
                        ("bo", inputs["L0_attn_bo"][l]),
                        ("g1", inputs["L0_attn_ln1_g"][l]),
                        ("b1", inputs["L0_attn_ln1_b"][l]),
                        ("g2", inputs["L0_attn_ln2_g"][l]),
                        ("b2", inputs["L0_attn_ln2_b"][l]),
                        ("bf1", inputs["L0_attn_bf1"][l]),
                        ("bf2", inputs["L0_attn_bf2"][l])):
            blk = svblock(src)
            sv[:, SVC[(nm, l)]:SVC[(nm, l)] + blk.shape[1]] = blk
    for nm, src in (("s1", inputs["L0_ffn_w1_s"]), ("s2", inputs["L0_ffn_w2_s"]),
                    ("sh1", inputs["L0_ffn_h1_s"]), ("sh2", inputs["L0_ffn_h2_s"]),
                    ("s3", inputs["L0_ffn_w3_s"])):
        blk = svblock(src)
        sv[:, SVC[(nm, None)]:SVC[(nm, None)] + blk.shape[1]] = blk

    obd = np.zeros((128, 64 * LAY), np.float32)
    om = np.asarray(inputs["L0_attn_omega"])
    for l in range(LAY):
        obd[0:64, l * 64:l * 64 + 32] = om[l]
        obd[64:128, l * 64 + 32:l * 64 + 64] = om[l]

    cf = np.zeros((128, 272), np.float32)
    cf[:, 0:128] = np.triu(np.ones((128, 128), np.float32))
    cf[0, 128:192] = 1.0
    cf[1, 192:256] = 1.0
    cf[:, 256:256 + len(CONSTVALS)] = np.asarray(CONSTVALS, np.float32)[None, :]
    cb = np.zeros((128, 131), np.float32)
    cb[:, 0:128] = np.eye(128, dtype=np.float32)
    cb[:, 128] = 1.0
    cb[0:64, 129] = 1.0
    cb[64:128, 130] = 1.0
    cb = cb.astype(ml_dtypes.bfloat16)

    shared = dict(
        w1t=f(np.asarray(inputs["L0_ffn_w1_w"]).T),
        w2t=f(np.asarray(inputs["L0_ffn_w2_w"]).T),
        h1t=f(np.asarray(inputs["L0_ffn_h1_w"]).T),
        h2t=f(np.asarray(inputs["L0_ffn_h2_w"]).T),
        w3t=f(np.asarray(inputs["L0_ffn_w3_w"]).T),
        wq=f(np.asarray(inputs["L0_attn_Wq"])),
        wk=f(np.asarray(inputs["L0_attn_Wk"])),
        wv=f(np.asarray(inputs["L0_attn_Wv"])),
        wo=f(np.asarray(inputs["L0_attn_Wo"])),
        wff1=f(np.asarray(inputs["L0_attn_W1"])),
        wff2=f(np.asarray(inputs["L0_attn_W2"])),
        svin=sv, obdin=obd, cfin=cf, cbin=cb,
    )
    x = np.asarray(inputs["L0_attn_x"])
    xts = [f(x[c].T).reshape(-1) for c in range(B)]
    return shared, xts


_CACHE = {}


def kernel(**inputs):
    shared, xts = host_prepare(inputs)
    if "nc" not in _CACHE:
        nc, n_ops, n_waits = build_nc(NTILE)
        _CACHE["nc"] = nc
    nc = _CACHE["nc"]
    in_maps = [dict(shared, xt=xts[c]) for c in range(B)]
    res = run_bass_kernel_spmd(nc, in_maps, core_ids=list(range(B)))
    out = np.empty((B, S, D), np.float32)
    for c in range(B):
        out[c] = res.results[c]["yt"].reshape(D, S).T
    return out


if __name__ == "__main__":
    nc, n_ops, n_waits = build_nc(1)
    print(f"ops={n_ops} waits={n_waits}")
